# version 06
# baseline (speedup 1.0000x reference)
import contextlib
import numpy as np
import concourse.bass as bass
import concourse.mybir as mybir
from concourse.bass_utils import run_bass_kernel_spmd

F32 = mybir.dt.float32
BF16 = mybir.dt.bfloat16
AF = mybir.ActivationFunctionType
ALU = mybir.AluOpType
AX = mybir.AxisListType

NCORES = 8
SEQ = 2048
T = 4096
D = 1024
DIN = 2048
DSTATE = 128
NH = 16
HDIM = 128
CHUNK = 256
NCH = SEQ // CHUNK
CONVD = DIN + 2 * DSTATE
NJC = CONVD // 128
EPS = 1e-5

ENGS = ("te", "ve", "sc", "gp", "sy")
S2_PIPE = False
S2_ILV = True
DMAQS = ("gp", "sy", "sc")


NDMASEM = 8


class Em:

    def __init__(self):
        self.ops = {e: [] for e in ENGS}
        self.n = {e: 0 for e in ENGS}
        self.nd = {}
        self.rr = {q: 0 for q in DMAQS}
        self.last_wait = {}

    def op(self, eng, fn):
        if eng in ("ve", "sc", "gp") and self.n[eng] > 0:
            self.wait(eng, (eng, self.n[eng]))
        self.n[eng] += 1
        self.ops[eng].append(("op", fn))
        return (eng, self.n[eng])

    def dma(self, q, fn, slot=None):
        if q == "gp" and self.n["gp"] > 0:
            self.wait(q, ("gp", self.n["gp"]))
        if slot is None:
            slot = self.rr[q]
            self.rr[q] = (self.rr[q] + 1) % NDMASEM
        key = f"{q}{slot}_dma"
        prev = self.nd.get(key, 0)
        self.wait(q, (key, prev))
        self.nd[key] = prev + 16
        self.ops[q].append(("dma", fn, key))
        return (key, prev + 16)

    def wait(self, eng, tok):
        if tok is None:
            return
        key, cnt = tok
        if cnt <= 0:
            return
        if self.last_wait.get((eng, key), 0) >= cnt:
            return
        self.last_wait[(eng, key)] = cnt
        self.ops[eng].append(("wait", key, cnt))

    def fence(self, eng):
        for e in ENGS:
            if e != eng and self.n[e]:
                self.wait(eng, (e, self.n[e]))
        for key, cnt in self.nd.items():
            self.wait(eng, (key, cnt))

    def barrier(self):
        for e in ENGS:
            self.fence(e)

    def tok(self, eng):
        return (eng, self.n[eng])

    def emit(self, nc, blk, sems):
        def run_stream(eng_obj, eng_name):
            for item in self.ops[eng_name]:
                if item[0] == "op":
                    ins = item[1](eng_obj)
                    ins.then_inc(sems[eng_name], 1)
                elif item[0] == "dma":
                    ins = item[1](eng_obj)
                    ins.then_inc(sems[item[2]], 16)
                else:
                    eng_obj.wait_ge(sems[item[1]], item[2])
        blk.tensor(lambda e: run_stream(e, "te"))
        blk.vector(lambda e: run_stream(e, "ve"))
        blk.scalar(lambda e: run_stream(e, "sc"))
        blk.gpsimd(lambda e: run_stream(e, "gp"))
        blk.sync(lambda e: run_stream(e, "sy"))

    def make_sems(self, nc, stx):
        sems = {}
        for e in ENGS:
            sems[e] = stx.enter_context(nc.semaphore("sem_" + e))
        for q in DMAQS:
            for k in range(NDMASEM):
                sems[f"{q}{k}_dma"] = stx.enter_context(
                    nc.semaphore(f"sem_{q}{k}_dma"))
        return sems


def host_prep(inputs):
    I = inputs
    w = {}
    w["W_inT"] = np.ascontiguousarray(I["L0_attn_W_in"].T)
    w["conv_w"] = np.ascontiguousarray(I["L0_attn_conv_w"][:, 0, :].reshape(NJC, 128, 4))
    w["conv_b"] = np.ascontiguousarray(I["L0_attn_conv_b"].reshape(NJC, 128, 1))
    w["dt_bias"] = np.ascontiguousarray(I["L0_attn_dt_bias"].reshape(16, 1))
    w["A_log"] = np.ascontiguousarray(I["L0_attn_A_log"].reshape(16, 1))
    w["Dh"] = np.ascontiguousarray(I["L0_attn_D"].reshape(1, 16))
    w["norm_weight"] = np.ascontiguousarray(I["L0_attn_norm_weight"].reshape(1, DIN))
    w["W_outT"] = np.ascontiguousarray(I["L0_attn_W_out"].T)
    w["Wg"] = np.ascontiguousarray(I["L0_ffn_Wg"])
    w["bg"] = np.ascontiguousarray(I["L0_ffn_bg"].reshape(1, 8))
    w["We"] = np.ascontiguousarray(I["L0_ffn_We"])
    w["be"] = np.ascontiguousarray(I["L0_ffn_be"])
    w["WqT"] = np.ascontiguousarray(I["L1_attn_Wq"].reshape(1024, 1024).T)
    w["WkT"] = np.ascontiguousarray(I["L1_attn_Wk"].reshape(1024, 1024).T)
    w["WvT"] = np.ascontiguousarray(I["L1_attn_Wv"].reshape(1024, 1024).T)
    w["bq"] = np.ascontiguousarray(I["L1_attn_bq"].reshape(1024, 1))
    w["bk"] = np.ascontiguousarray(I["L1_attn_bk"].reshape(1024, 1))
    w["bv"] = np.ascontiguousarray(I["L1_attn_bv"].reshape(1024, 1))
    w["proj"] = np.ascontiguousarray(I["L1_attn_proj"])
    w["W_QT"] = np.ascontiguousarray(I["L1_ffn_W_Q"].T)
    w["W_KT"] = np.ascontiguousarray(I["L1_ffn_W_K"].T)
    w["W_OT"] = np.ascontiguousarray(I["L1_ffn_W_O"].T)
    w["W_invT"] = np.ascontiguousarray(I["L1_ffn_W_inv"].T)
    def flatT(a):
        return np.ascontiguousarray(a.transpose(1, 0, 2).reshape(1024, 512))
    def flat(a):
        return np.ascontiguousarray(a.transpose(0, 2, 1).reshape(512, 1024))
    w["VfT"] = flatT(I["L1_ffn_V_fwd"])
    w["WfT"] = flatT(I["L1_ffn_W_fwd"])
    w["ViT"] = flatT(I["L1_ffn_W_inv_exp"])
    w["WiT"] = flatT(I["L1_ffn_V_inv"])
    w["Uf"] = flat(I["L1_ffn_U_fwd"])
    w["Ui"] = flat(I["L1_ffn_U_inv"])
    w["b_fwd"] = np.ascontiguousarray(I["L1_ffn_b_fwd"])
    w["b_inv"] = np.ascontiguousarray(I["L1_ffn_b_inv"])
    w["rw1T"] = np.ascontiguousarray(I["L1_ffn_router_w1"].T)
    w["rb1"] = np.ascontiguousarray(I["L1_ffn_router_b1"].reshape(1024, 1))
    w["rw2T"] = np.ascontiguousarray(I["L1_ffn_router_w2"].T)
    w["rb2"] = np.ascontiguousarray(I["L1_ffn_router_b2"].reshape(1, 8))
    w["expert_bias"] = np.ascontiguousarray(I["L1_ffn_expert_bias"].reshape(1, 8))
    w["alpha_bi"] = np.ascontiguousarray(I["L1_ffn_alpha_bi"].reshape(1, 1))
    return w


class Arena:

    def __init__(self, ar_handle, n_units):
        self.ar = ar_handle
        self.n_units = n_units
        self.off = 0
        self.peak = 0

    def alloc(self, shape, dtype):
        p = shape[0]
        nfree = 1
        for d in shape[1:]:
            nfree *= d
        units = nfree * (2 if dtype == F32 else 1)
        units = (units + 63) // 64 * 64
        assert self.off + units <= self.n_units, (
            f"arena overflow: need {units} at {self.off}, cap {self.n_units}")
        base = self.ar[:, self.off:self.off + units]
        self.off += units
        self.peak = max(self.peak, self.off)
        if dtype != BF16:
            base = base.bitcast(dtype)
        strides = []
        st = 1
        for d in reversed(shape[1:]):
            strides.append([st, d])
            st *= d
        strides.reverse()
        pdim = [base.ap[0][0], p]
        return bass.AP(base.tensor, base.offset, [pdim] + strides)

    def mark(self):
        return self.off

    def release(self, m):
        self.off = m


def mk_psum_pool(nc, stx, bufs):
    ps2 = [stx.enter_context(nc.psum_tensor(f"ps2_{i}", [128, 1024], F32))
           for i in range(3)]
    bufs["ps2"] = [t[:] for t in ps2]
    bufs["ps"] = [ps2[i // 2][:, 512 * (i % 2):512 * (i % 2) + 512] for i in range(6)]
    bufs["ps_bf"] = [stx.enter_context(nc.psum_tensor(f"pbf{i}", [128, 1024], BF16))
                     for i in range(2)]


def carve(region, unit_off, shape, dtype):
    ap2 = region[:] if not isinstance(region, bass.AP) else region
    p = shape[0]
    nfree = 1
    for d in shape[1:]:
        nfree *= d
    units = nfree * (2 if dtype == F32 else 1)
    base = bass.AP(ap2.tensor, ap2.offset + unit_off, [list(ap2.ap[0]), [1, units]])
    if dtype != BF16:
        base = base.bitcast(dtype)
    strides = []
    st = 1
    for d in reversed(shape[1:]):
        strides.append([st, d])
        st *= d
    strides.reverse()
    return bass.AP(base.tensor, base.offset, [[base.ap[0][0], p]] + strides)


def flatT_note():
    pass



def build_identity(nc, em, bufs):
    ident = bufs["ident"]

    em.op("gp", lambda gp: gp.memset(ident[:], 1.0))
    return em.op("gp", lambda gp: gp.affine_select(
        out=ident[:], in_=ident[:], pattern=[[-1, 128]],
        compare_op=ALU.is_equal, fill=0.0, base=0, channel_multiplier=1))


def stage_ln(nc, em, bufs, x_dram, xn_fm, key="ln"):
    xt = bufs[key + "_x"]
    sq = bufs[key + "_sq"]
    col = bufs[key + "_col"]
    xn_tm = bufs[key + "_xn"]
    pbf = bufs["ps_bf"]
    ident = bufs["ident"]
    id_tok = bufs["ident_tok"]

    NTT = T // 128
    last_use = {}
    t_ev = None
    for i in range(NTT):
        sl = i % 2
        cb = 8 * sl

        em.wait("sy", last_use.get(("x", sl)))
        ld = em.dma("sy", lambda sy, i=i, sl=sl: sy.dma_start(
            out=xt[:, sl], in_=x_dram[128 * i:128 * (i + 1), :]))
        em.wait("sc", ld)
        em.wait("sc", last_use.get(("col", sl)))
        t_sq = em.op("sc", lambda sc, sl=sl, cb=cb: sc.activation(
            out=sq[:], in_=xt[:, sl], func=AF.Square, accum_out=col[:, cb:cb + 1]))
        em.wait("ve", ld)
        em.wait("ve", last_use.get(("col", sl)))
        em.op("ve", lambda ve, sl=sl, cb=cb: ve.reduce_sum(
            col[:, cb + 1:cb + 2], xt[:, sl], axis=AX.X))
        em.wait("ve", t_sq)
        em.op("ve", lambda ve, cb=cb: ve.tensor_scalar_mul(
            col[:, cb + 2:cb + 3], col[:, cb + 1:cb + 2], 1.0 / D))
        em.op("ve", lambda ve, cb=cb: ve.tensor_tensor(
            col[:, cb + 3:cb + 4], col[:, cb + 2:cb + 3], col[:, cb + 2:cb + 3], ALU.mult))
        em.op("ve", lambda ve, cb=cb: ve.tensor_scalar_mul(
            col[:, cb + 4:cb + 5], col[:, cb:cb + 1], 1.0 / D))
        em.op("ve", lambda ve, cb=cb: ve.tensor_sub(
            col[:, cb + 5:cb + 6], col[:, cb + 4:cb + 5], col[:, cb + 3:cb + 4]))
        em.op("ve", lambda ve, cb=cb: ve.tensor_scalar_add(
            col[:, cb + 6:cb + 7], col[:, cb + 5:cb + 6], EPS))
        t_rec = em.op("ve", lambda ve, cb=cb: ve.reciprocal(
            col[:, cb + 7:cb + 8], col[:, cb + 6:cb + 7]))
        em.wait("sc", t_rec)
        t_rstd = em.op("sc", lambda sc, cb=cb: sc.activation(
            out=col[:, cb + 6:cb + 7], in_=col[:, cb + 7:cb + 8], func=AF.Sqrt))
        em.wait("ve", t_rstd)
        em.wait("ve", last_use.get(("xn", sl)))
        t_xn = em.op("ve", lambda ve, sl=sl, cb=cb: ve.tensor_scalar(
            xn_tm[:, sl], xt[:, sl], col[:, cb + 2:cb + 3], col[:, cb + 6:cb + 7],
            ALU.subtract, ALU.mult))
        last_use[("x", sl)] = t_xn
        last_use[("col", sl)] = t_xn
        em.wait("te", t_xn)
        em.wait("te", id_tok)
        pb = pbf[i % 2]
        em.wait("te", last_use.get(("pb", i % 2)))
        def f_tr(te, sl=sl, pb=pb):
            for c in range(8):
                ins = te.transpose(pb[:, 128 * c:128 * (c + 1)],
                                   xn_tm[:, sl, 128 * c:128 * (c + 1)], ident[:])
            return ins
        t_tr = em.op("te", f_tr)
        em.wait("sc", t_tr)
        def f_ev(sc, i=i, pb=pb):
            return sc.activation(
                out=xn_fm[:, :, 128 * i:128 * (i + 1)],
                in_=pb[:].rearrange("p (c f) -> p c f", f=128), func=AF.Copy)
        t_ev = em.op("sc", f_ev)
        last_use[("pb", i % 2)] = t_ev
        last_use[("xn", sl)] = t_ev
    return t_ev


def stage_s1(nc, em, bufs, wd, scr, xn_tok):
    xn = bufs["xn_fm"]
    W = bufs["s1_w"]
    Wdt = bufs["s1_wdt"]
    zbuf = bufs["s1_z"]
    raw = bufs["s1_raw"]
    acc = bufs["s1_acc"]
    act = bufs["s1_act"]
    cw = bufs["s1_cw"]
    cb = bufs["s1_cb"]
    dtr = bufs["s1_dtr"]
    PS = bufs["ps"]

    wz_toks = []
    for kd in range(8):
        wz_toks.append(em.dma("gp", lambda gp, kd=kd: gp.dma_start(
            out=W[:, kd, 0:2048], in_=wd["W_inT"][128 * kd:128 * (kd + 1), 0:2048])))
    wdt_toks = []
    for kd in range(8):
        wdt_toks.append(em.dma("gp", lambda gp, kd=kd: gp.dma_start(
            out=Wdt[:, kd], in_=wd["W_inT"][128 * kd:128 * (kd + 1), 4352:4368])))

    em.wait("te", xn_tok)
    z_done = []
    last_ev = {}
    for ti in range(T // 128):
        sl = ti % 2
        ev_toks = []
        for jn in range(4):
            bank = PS[jn % 2]
            em.wait("te", last_ev.get(("zps", jn % 2)))
            for tok in wz_toks:
                em.wait("te", tok)
            def f_mm(te, ti=ti, jn=jn, bank=bank):
                for kd in range(8):
                    ins = te.matmul(bank[:], xn[:, kd, 128 * ti:128 * (ti + 1)],
                                    W[:, kd, 512 * jn:512 * (jn + 1)],
                                    start=(kd == 0), stop=(kd == 7))
                return ins
            t_mm = em.op("te", f_mm)
            em.wait("sc", t_mm)
            em.wait("sc", last_ev.get(("zbuf", sl)))
            t_ev = em.op("sc", lambda sc, sl=sl, jn=jn, bank=bank: sc.activation(
                out=zbuf[:, sl, 512 * jn:512 * (jn + 1)], in_=bank[:], func=AF.Copy))
            last_ev[("zps", jn % 2)] = t_ev
            ev_toks.append(t_ev)
        for tok in ev_toks:
            em.wait("sy", tok)
        st = em.dma("sy", lambda sy, ti=ti, sl=sl: sy.dma_start(
            out=scr["z_scr"][128 * ti:128 * (ti + 1), :], in_=zbuf[:, sl]))
        last_ev[("zbuf", sl)] = st
        z_done.append(st)

    t_zmm_done = em.tok("te")
    wx_toks = []
    for kd in range(8):
        em.wait("gp", t_zmm_done)
        wx_toks.append(em.dma("gp", lambda gp, kd=kd: gp.dma_start(
            out=W[:, kd, 0:2304],
            in_=wd["W_inT"][128 * kd:128 * (kd + 1), 2048:4352])))

    bank_dt = PS[4]
    dt_evs = []
    for tb in range(8):
        for tok in wdt_toks:
            em.wait("te", tok)
        def f_mmdt(te, tb=tb):
            for kd in range(8):
                ins = te.matmul(bank_dt[0:16, 0:512],
                                Wdt[:, kd], xn[:, kd, 512 * tb:512 * (tb + 1)],
                                start=(kd == 0), stop=(kd == 7))
            return ins
        em.wait("te", dt_evs[-1] if dt_evs else None)
        t_mm = em.op("te", f_mmdt)
        em.wait("ve", t_mm)
        t_ev = em.op("ve", lambda ve, tb=tb: ve.tensor_copy(
            dtr[:, 512 * tb:512 * (tb + 1)], bank_dt[0:16, 0:512]))
        dt_evs.append(t_ev)
    em.wait("sy", dt_evs[-1])
    st_dtr = em.dma("sy", lambda sy: sy.dma_start(out=scr["dtr_scr"][:, :], in_=dtr[:]))

    em.op("ve", lambda ve: ve.memset(raw[:, 0:3], 0.0))
    em.op("ve", lambda ve: ve.memset(raw[:, 2051:2054], 0.0))
    t_pad = em.tok("ve")
    last = {}
    for jc in range(NJC):
        csl = jc % 2
        em.wait("sy", last.get(("cw", csl)))
        t_cw = em.dma("sy", lambda sy, jc=jc, csl=csl: sy.dma_start(
            out=cw[:, csl], in_=wd["conv_w"][jc]))
        t_cb = em.dma("sy", lambda sy, jc=jc, csl=csl: sy.dma_start(
            out=cb[:, csl], in_=wd["conv_b"][jc]))
        raw_evs = []
        for tb in range(8):
            bank = PS[2 + tb % 2]
            em.wait("te", last.get(("xps", tb % 2)))
            for tok in wx_toks:
                em.wait("te", tok)
            def f_mm(te, jc=jc, tb=tb, bank=bank):
                for kd in range(8):
                    ins = te.matmul(bank[:], W[:, kd, 128 * jc:128 * (jc + 1)],
                                    xn[:, kd, 512 * tb:512 * (tb + 1)],
                                    start=(kd == 0), stop=(kd == 7))
                return ins
            t_mm = em.op("te", f_mm)
            em.wait("ve", t_mm)
            em.wait("ve", last.get("rawfull"))
            col = 3 + 512 * tb if tb < 4 else 2054 + 512 * (tb - 4)
            t_ev = em.op("ve", lambda ve, col=col, bank=bank: ve.tensor_copy(
                raw[:, col:col + 512], bank[:]))
            last[("xps", tb % 2)] = t_ev
            raw_evs.append(t_ev)
        em.wait("ve", t_pad)
        for tok in raw_evs:
            em.wait("ve", tok)
        em.wait("ve", t_cw)
        em.wait("ve", last.get("accfull"))
        def rawview(j):
            r = raw[:]
            return bass.AP(r.tensor, r.offset + j, [r.ap[0], [2051, 2], [1, 2048]])
        def f_c0(ve, csl=csl):
            a3 = acc[:].rearrange("p (s c) -> p s c", s=2)
            return ve.tensor_scalar(a3, rawview(0), cw[:, csl, 0:1], None, ALU.mult)
        em.op("ve", f_c0)
        for j in range(1, 4):
            def f_cj(ve, j=j, csl=csl):
                a3 = acc[:].rearrange("p (s c) -> p s c", s=2)
                return ve.scalar_tensor_tensor(a3, rawview(j), cw[:, csl, j:j + 1],
                                               a3, ALU.mult, ALU.add)
            em.op("ve", f_cj)
        t_conv = em.tok("ve")
        last["rawfull"] = t_conv
        em.wait("sc", t_conv)
        em.wait("sc", t_cb)
        em.wait("sc", last.get("actfull"))
        t_act = em.op("sc", lambda sc, csl=csl: sc.activation(
            out=act[:], in_=acc[:], func=AF.Silu, bias=cb[:, csl]))
        last["accfull"] = t_act
        last[("cw", csl)] = t_act
        em.wait("sy", t_act)
        if jc < 16:
            st = em.dma("sy", lambda sy, jc=jc: sy.dma_start(
                out=scr["xfm_scr"][128 * jc:128 * (jc + 1), :], in_=act[:]))
        elif jc == 16:
            st = em.dma("sy", lambda sy: sy.dma_start(
                out=scr["bfm_scr"][:, :], in_=act[:]))
        else:
            st = em.dma("sy", lambda sy: sy.dma_start(
                out=scr["cfm_scr"][:, :], in_=act[:]))
        last["actfull"] = st
    return st_dtr


def stage_s2(nc, em, bufs, wd, scr, dbg=None):
    PS = bufs["ps"]
    pbf = bufs["ps_bf"]
    identb = bufs["ident"]
    identf = bufs["ident_f32"]
    id_tok = bufs["ident_tok"]
    idf_tok = bufs["identf_tok"]

    dtr = bufs["s2_dtr"]
    dt = bufs["s2_dt"]
    t16 = bufs["s2_t16"]
    acs = bufs["s2_acs"]
    bcol = bufs["s2_bcol"]
    dtT = bufs["s2_dtT"]
    nacsT = bufs["s2_nacsT"]
    eacsT = bufs["s2_eacsT"]
    sfacT = bufs["s2_sfacT"]
    decfm = bufs["s2_decfm"]
    cd16 = bufs["s2_cd16"]
    cdbc = bufs["s2_cdbc"]
    dbc = bufs["s2_dbc"]
    nwbc = bufs["s2_nwbc"]
    Wo = bufs["s2_wo"]
    xch = bufs["s2_xch"]
    xtm = bufs["s2_xtm"]
    Bc = bufs["s2_bc"]
    Cc = bufs["s2_cc"]
    Btm = bufs["s2_btm"]
    bsa = bufs["s2_bsa"]
    cbt = bufs["s2_cbt"]
    seg1 = bufs["s2_seg1"]
    mta = bufs["s2_mta"]
    DI = bufs["s2_di"]
    carry = bufs["s2_carry"]
    carrybf = bufs["s2_carrybf"]
    yo = bufs["s2_yo"]
    ytm = bufs["s2_ytm"]
    ztm = bufs["s2_ztm"]
    yfm = bufs["s2_yfm"]
    rcol = bufs["s2_rcol"]
    sout = bufs["s2_sout"]

    lu = {}

    def bcast_h(ap2, nh, nq):
        return bass.AP(ap2.tensor, ap2.offset, [ap2.ap[0], [0, nh], list(ap2.ap[1])])

    def bcast_q(ap2, nq):
        return bass.AP(ap2.tensor, ap2.offset, [ap2.ap[0], list(ap2.ap[1]), [0, nq]])

    t_db = em.dma("sy", lambda sy: sy.dma_start(out=bcol[:, 0:1], in_=wd["dt_bias"]))
    t_al = em.dma("sy", lambda sy: sy.dma_start(out=bcol[:, 1:2], in_=wd["A_log"]))
    dsrc = wd["Dh"]
    t_dbc = em.dma("sy", lambda sy: sy.dma_start(
        out=dbc[:], in_=bass.AP(dsrc.tensor, dsrc.offset, [[0, 128], [1, 16]])))
    nsrc = wd["norm_weight"]
    t_nw = em.dma("gp", lambda gp: gp.dma_start(
        out=nwbc[:], in_=bass.AP(nsrc.tensor, nsrc.offset, [[0, 128], [1, 2048]])))
    t_dtr = em.dma("sy", lambda sy: sy.dma_start(out=dtr[:], in_=scr["dtr_scr"][:, :]))
    em.wait("ve", t_dbc)
    em.wait("ve", id_tok)
    t_DI = em.op("ve", lambda ve: ve.tensor_tensor(
        DI[:],
        bass.AP(identb[:].tensor, identb[:].offset,
                [identb[:].ap[0], [0, 16], [1, 128]]),
        bass.AP(dbc[:].tensor, dbc[:].offset, [dbc[:].ap[0], [1, 16], [0, 128]]),
        ALU.mult))

    em.wait("sc", t_dtr)
    em.wait("sc", t_db)
    em.op("sc", lambda sc: sc.activation(out=t16[:], in_=dtr[:], func=AF.Abs,
                                         bias=bcol[:, 0:1]))
    em.op("sc", lambda sc: sc.activation(out=t16[:], in_=t16[:], func=AF.Exp, scale=-1.0))
    em.op("sc", lambda sc: sc.activation(out=t16[:], in_=t16[:], func=AF.Ln, bias=1.0))
    t_sp1 = em.op("sc", lambda sc: sc.activation(out=dt[:], in_=dtr[:], func=AF.Relu,
                                                 bias=bcol[:, 0:1]))
    em.wait("ve", t_sp1)
    t_dt = em.op("ve", lambda ve: ve.tensor_add(dt[:], dt[:], t16[:]))
    em.wait("sc", t_al)
    t_a = em.op("sc", lambda sc: sc.activation(out=bcol[:, 2:3], in_=bcol[:, 1:2], func=AF.Exp))
    em.wait("ve", t_a)
    em.wait("ve", t_dt)
    t_dA = em.op("ve", lambda ve: ve.tensor_scalar(
        t16[:], dt[:], bcol[:, 2:3], -1.0, ALU.mult, ALU.mult))
    zc = bufs["zcol16"]
    t_zc = em.op("ve", lambda ve: ve.memset(zc[:], 0.0))
    scan_toks = []
    for s in range(2):
        for c in range(NCH):
            t0 = SEQ * s + CHUNK * c
            def f_scan(ve, t0=t0):
                zb = bass.AP(zc[:].tensor, zc[:].offset, [zc[:].ap[0], [0, CHUNK]])
                return ve.tensor_tensor_scan(
                    out=acs[:, t0:t0 + CHUNK], data0=t16[:, t0:t0 + CHUNK],
                    data1=zb, initial=0.0, op0=ALU.add, op1=ALU.add)
            scan_toks.append(em.op("ve", f_scan))
    t_acs = scan_toks[-1]
    em.wait("sy", t_acs)
    t_acsd = em.dma("sy", lambda sy: sy.dma_start(out=scr["acs_scr"][:, :], in_=acs[:]))
    t_cdbc = []
    for s in range(2):
        a0 = acs[:]
        ends = bass.AP(a0.tensor, a0.offset + SEQ * s + CHUNK - 1, [a0.ap[0], [CHUNK, NCH]])
        em.wait("sc", t_acs)
        em.wait("sc", lu.get("cd16"))
        t_e = em.op("sc", lambda sc, ends=ends: sc.activation(
            out=cd16[:], in_=ends, func=AF.Exp))
        em.wait("sy", t_e)
        t_w = em.dma("sy", lambda sy, s=s: sy.dma_start(
            out=scr["cdec_scr"][s], in_=cd16[:]))
        em.wait("sy", t_w)
        csrc = scr["cdec_scr"]
        t_b = em.dma("sy", lambda sy, s=s, csrc=csrc: sy.dma_start(
            out=cdbc[:, s], in_=bass.AP(csrc.tensor, csrc.offset + 128 * s,
                                        [[0, 128], [1, 128]])))
        lu["cd16"] = t_w
        t_cdbc.append(t_b)

    em.wait("te", idf_tok)
    for ti in range(32):
        em.wait("te", t_dt)
        em.wait("te", t_acs)
        em.wait("te", lu.get(("tps", ti % 2)))
        em.wait("te", lu.get(("tpsv", ti % 2)))
        bank = PS[ti % 2]
        def f_tr(te, ti=ti, bank=bank):
            te.transpose(bank[0:128, 0:16], dt[:, 128 * ti:128 * (ti + 1)],
                         identf[0:16, 0:16])
            return te.transpose(bank[0:128, 16:32], acs[:, 128 * ti:128 * (ti + 1)],
                                identf[0:16, 0:16])
        t_tr = em.op("te", f_tr)
        em.wait("ve", t_tr)
        t_c1 = em.op("ve", lambda ve, ti=ti, bank=bank: ve.tensor_copy(
            dtT[:, ti], bank[0:128, 0:16]))
        t_c2 = em.op("ve", lambda ve, ti=ti, bank=bank: ve.tensor_scalar_mul(
            nacsT[:, ti], bank[0:128, 16:32], -1.0))
        em.wait("sc", t_tr)
        t_c3 = em.op("sc", lambda sc, ti=ti, bank=bank: sc.activation(
            out=eacsT[:, ti], in_=bank[0:128, 16:32], func=AF.Exp))
        lu[("tps", ti % 2)] = t_c3
        lu[("tpsv", ti % 2)] = t_c2
    t_cols = (t_c1, t_c2, t_c3)
    for s in range(2):
        for c in range(NCH):
            t0 = SEQ * s + CHUNK * c
            em.wait("ve", t_acs)
            em.wait("ve", lu.get("decfm"))
            t_d1 = em.op("ve", lambda ve, t0=t0: ve.tensor_scalar(
                decfm[:], acs[:, t0:t0 + CHUNK], acs[:, t0 + CHUNK - 1:t0 + CHUNK],
                None, ALU.subtract))
            em.wait("sc", t_d1)
            t_d2 = em.op("sc", lambda sc: sc.activation(
                out=decfm[:], in_=decfm[:], func=AF.Exp, scale=-1.0))
            ti0 = 2 * (NCH * s + c)
            em.wait("te", t_d2)
            em.wait("te", lu.get("decps"))
            def f_tr2(te):
                te.transpose(PS[2][0:128, 0:16], decfm[:, 0:128], identf[0:16, 0:16])
                return te.transpose(PS[2][0:128, 16:32], decfm[:, 128:256],
                                    identf[0:16, 0:16])
            t_tr2 = em.op("te", f_tr2)
            for st in range(2):
                em.wait("ve", t_tr2)
                t_sf = em.op("ve", lambda ve, ti0=ti0, st=st: ve.tensor_mul(
                    sfacT[:, ti0 + st], dtT[:, ti0 + st],
                    PS[2][0:128, 16 * st:16 * (st + 1)]))
            lu["decps"] = t_sf
            lu["decfm"] = t_tr2

    em.wait("gp", em.tok("te"))
    em.wait("gp", em.tok("ve"))
    wo_toks = [em.dma("gp", lambda gp, kc=kc: gp.dma_start(
        out=Wo[:, kc], in_=wd["W_outT"][128 * kc:128 * (kc + 1), :]))
        for kc in range(16)]


    def P2(cc):
        s, c = (cc % 2, cc // 2) if S2_ILV else (cc // NCH, cc % NCH)
        t0 = SEQ * s + CHUNK * c
        ti0 = 2 * (NCH * s + c)
        g = cc % 2
        em.wait("gp", lu.get(("xchg", g)))
        xsrc = scr["xfm_scr"].rearrange("(c p) t -> p c t", p=128)
        t_xch = em.dma("gp", lambda gp, t0=t0, g=g, xsrc=xsrc: gp.dma_start(
            out=xch[:, g], in_=xsrc[:, :, t0:t0 + CHUNK]))
        em.wait("gp", lu.get(("bcg", g)))
        t_bc = em.dma("gp", lambda gp, t0=t0, g=g: gp.dma_start(
            out=Bc[:, g], in_=scr["bfm_scr"][:, t0:t0 + CHUNK]))
        t_cc_ = em.dma("gp", lambda gp, t0=t0, g=g: gp.dma_start(
            out=Cc[:, g], in_=scr["cfm_scr"][:, t0:t0 + CHUNK]))
        lu[("ccld", g)] = t_cc_
        for gg in range(4):
            pb = pbf[gg % 2]
            em.wait("te", t_xch)
            em.wait("te", id_tok)
            em.wait("te", lu.get(("pbf", gg % 2)))
            em.wait("te", lu.get(("xtmg", g)))
            def f_xtr(te, gg=gg, pb=pb, g=g):
                for k in range(8):
                    idx = 8 * gg + k
                    st, c16 = idx // 16, idx % 16
                    ins = te.transpose(pb[:, 128 * k:128 * (k + 1)],
                                       xch[:, g, c16, 128 * st:128 * (st + 1)],
                                       identb[:])
                return ins
            t_xtr = em.op("te", f_xtr)
            em.wait("sc", t_xtr)
            em.wait("sc", lu.get(("xtmg", g)))
            def f_xev(sc, gg=gg, pb=pb, g=g):
                st, c0 = (8 * gg) // 16, (8 * gg) % 16
                return sc.activation(
                    out=xtm[:, g, st, 128 * c0:128 * c0 + 1024],
                    in_=pb[:], func=AF.Copy)
            t_xev = em.op("sc", f_xev)
            lu[("pbf", gg % 2)] = t_xev
        lu[("xchg", g)] = t_xtr
        lu[("xtm", cc)] = t_xev
        em.wait("te", t_bc)
        em.wait("te", lu.get(("btmg", g)))
        em.wait("te", lu.get(("pbf", 0)))
        def f_btr(te, g=g):
            te.transpose(pbf[0][:, 0:128], Bc[:, g, 0:128], identb[:])
            return te.transpose(pbf[0][:, 128:256], Bc[:, g, 128:256], identb[:])
        t_btr = em.op("te", f_btr)
        em.wait("ve", t_btr)
        t_bev = em.op("ve", lambda ve, g=g: ve.tensor_copy(
            Btm[:, g].rearrange("p a b -> p (a b)"), pbf[0][:, 0:256]))
        lu[("pbf", 0)] = t_bev
        lu[("bcg", g)] = t_btr
        lu[("btm", cc)] = t_bev
        em.wait("te", t_cc_)
        em.wait("te", lu.get(("bank", 5)))
        def f_cbt(te, g=g):
            for st in range(2):
                ins = te.matmul(PS[5][:, 256 * st:256 * (st + 1)],
                                Bc[:, g, 128 * st:128 * (st + 1)], Cc[:, g],
                                start=True, stop=True)
            return ins
        t_cbtm = em.op("te", f_cbt)
        em.wait("ve", t_cbtm)
        em.wait("ve", lu.get(("cbtg", g)))
        t_cbt = em.op("ve", lambda ve, g=g: ve.tensor_copy(
            cbt[:, g].rearrange("p a b -> p (a b)"), PS[5][:]))
        lu[("bank", 5)] = t_cbt
        mta_toks = []
        asrc = scr["acs_scr"]
        for st in range(2):
            t_exs = []
            for hh in range(2):
                sl = (2 * st + hh) % 2
                em.wait("gp", lu.get(("seg1", sl)))
                em.wait("gp", t_acsd)
                t_ab = em.dma("gp", lambda gp, t0=t0, hh=hh, sl=sl, asrc=asrc:
                              gp.dma_start(
                    out=seg1[:, sl],
                    in_=bass.AP(asrc.tensor, asrc.offset + 8 * hh * T + t0,
                                [[0, 128], [T, 8], [1, 256]])))
                em.wait("gp", t_ab)
                em.wait("gp", t_cols[1])
                a1 = nacsT[:, ti0 + st, 8 * hh:8 * (hh + 1)]
                t_s1 = em.op("gp", lambda gp, sl=sl, a1=a1: gp.tensor_tensor(
                    seg1[:, sl], seg1[:, sl], bcast_q(a1, 256), ALU.add))
                em.wait("sc", t_s1)
                t_rl = em.op("sc", lambda sc, sl=sl: sc.activation(
                    out=seg1[:, sl], in_=seg1[:, sl], func=AF.Relu, scale=-1.0))
                em.wait("sc", lu.get(("mtag", g, st)))
                t_ex = em.op("sc", lambda sc, st=st, hh=hh, sl=sl, g=g: sc.activation(
                    out=mta[:, g, st, 8 * hh:8 * (hh + 1), :], in_=seg1[:, sl],
                    func=AF.Exp, scale=-1.0))
                lu[("seg1", sl)] = t_ex
                t_exs.append(t_ex)
            for t_ex in t_exs:
                em.wait("ve", t_ex)
            d1 = dtT[:, ti0 + st, :]
            em.op("ve", lambda ve, st=st, d1=d1, g=g: ve.tensor_tensor(
                mta[:, g, st], mta[:, g, st], bcast_q(d1, 256), ALU.mult))
            em.wait("ve", t_cbt)
            c1 = cbt[:, g, st]
            t_m2 = em.op("ve", lambda ve, st=st, c1=c1, g=g: ve.tensor_tensor(
                mta[:, g, st], mta[:, g, st], bcast_h(c1, 16, 256), ALU.mult))
            em.wait("gp", t_m2)
            t_msk = em.op("gp", lambda gp, st=st, g=g: gp.affine_select(
                out=mta[:, g, st], in_=mta[:, g, st], pattern=[[0, 16], [1, 256]],
                compare_op=ALU.is_ge, fill=0.0, base=-128 * st,
                channel_multiplier=-1))
            em.wait("ve", t_msk)
            em.wait("ve", t_DI)
            t_di = em.op("ve", lambda ve, st=st, g=g: ve.tensor_add(
                mta[:, g, st, :, 128 * st:128 * (st + 1)],
                mta[:, g, st, :, 128 * st:128 * (st + 1)], DI[:]))
            mta_toks.append(t_di)
        lu[("cbtg", g)] = t_m2
        lu[("mta_ready", cc)] = mta_toks

    def P3(cc):
        s, c = (cc % 2, cc // 2) if S2_ILV else (cc // NCH, cc % NCH)
        ti0 = 2 * (NCH * s + c)
        g = cc % 2
        if cc < 2:
            em.wait("ve", t_acsd)
            em.wait("ve", t_cols[2])
            t_cz = em.op("ve", lambda ve, s=s: ve.memset(carry[:, s], 0.0))
            lu["carry"] = t_cz
        em.wait("ve", lu.get("bsa_used"))
        em.wait("ve", lu.get(("btm", cc)))
        bsa_toks = []
        for st in range(2):
            b1 = Btm[:, g, st]
            s1_ = sfacT[:, ti0 + st, :]
            t_bs = em.op("ve", lambda ve, st=st, b1=b1, s1_=s1_: ve.tensor_tensor(
                bsa[:, st], bcast_h(b1, 16, 128), bcast_q(s1_, 128), ALU.mult))
            bsa_toks.append(t_bs)
        for h in range(NH):
            par = h % 2
            psA = PS[0 + par]
            psB = PS[2 + par]
            psC = PS[4]
            em.wait("te", lu.get(("bank", par)))
            for tok in lu[("mta_ready", cc)]:
                em.wait("te", tok)
            em.wait("te", lu.get(("xtm", cc)))
            def f_yd(te, h=h, psA=psA, g=g):
                for qt in range(2):
                    for st in range(2):
                        ins = te.matmul(psA[:, 128 * qt:128 * (qt + 1)],
                                        mta[:, g, st, h, 128 * qt:128 * (qt + 1)],
                                        xtm[:, g, st, 128 * h:128 * (h + 1)],
                                        start=(st == 0), stop=(st == 1))
                return ins
            t_yd = em.op("te", f_yd)
            if c > 0:
                em.wait("te", lu.get(("bank", 2 + par)))
                em.wait("te", lu.get(("ccld", g)))
                em.wait("te", lu.get(("carrybf", s)))
                def f_yo(te, h=h, psB=psB, g=g, s=s):
                    for qt in range(2):
                        ins = te.matmul(psB[:, 128 * qt:128 * (qt + 1)],
                                        Cc[:, g, 128 * qt:128 * (qt + 1)],
                                        carrybf[:, s, h], start=True, stop=True)
                    return ins
                t_yo = em.op("te", f_yo)
            em.wait("te", lu.get(("bank", 4)))
            for tok in bsa_toks:
                em.wait("te", tok)
            def f_st(te, h=h, psC=psC, g=g):
                for st in range(2):
                    ins = te.matmul(psC[:, 0:128], bsa[:, st, h],
                                    xtm[:, g, st, 128 * h:128 * (h + 1)],
                                    start=(st == 0), stop=(st == 1))
                return ins
            t_stm = em.op("te", f_st)
            for qt in range(2):
                if c > 0:
                    em.wait("sc", t_yo)
                    em.wait("sc", lu.get(("yo", par)))
                    t_yoev = em.op("sc", lambda sc, qt=qt, ti0=ti0, h=h, psB=psB,
                                   par=par:
                                   sc.activation(out=yo[:, par, qt],
                                                 in_=psB[:, 128 * qt:128 * (qt + 1)],
                                                 func=AF.Copy,
                                                 scale=eacsT[:, ti0 + qt, h:h + 1]))
                    em.wait("ve", t_yoev)
                    em.wait("ve", t_yd)
                    em.wait("ve", lu.get("ytm_used"))
                    t_y2 = em.op("ve", lambda ve, qt=qt, h=h, psA=psA, par=par:
                                 ve.tensor_add(
                                     ytm[:, qt, 128 * h:128 * (h + 1)],
                                     yo[:, par, qt],
                                     psA[:, 128 * qt:128 * (qt + 1)]))
                else:
                    em.wait("sc", t_yd)
                    em.wait("sc", lu.get("ytm_used"))
                    em.wait("sc", lu.get(("yo", par)))
                    t_y2 = em.op("sc", lambda sc, qt=qt, h=h, psA=psA: sc.activation(
                        out=ytm[:, qt, 128 * h:128 * (h + 1)],
                        in_=psA[:, 128 * qt:128 * (qt + 1)], func=AF.Copy))
            lu[("yo", par)] = t_y2
            if c > 0:
                lu[("bank", 2 + par)] = t_y2
            lu[("bank", par)] = t_y2
            em.wait("ve", t_stm)
            em.wait("ve", lu.get("carry"))
            em.wait("ve", t_cdbc[s])
            t_cu = em.op("ve", lambda ve, h=h, c=c, s=s, psC=psC: ve.scalar_tensor_tensor(
                carry[:, s, h], carry[:, s, h], cdbc[:, s, 8 * h + c:8 * h + c + 1],
                psC[:, 0:128], ALU.mult, ALU.add))
            lu[("bank", 4)] = t_cu
            em.wait("gp", t_cu)
            em.wait("gp", lu.get(("cbf", s, h)))
            t_cb2 = em.op("gp", lambda gp, h=h, s=s: gp.tensor_copy(
                carrybf[:, s, h], carry[:, s, h]))
            lu[("cbf", s, h)] = t_cb2
        lu[("carrybf", s)] = t_cb2
        lu["carry_used"] = t_cu
        lu["bsa_used"] = t_stm
        lu[("xtmg", g)] = t_cu
        lu[("mtag", g, 0)] = t_yd
        lu[("mtag", g, 1)] = t_yd
        lu[("btmg", g)] = t_bs
        lu["lasty2"] = t_y2

    def P4(cc):
        s, c = (cc % 2, cc // 2) if S2_ILV else (cc // NCH, cc % NCH)
        t0 = SEQ * s + CHUNK * c
        g = cc % 2
        em.wait("sy", lu.get("ztm_used"))
        t_ztm = em.dma("sy", lambda sy, t0=t0: sy.dma_start(
            out=ztm[:], in_=scr["z_scr"][t0:t0 + CHUNK, :].rearrange(
                "(h p) d -> p h d", p=128)))
        em.wait("sc", t_ztm)
        em.wait("sc", lu.get("ztm_used2"))
        t_sz = em.op("sc", lambda sc: sc.activation(out=ztm[:], in_=ztm[:], func=AF.Silu))
        for qt in range(2):
            em.wait("ve", t_sz)
            em.wait("ve", lu["lasty2"])
            t_g = em.op("ve", lambda ve, qt=qt: ve.tensor_mul(
                ytm[:, qt], ytm[:, qt], ztm[:, qt]))
            t_sq = em.op("ve", lambda ve, qt=qt: ve.scalar_tensor_tensor(
                ztm[:, qt], ytm[:, qt], 1.0, ytm[:, qt], ALU.mult, ALU.mult,
                accum_out=rcol[:, 4 * qt:4 * qt + 1]))
            em.op("ve", lambda ve, qt=qt: ve.tensor_scalar(
                rcol[:, 4 * qt + 1:4 * qt + 2], rcol[:, 4 * qt:4 * qt + 1],
                1.0 / DIN, EPS, ALU.mult, ALU.add))
            t_rc = em.op("ve", lambda ve, qt=qt: ve.reciprocal(
                rcol[:, 4 * qt + 2:4 * qt + 3], rcol[:, 4 * qt + 1:4 * qt + 2]))
            em.wait("sc", t_rc)
            t_rs = em.op("sc", lambda sc, qt=qt: sc.activation(
                out=rcol[:, 4 * qt + 3:4 * qt + 4],
                in_=rcol[:, 4 * qt + 2:4 * qt + 3], func=AF.Sqrt))
            em.wait("ve", t_rs)
            em.wait("ve", t_nw)
            t_y3 = em.op("ve", lambda ve, qt=qt: ve.scalar_tensor_tensor(
                ytm[:, qt], ytm[:, qt], rcol[:, 4 * qt + 3:4 * qt + 4],
                nwbc[:], ALU.mult, ALU.mult))
            lu[("y3", qt)] = t_y3
        lu["ztm_used"] = t_sq
        lu["ztm_used2"] = t_y3
        for gg in range(4):
            pb = pbf[gg % 2]
            em.wait("te", lu[("y3", 0)])
            em.wait("te", lu[("y3", 1)])
            em.wait("te", lu.get(("pbf", gg % 2)))
            em.wait("te", lu.get("yfm_used"))
            def f_ytr(te, gg=gg, pb=pb):
                for k in range(8):
                    idx = 8 * gg + k
                    st, c16 = idx // 16, idx % 16
                    ins = te.transpose(pb[:, 128 * k:128 * (k + 1)],
                                       ytm[:, st, 128 * c16:128 * (c16 + 1)],
                                       identb[:])
                return ins
            t_ytr = em.op("te", f_ytr)
            em.wait("sc", t_ytr)
            em.wait("sc", lu.get("yfm_used"))
            def f_yev(sc, gg=gg, pb=pb):
                st, c0 = (8 * gg) // 16, (8 * gg) % 16
                return sc.activation(
                    out=yfm[:, c0:c0 + 8, 128 * st:128 * (st + 1)],
                    in_=pb[:].rearrange("p (c f) -> p c f", f=128),
                    func=AF.Copy)
            t_yev = em.op("sc", f_yev)
            lu[("pbf", gg % 2)] = t_yev
        lu["ytm_used"] = t_yev
        sev_toks = []
        for qt in range(2):
            for dn in range(2):
                bi = 2 + (qt * 2 + dn) % 2
                bank = PS[bi]
                em.wait("te", t_yev)
                for tok in wo_toks:
                    em.wait("te", tok)
                em.wait("te", lu.get(("bank", bi)))
                def f_op(te, qt=qt, dn=dn, bank=bank):
                    for kc in range(16):
                        ins = te.matmul(bank[:],
                                        yfm[:, kc, 128 * qt:128 * (qt + 1)],
                                        Wo[:, kc, 512 * dn:512 * (dn + 1)],
                                        start=(kc == 0), stop=(kc == 15))
                    return ins
                t_op = em.op("te", f_op)
                em.wait("sc", t_op)
                em.wait("sc", lu.get("sout_used"))
                t_sev = em.op("sc", lambda sc, qt=qt, dn=dn, bank=bank: sc.activation(
                    out=sout[:, qt, 512 * dn:512 * (dn + 1)], in_=bank[:], func=AF.Copy))
                lu[("bank", bi)] = t_sev
                sev_toks.append(t_sev)
        for tok in sev_toks:
            em.wait("sy", tok)
        t_sdma = em.dma("sy", lambda sy, t0=t0: sy.dma_start(
            out=scr["s_scr"][t0:t0 + CHUNK, :].rearrange(
                "(h p) d -> p h d", p=128),
            in_=sout[:]))
        lu["sout_used"] = t_sdma
        lu["yfm_used"] = t_op
        lu["s_done"] = t_sdma

    NCC = 2 * NCH
    PIPE = S2_PIPE
    if PIPE:
        P2(0)
        for cc in range(1, NCC):
            P2(cc)
            P3(cc - 1)
            P4(cc - 1)
        P3(NCC - 1)
        P4(NCC - 1)
    else:
        for cc in range(NCC):
            P2(cc)
            P3(cc)
            P4(cc)
    return lu["s_done"]


def stage_s3(nc, em, bufs, wd, scr, xn_tok):
    xn = bufs["xn_fm"]
    PS = bufs["ps"]
    pbf = bufs["ps_bf"]
    identb = bufs["ident"]
    id_tok = bufs["ident_tok"]
    wg = bufs["s3_wg"]
    bgbc = bufs["s3_bgbc"]
    besb = bufs["s3_be"]
    gcl = bufs["s3_gcl"]
    gsc = bufs["s3_gsc"]
    gax1 = bufs["s3_gax1"]
    gax2 = bufs["s3_gax2"]
    gw_all = bufs["s3_gw"]
    gwb_all = bufs["s3_gwb"]
    wT_all = bufs["s3_wt"]
    We_sb = bufs["s3_we"]
    acc = bufs["s3_acc"]
    ut = bufs["s3_u"]
    st_ = bufs["s3_s"]
    gbf = bufs["s3_g"]
    sg = bufs["s3_sg"]
    x1t = bufs["s3_x1"]
    xn1t = bufs["s3_xn1"]
    sqs = bufs["s1_acc"]
    lu = {}

    t_wg = []
    for kd in range(8):
        t_wg.append(em.dma("gp", lambda gp, kd=kd: gp.dma_start(
            out=wg[:, kd], in_=wd["Wg"][128 * kd:128 * (kd + 1), :])))
    bsrc = wd["bg"]
    t_bg = em.dma("sy", lambda sy: sy.dma_start(
        out=bgbc[:], in_=bass.AP(bsrc.tensor, bsrc.offset, [[0, 128], [1, 8]])))
    t_be = em.dma("gp", lambda gp: gp.dma_start(out=besb[:], in_=wd["be"][:, :]))

    ge_toks = []
    for ti in range(32):
        bank = PS[4 + ti % 2]
        for tok in t_wg:
            em.wait("te", tok)
        em.wait("te", xn_tok)
        em.wait("te", lu.get(("gps", ti % 2)))
        def f_g(te, ti=ti, bank=bank):
            for kd in range(8):
                ins = te.matmul(bank[0:128, 0:8], xn[:, kd, 128 * ti:128 * (ti + 1)],
                                wg[:, kd], start=(kd == 0), stop=(kd == 7))
            return ins
        t_gm = em.op("te", f_g)
        em.wait("ve", t_gm)
        em.wait("ve", t_bg)
        t_ge = em.op("ve", lambda ve, ti=ti, bank=bank: ve.tensor_add(
            gsc[:, ti], bank[0:128, 0:8], bgbc[:]))
        lu[("gps", ti % 2)] = t_ge
        ge_toks.append(t_ge)

    def bmid(ap3, n_last):
        return bass.AP(ap3.tensor, ap3.offset,
                       [ap3.ap[0], list(ap3.ap[1]), [0, n_last]])

    em.wait("ve", ge_toks[-1])
    em.op("ve", lambda ve: ve.reduce_max(gcl[:, 0], gsc[:], axis=AX.X))
    em.op("ve", lambda ve: ve.tensor_tensor(
        gax1[:], gsc[:], bmid(gcl[:, 0], 8), ALU.is_ge))
    em.op("ve", lambda ve: ve.scalar_tensor_tensor(
        gax2[:], gax1[:], -1e30, gsc[:], ALU.mult, ALU.add))
    em.op("ve", lambda ve: ve.reduce_max(gcl[:, 1], gax2[:], axis=AX.X))
    em.op("ve", lambda ve: ve.tensor_tensor(
        gax1[:], gsc[:], bmid(gcl[:, 1], 8), ALU.is_ge))
    t_d = em.op("ve", lambda ve: ve.tensor_tensor(
        gax2[:], gsc[:], bmid(gcl[:, 0], 8), ALU.subtract))
    em.wait("sc", t_d)
    t_e1 = em.op("sc", lambda sc: sc.activation(out=gax2[:], in_=gax2[:], func=AF.Exp))
    em.wait("ve", t_e1)
    em.op("ve", lambda ve: ve.tensor_mul(gax2[:], gax2[:], gax1[:]))
    em.op("ve", lambda ve: ve.reduce_sum(gcl[:, 2], gax2[:], axis=AX.X))
    em.op("ve", lambda ve: ve.reciprocal(gcl[:, 3], gcl[:, 2]))
    em.op("ve", lambda ve: ve.tensor_tensor(
        gw_all[:], gax2[:], bmid(gcl[:, 3], 8), ALU.mult))
    t_gwb = em.op("ve", lambda ve: ve.tensor_copy(gwb_all[:], gw_all[:]))
    lu["gw_done"] = t_gwb
    for tg in range(4):
        em.wait("te", t_gwb)
        em.wait("te", id_tok)
        em.wait("te", lu.get("wtps"))
        def f_wt(te, tg=tg):
            for j in range(8):
                ins = te.transpose(pbf[0][0:8, 128 * j:128 * (j + 1)],
                                   gwb_all[:, 8 * tg + j, :], identb[:])
            return ins
        t_wt = em.op("te", f_wt)
        em.wait("ve", t_wt)
        t_wte = em.op("ve", lambda ve, tg=tg: ve.tensor_copy(
            wT_all[:, 8 * tg:8 * (tg + 1), :],
            pbf[0][0:8, 0:1024].rearrange("p (j f) -> p j f", f=128)))
        lu["wtps"] = t_wte
        for j in range(8):
            lu[("wtok", 8 * tg + j)] = t_wte
    t_gate_done = t_wte

    x1_toks = []
    for half in range(2):
        tis = range(16 * half, 16 * (half + 1))
        for ti in tis:
            tloc = ti - 16 * half
            for dn in range(2):
                bank = PS[4 + (2 * tloc + dn) % 2]
                em.wait("te", lu[("wtok", ti)])
                em.wait("te", t_be)
                em.wait("te", lu.get(("bps", (2 * tloc + dn) % 2)))
                t_bm = em.op("te", lambda te, ti=ti, dn=dn, bank=bank: te.matmul(
                    bank[:], wT_all[:, ti], besb[:, 512 * dn:512 * (dn + 1)],
                    start=True, stop=True))
                em.wait("sc", t_bm)
                em.wait("sc", lu.get(("accw", tloc)))
                t_ai = em.op("sc", lambda sc, tloc=tloc, dn=dn, bank=bank: sc.activation(
                    out=acc[:, tloc, 512 * dn:512 * (dn + 1)], in_=bank[:], func=AF.Copy))
                lu[("bps", (2 * tloc + dn) % 2)] = t_ai
            lu[("acci", tloc)] = t_ai
        for e in range(8):
            sl = e % 2
            we_toks = []
            em.wait("gp", lu.get(("wesl", sl)))
            for kd in range(8):
                we_toks.append(em.dma("gp", lambda gp, e=e, sl=sl, kd=kd: gp.dma_start(
                    out=We_sb[:, sl, kd], in_=wd["We"][e, 128 * kd:128 * (kd + 1), :])))
            for ti in tis:
                tloc = ti - 16 * half
                for dn in range(2):
                    bank = PS[(2 * tloc + dn) % 4]
                    for tok in we_toks:
                        em.wait("te", tok)
                    em.wait("te", lu.get(("eps", (2 * tloc + dn) % 4)))
                    def f_e(te, ti=ti, sl=sl, dn=dn, bank=bank):
                        for kd in range(8):
                            ins = te.matmul(bank[:], xn[:, kd, 128 * ti:128 * (ti + 1)],
                                            We_sb[:, sl, kd, 512 * dn:512 * (dn + 1)],
                                            start=(kd == 0), stop=(kd == 7))
                        return ins
                    t_em = em.op("te", f_e)
                    em.wait("ve", t_em)
                    em.wait("ve", lu.get(("acci", tloc)))
                    t_up = em.op("ve", lambda ve, tloc=tloc, dn=dn, e=e, bank=bank,
                                 ti=ti: ve.scalar_tensor_tensor(
                                     acc[:, tloc, 512 * dn:512 * (dn + 1)],
                                     bank[:], gw_all[:, ti, e:e + 1],
                                     acc[:, tloc, 512 * dn:512 * (dn + 1)],
                                     ALU.mult, ALU.add))
                    lu[("eps", (2 * tloc + dn) % 4)] = t_up
                    lu[("accu", tloc)] = t_up
            lu[("wesl", sl)] = em.tok("te")
        for ti in tis:
            tloc = ti - 16 * half
            sl2 = ti % 2
            em.wait("sy", lu.get(("ut", 0)))
            t_u = em.dma("sy", lambda sy, ti=ti: sy.dma_start(
                out=ut[:, 0], in_=wd["u_d"][128 * ti:128 * (ti + 1), :]))
            t_s = em.dma("sy", lambda sy, ti=ti, sl2=sl2: sy.dma_start(
                out=st_[:, sl2], in_=scr["s_scr"][128 * ti:128 * (ti + 1), :]))
            em.wait("sc", lu.get(("accu", tloc)))
            em.wait("sc", lu.get("gbf"))
            t_g = em.op("sc", lambda sc, tloc=tloc: sc.activation(
                out=gbf[:], in_=acc[:, tloc], func=AF.Silu))
            lu[("accw", tloc)] = t_g
            em.wait("ve", t_g)
            em.wait("ve", t_s)
            em.wait("ve", lu.get("sg"))
            t_sg = em.op("ve", lambda ve, sl2=sl2: ve.tensor_mul(sg[:], st_[:, sl2], gbf[:]))
            lu["gbf"] = t_sg
            em.wait("ve", t_u)
            em.wait("ve", lu.get(("x1", sl2)))
            em.wait("ve", lu.get(("x1d", sl2)))
            t_x1 = em.op("ve", lambda ve, sl2=sl2: ve.tensor_add(
                x1t[:, sl2], sg[:], ut[:, 0]))
            lu["sg"] = t_x1
            lu[("ut", 0)] = t_x1
            em.wait("sy", t_x1)
            t_x1d = em.dma("sy", lambda sy, ti=ti, sl2=sl2: sy.dma_start(
                out=scr["x1_scr"][128 * ti:128 * (ti + 1), :], in_=x1t[:, sl2]))
            lu[("x1d", sl2)] = t_x1d
            x1_toks.append(t_x1d)
            cb = 8 * sl2
            col = bufs["ln_col"]
            em.wait("sc", t_x1)
            em.wait("sc", lu.get(("lncol", sl2)))
            t_sq = em.op("sc", lambda sc, sl2=sl2, cb=cb: sc.activation(
                out=sqs[:, 0:1024], in_=x1t[:, sl2], func=AF.Square,
                accum_out=col[:, cb:cb + 1]))
            em.wait("ve", t_sq)
            em.op("ve", lambda ve, sl2=sl2, cb=cb: ve.reduce_sum(
                col[:, cb + 1:cb + 2], x1t[:, sl2], axis=AX.X))
            em.op("ve", lambda ve, cb=cb: ve.tensor_scalar_mul(
                col[:, cb + 2:cb + 3], col[:, cb + 1:cb + 2], 1.0 / D))
            em.op("ve", lambda ve, cb=cb: ve.tensor_tensor(
                col[:, cb + 3:cb + 4], col[:, cb + 2:cb + 3], col[:, cb + 2:cb + 3],
                ALU.mult))
            em.op("ve", lambda ve, cb=cb: ve.tensor_scalar_mul(
                col[:, cb + 4:cb + 5], col[:, cb:cb + 1], 1.0 / D))
            em.op("ve", lambda ve, cb=cb: ve.tensor_sub(
                col[:, cb + 5:cb + 6], col[:, cb + 4:cb + 5], col[:, cb + 3:cb + 4]))
            em.op("ve", lambda ve, cb=cb: ve.tensor_scalar_add(
                col[:, cb + 6:cb + 7], col[:, cb + 5:cb + 6], EPS))
            t_rec = em.op("ve", lambda ve, cb=cb: ve.reciprocal(
                col[:, cb + 7:cb + 8], col[:, cb + 6:cb + 7]))
            em.wait("sc", t_rec)
            t_rstd = em.op("sc", lambda sc, cb=cb: sc.activation(
                out=col[:, cb + 6:cb + 7], in_=col[:, cb + 7:cb + 8], func=AF.Sqrt))
            em.wait("ve", t_rstd)
            em.wait("ve", lu.get(("xn1", sl2)))
            t_xn1 = em.op("ve", lambda ve, sl2=sl2, cb=cb: ve.tensor_scalar(
                xn1t[:, sl2], x1t[:, sl2], col[:, cb + 2:cb + 3], col[:, cb + 6:cb + 7],
                ALU.subtract, ALU.mult))
            lu[("x1", sl2)] = t_xn1
            lu[("lncol", sl2)] = t_xn1
            em.wait("sy", t_xn1)
            t_xn1d = em.dma("sy", lambda sy, ti=ti, sl2=sl2: sy.dma_start(
                out=scr["xn1_scr"][128 * ti:128 * (ti + 1), :], in_=xn1t[:, sl2]))
            lu[("xn1", sl2)] = t_xn1d
            x1_toks.append(t_xn1d)
    return x1_toks[-1]


def stage_s4(nc, em, bufs, wd, scr):
    PS = bufs["ps"]
    PS2 = bufs["ps2"]
    pbf = bufs["ps_bf"]
    identb = bufs["ident"]
    id_tok = bufs["ident_tok"]
    xn1f = bufs["s4_xn1"]
    tmt = bufs["s4_tm"]
    wsb = bufs["s4_w"]
    bsb = bufs["s4_b"]
    q_sb = bufs["s4_q"]
    k_sb = bufs["s4_k"]
    v_fm = bufs["s4_vf"]
    v_tm = bufs["s4_vt"]
    pt = bufs["s4_pt"]
    zfm = bufs["s4_zfm"]
    projsb = bufs["s4_proj"]
    ones64 = bufs["s4_ones"]
    rrow = bufs["s4_r"]
    rbf = bufs["s4_rbf"]
    bcsb = bufs["s4_bc"]
    s1b = bufs["s4_s1"]
    lu = {}

    bias_toks = []
    for x, name in ((0, "bq"), (1, "bk"), (2, "bv")):
        src = wd[name].rearrange("(c p) o -> p c o", p=128)
        bias_toks.append(em.dma("sy", lambda sy, x=x, src=src: sy.dma_start(
            out=bsb[:, x, :], in_=src[:, :, 0], allow_slow_non_contiguous=True)))
    proj_toks = [em.dma("gp", lambda gp, kc=kc: gp.dma_start(
        out=projsb[:, kc], in_=wd["proj"][128 * kc:128 * (kc + 1), :]))
        for kc in range(8)]
    t_ones = em.op("gp", lambda gp: gp.memset(ones64[:], 1.0))
    em.op("gp", lambda gp: gp.memset(v_tm[:, :, :, 64:65], 1.0))
    t_vones = em.op("gp", lambda gp: gp.memset(v_tm[:, :, :, 129:130], 1.0))

    for s in range(2):
        soff = SEQ * s
        for ti in range(16):
            sl = ti % 2
            em.wait("sy", lu.get(("tmt", sl)))
            t_ld = em.dma("sy", lambda sy, ti=ti, sl=sl, soff=soff: sy.dma_start(
                out=tmt[:, sl], in_=scr["xn1_scr"][soff + 128 * ti:soff + 128 * (ti + 1), :]))
            em.wait("te", t_ld)
            em.wait("te", id_tok)
            em.wait("te", lu.get(("pbt", ti % 2)))
            pb = pbf[ti % 2]
            def f_tr(te, sl=sl, pb=pb):
                for c in range(8):
                    ins = te.transpose(pb[:, 128 * c:128 * (c + 1)],
                                       tmt[:, sl, 128 * c:128 * (c + 1)], identb[:])
                return ins
            t_tr = em.op("te", f_tr)
            lu[("tmt", sl)] = t_tr
            em.wait("sc", t_tr)
            em.wait("sc", lu.get("xn1f_used"))
            t_ev = em.op("sc", lambda sc, ti=ti, pb=pb: sc.activation(
                out=xn1f[:, :, 128 * ti:128 * (ti + 1)],
                in_=pb[:].rearrange("p (c f) -> p c f", f=128), func=AF.Copy))
            lu[("pbt", ti % 2)] = t_ev
        t_xn1f = t_ev

        for hp in range(8):
            pp = 0
            w_toks = []
            em.wait("gp", lu.get(("wsb", pp)))
            for x, name in ((0, "WqT"), (1, "WkT"), (2, "WvT")):
                src = wd[name].rearrange("(c p) m -> p c m", p=128)
                w_toks.append(em.dma("gp", lambda gp, x=x, src=src, hp=hp, pp=pp:
                                     gp.dma_start(
                    out=wsb[:, pp, x], in_=src[:, :, 128 * hp:128 * (hp + 1)])))
            for x, dst in ((0, q_sb), (1, k_sb), (2, v_fm)):
                for tb in range(4):
                    bank = PS[tb % 2]
                    for tok in w_toks:
                        em.wait("te", tok)
                    em.wait("te", t_xn1f)
                    em.wait("te", lu.get(("bank", tb % 2)))
                    def f_mm(te, x=x, tb=tb, bank=bank, pp=pp):
                        for kd in range(8):
                            ins = te.matmul(bank[:], wsb[:, pp, x, kd],
                                            xn1f[:, kd, 512 * tb:512 * (tb + 1)],
                                            start=(kd == 0), stop=(kd == 7))
                        return ins
                    t_mm = em.op("te", f_mm)
                    em.wait("ve", t_mm)
                    for tok in bias_toks:
                        em.wait("ve", tok)
                    em.wait("ve", lu.get(("qkv", x, pp)))
                    t_qe = em.op("ve", lambda ve, x=x, tb=tb, dst=dst, bank=bank,
                                 hp=hp, pp=pp:
                                 ve.tensor_scalar_add(dst[:, pp, 512 * tb:512 * (tb + 1)],
                                                      bank[:], bsb[:, x, hp:hp + 1]))
                    lu[("bank", tb % 2)] = t_qe
                lu[("qkv", x, pp)] = t_qe
            t_qkv = t_qe
            for kc in range(16):
                pb = pbf[kc % 2]
                em.wait("te", t_qkv)
                em.wait("te", lu.get(("pbt", kc % 2)))
                em.wait("te", lu.get(("vtm_used", pp)))
                t_vt = em.op("te", lambda te, kc=kc, pb=pb, pp=pp: te.transpose(
                    pb[:, 0:128], v_fm[:, pp, 128 * kc:128 * (kc + 1)], identb[:]))
                em.wait("sc", t_vt)
                em.wait("sc", lu.get(("vtm_used", pp)))
                vt = v_tm[:, pp, kc]
                vdst = bass.AP(vt.tensor, vt.offset, [vt.ap[0], [65, 2], [1, 64]])
                t_ve = em.op("sc", lambda sc, pb=pb, vdst=vdst: sc.activation(
                    out=vdst, in_=pb[:, 0:128].rearrange("p (a b) -> p a b", b=64),
                    func=AF.Copy))
                lu[("pbt", kc % 2)] = t_ve
                lu[("qkv", 2, pp)] = t_vt
            t_vtm = t_ve
            for qg in range(4):
                nkc = 4 * qg + 4
                for kc in range(nkc):
                    wb = PS2[kc % 2]
                    em.wait("te", t_qkv)
                    em.wait("te", lu.get(("stps", kc % 2)))
                    em.wait("te", lu.get(("bank", 2 * (kc % 2))))
                    em.wait("te", lu.get(("bank", 2 * (kc % 2) + 1)))
                    def f_qk(te, kc=kc, qg=qg, wb=wb, pp=pp):
                        te.matmul(wb[:, 0:512], k_sb[0:64, pp, 128 * kc:128 * (kc + 1)],
                                  q_sb[0:64, pp, 512 * qg:512 * (qg + 1)],
                                  start=True, stop=True, tile_position=(0, 0))
                        return te.matmul(wb[:, 512:1024],
                                         k_sb[64:128, pp, 128 * kc:128 * (kc + 1)],
                                         q_sb[64:128, pp, 512 * qg:512 * (qg + 1)],
                                         start=True, stop=True, tile_position=(64, 0))
                    t_st = em.op("te", f_qk)
                    em.wait("sc", t_st)
                    em.wait("sc", lu.get(("ptw", pp, kc)))
                    def f_ex(sc, kc=kc, wb=wb, pp=pp):
                        return sc.activation(
                            out=pt[:, pp, :, kc, :],
                            in_=wb.rearrange("p (h q) -> p h q", q=512),
                            func=AF.Exp, scale=0.125)
                    t_ex = em.op("sc", f_ex)
                    lu[("stps", kc % 2)] = t_ex
                    lu[("bank", 2 * (kc % 2))] = t_ex
                    lu[("bank", 2 * (kc % 2) + 1)] = t_ex
                    if kc >= 4 * qg:
                        em.wait("gp", t_ex)
                        def f_msk(gp, kc=kc, qg=qg, pp=pp):
                            base = 512 * qg - 128 * kc
                            return gp.affine_select(
                                out=pt[:, pp, :, kc, :], in_=pt[:, pp, :, kc, :],
                                pattern=[[0, 2], [1, 512]], compare_op=ALU.is_ge,
                                fill=0.0, base=base, channel_multiplier=-1)
                        t_msk = em.op("gp", f_msk)
                        lu[("ptm", pp, kc)] = t_msk
                    else:
                        lu[("ptm", pp, kc)] = t_ex
                for hd in range(2):
                    psz = PS[4 + hd]
                    em.wait("te", lu.get(("zps", hd)))
                    em.wait("te", t_vtm)
                    for kc in range(nkc):
                        em.wait("te", lu[("ptm", pp, kc)])
                    def f_av(te, hd=hd, nkc=nkc, psz=psz, pp=pp):
                        for kc in range(nkc):
                            ins = te.matmul(psz[0:65, :],
                                            v_tm[:, pp, kc, 65 * hd:65 * (hd + 1)],
                                            pt[:, pp, hd, kc], start=(kc == 0),
                                            stop=(kc == nkc - 1))
                        return ins
                    t_av = em.op("te", f_av)
                    for kc in range(nkc):
                        lu[("ptw", pp, kc)] = t_av
                    em.wait("sc", t_av)
                    em.wait("sc", lu.get("rrow"))
                    em.op("sc", lambda sc, psz=psz: sc.activation(
                        out=rrow[:], in_=psz[64:65, :], func=AF.Ln, scale=8.0))
                    t_rb = em.op("sc", lambda sc, pp=pp: sc.activation(
                        out=rbf[:, pp], in_=rrow[:], func=AF.Exp, scale=-1.0))
                    em.wait("te", t_rb)
                    em.wait("te", t_ones)
                    t_bc = em.op("te", lambda te, psz=psz, pp=pp: te.matmul(
                        psz[64:128, :], ones64[:], rbf[:, pp], start=True, stop=True,
                        tile_position=(0, 64)))
                    em.wait("sc", t_bc)
                    em.wait("sc", lu.get(("bcsb", pp)))
                    t_bce = em.op("sc", lambda sc, psz=psz, pp=pp: sc.activation(
                        out=bcsb[:, pp], in_=psz[64:128, :][0:64, :], func=AF.Copy))
                    lu["rrow"] = t_bc
                    em.wait("ve", t_bce)
                    em.wait("ve", lu.get("zfm_used"))
                    t_zf = em.op("ve", lambda ve, hd=hd, hp=hp, qg=qg, psz=psz, pp=pp:
                                 ve.tensor_mul(
                                     zfm[64 * hd:64 * (hd + 1), hp, 512 * qg:512 * (qg + 1)],
                                     psz[0:64, :], bcsb[:, pp]))
                    lu[("bcsb", pp)] = t_zf
                    lu[("zps", hd)] = t_zf
            lu[("vtm_used", pp)] = t_av
            lu[("wsb", pp)] = t_st
            lu[("qkv", 0, pp)] = t_st
            lu[("qkv", 1, pp)] = t_st
        lu["xn1f_used"] = t_st
        for ti in range(16):
            for dn in range(2):
                bank = PS[(2 * ti + dn) % 2]
                for tok in proj_toks:
                    em.wait("te", tok)
                em.wait("te", t_zf)
                em.wait("te", lu.get(("bank", (2 * ti + dn) % 2)))
                def f_pr(te, ti=ti, dn=dn, bank=bank):
                    for kc in range(8):
                        ins = te.matmul(bank[:], zfm[:, kc, 128 * ti:128 * (ti + 1)],
                                        projsb[:, kc, 512 * dn:512 * (dn + 1)],
                                        start=(kc == 0), stop=(kc == 7))
                    return ins
                t_pr = em.op("te", f_pr)
                em.wait("sc", t_pr)
                em.wait("sc", lu.get(("s1b", ti % 2)))
                t_se = em.op("sc", lambda sc, ti=ti, dn=dn, bank=bank: sc.activation(
                    out=s1b[:, ti % 2, 512 * dn:512 * (dn + 1)], in_=bank[:], func=AF.Copy))
                lu[("bank", (2 * ti + dn) % 2)] = t_se
            em.wait("sy", t_se)
            t_sd = em.dma("sy", lambda sy, ti=ti, s=s, soff=soff: sy.dma_start(
                out=scr["s1_scr"][soff + 128 * ti:soff + 128 * (ti + 1), :],
                in_=s1b[:, ti % 2]))
            lu[("s1b", ti % 2)] = t_sd
        lu["zfm_used"] = t_pr
    return t_sd


def stage_s5(nc, em, bufs, wd, scr, out_d):
    PS = bufs["ps"]
    pbf = bufs["ps_bf"]
    identb = bufs["ident"]
    id_tok = bufs["ident_tok"]
    wq = bufs["s5_wq"]; wk = bufs["s5_wk"]; wi = bufs["s5_wi"]; w1 = bufs["s5_w1"]
    wo = bufs["s5_wo"]
    vw = bufs["s5_vw"]
    us = bufs["s5_u"]
    bb = bufs["s5_bb"]
    rw2 = bufs["s5_rw2"]
    rb1c = bufs["s5_rb1"]
    ebc = bufs["s5_ebc"]
    alc = bufs["s5_al"]
    invn = bufs["s5_invn"]
    tmt = bufs["s4_tm"]
    xnb = bufs["s5_xnb"]
    qb = bufs["s5_qb"]
    kb = bufs["s5_kb"]
    iqb = bufs["s5_iqb"]
    ikb = bufs["s5_ikb"]
    xv = bufs["s5_xv"]
    yw = bufs["s5_yw"]
    cum = bufs["s5_cum"]
    carry = bufs["s5_carry"]
    hb = bufs["s5_hb"]
    lg = bufs["s5_lg"]
    lcol = bufs["s5_lcol"]
    gwb2 = bufs["s5_gwb"]
    wT = bufs["s5_wt"]
    wexp = bufs["s5_wexp"]
    prodw = bufs["s5_prodw"]
    osum = bufs["s5_osum"]
    x1t = bufs["s5_x1t"]
    s1t = bufs["s5_s1t"]
    g1b = bufs["s5_g1"]
    lu = {}

    wtoks = []
    for dst, name in ((wq, "W_QT"), (wk, "W_KT"), (wi, "W_invT"), (w1, "rw1T"),
                      (wo, "W_OT")):
        for kd in range(8):
            wtoks.append(em.dma("gp", lambda gp, dst=dst, name=name, kd=kd:
                                 gp.dma_start(out=dst[:, kd],
                                              in_=wd[name][128 * kd:128 * (kd + 1), :])))
    for i, name in enumerate(("VfT", "WfT", "ViT", "WiT")):
        for kd in range(8):
            wtoks.append(em.dma("gp", lambda gp, i=i, name=name, kd=kd:
                                 gp.dma_start(out=vw[:, i, kd],
                                              in_=wd[name][128 * kd:128 * (kd + 1), :])))
    for i, name in enumerate(("Uf", "Ui")):
        for kc in range(4):
            wtoks.append(em.dma("gp", lambda gp, i=i, name=name, kc=kc:
                                 gp.dma_start(out=us[:, i, kc],
                                              in_=wd[name][128 * kc:128 * (kc + 1), :])))
    for i, name in enumerate(("b_fwd", "b_inv")):
        wtoks.append(em.dma("gp", lambda gp, i=i, name=name: gp.dma_start(
            out=bb[:, i], in_=wd[name][:, :])))
    for kd in range(8):
        wtoks.append(em.dma("gp", lambda gp, kd=kd: gp.dma_start(
            out=rw2[:, kd], in_=wd["rw2T"][128 * kd:128 * (kd + 1), :])))
    src = wd["rb1"].rearrange("(c p) o -> p c o", p=128)
    t_rb1 = em.dma("sy", lambda sy, src=src: sy.dma_start(
        out=rb1c[:], in_=src[:, :, 0], allow_slow_non_contiguous=True))
    em.wait("sc", t_rb1)
    e0 = wd["rb2"]
    t_eb1 = em.dma("sy", lambda sy, e0=e0: sy.dma_start(
        out=ebc[:, 0], in_=bass.AP(e0.tensor, e0.offset, [[0, 128], [1, 8]])))
    e1 = wd["expert_bias"]
    t_eb2 = em.dma("sy", lambda sy, e1=e1: sy.dma_start(
        out=ebc[:, 1], in_=bass.AP(e1.tensor, e1.offset, [[0, 128], [1, 8]])))
    a0 = wd["alpha_bi"]
    t_al = em.dma("sy", lambda sy, a0=a0: sy.dma_start(
        out=alc[:], in_=bass.AP(a0.tensor, a0.offset, [[0, 128], [1, 1]])))
    em.wait("ve", t_eb1)
    em.wait("ve", t_eb2)
    t_ebs = em.op("ve", lambda ve: ve.tensor_add(ebc[:, 0], ebc[:, 0], ebc[:, 1]))
    t_io = em.op("gp", lambda gp: gp.iota(
        invn[:], pattern=[[0, 2], [128, 16]], base=1, channel_multiplier=1,
        allow_small_or_imprecise_dtypes=True))
    em.wait("ve", t_io)
    t_invn = em.op("ve", lambda ve: ve.reciprocal(invn[:], invn[:]))
    t_cz = em.op("ve", lambda ve: ve.memset(carry[:], 0.0))
    em.op("ve", lambda ve: ve.memset(bufs["zcol128"][:], 0.0))
    lu["carry_z"] = t_cz

    def proj_into(dst, wsb, rhs_buf, tag, nk=8, bias_col=None, act=AF.Copy):
        for dq in range(8):
            bank = PS[dq % 4]
            em.wait("te", lu.get(("bank", dq % 4)))
            def f(te, dq=dq, bank=bank, wsb=wsb, rhs_buf=rhs_buf):
                for kd in range(8):
                    ins = te.matmul(bank[:, 0:512], wsb[:, kd, 128 * dq:128 * (dq + 1)],
                                    rhs_buf[:, kd], start=(kd == 0), stop=(kd == 7))
                return ins
            t_mm = em.op("te", f)
            em.wait("sc", t_mm)
            em.wait("sc", lu.get(("w5", tag)))
            kw = {}
            if bias_col is not None:
                kw["bias"] = bias_col[:, dq:dq + 1]
            t_ev = em.op("sc", lambda sc, dq=dq, bank=bank, dst=dst, kw=kw, act=act:
                         sc.activation(out=dst[:, dq], in_=bank[:, 0:512], func=act, **kw))
            lu[("bank", dq % 4)] = t_ev
        return t_ev

    t_x2d = None
    for b in range(8):
        t0 = 512 * b
        ti0 = 4 * b
        for j in range(4):
            sl = j % 2
            em.wait("sy", lu.get(("tmt5", sl)))
            t_ld = em.dma("sy", lambda sy, t0=t0, j=j, sl=sl: sy.dma_start(
                out=tmt[:, sl], in_=scr["xn1_scr"][t0 + 128 * j:t0 + 128 * (j + 1), :]))
            em.wait("te", t_ld)
            em.wait("te", id_tok)
            em.wait("te", lu.get(("pb5", j % 2)))
            em.wait("te", lu.get("xnb_used"))
            pb = pbf[j % 2]
            def f_tr(te, sl=sl, pb=pb):
                for c in range(8):
                    ins = te.transpose(pb[:, 128 * c:128 * (c + 1)],
                                       tmt[:, sl, 128 * c:128 * (c + 1)], identb[:])
                return ins
            t_tr = em.op("te", f_tr)
            lu[("tmt5", sl)] = t_tr
            em.wait("sc", t_tr)
            em.wait("sc", lu.get("xnb_used"))
            t_xe = em.op("sc", lambda sc, j=j, pb=pb: sc.activation(
                out=xnb[:, :, 128 * j:128 * (j + 1)],
                in_=pb[:].rearrange("p (c f) -> p c f", f=128), func=AF.Copy))
            lu[("pb5", j % 2)] = t_xe
        t_xnb = t_xe
        for tok in wtoks:
            em.wait("te", tok)
        em.wait("te", t_xnb)
        lu[("w5", "q")] = lu.get("qb_used")
        t_q = proj_into(qb, wq, xnb, "q")
        lu[("w5", "k")] = lu.get("kb_used")
        t_k = proj_into(kb, wk, xnb, "k")

        for br in range(2):
            if br == 1:
                em.wait("te", t_q)
                lu[("w5", "iq")] = lu.get("iqb_used")
                t_iq = proj_into(iqb, wi, qb, "iq")
                em.wait("te", t_k)
                lu[("w5", "ik")] = lu.get("br0_om")
                t_ik = proj_into(ikb, wi, kb, "ik")
            Qs = qb if br == 0 else iqb
            Ks = kb if br == 0 else ikb
            t_qs = t_q if br == 0 else t_iq
            t_ks = t_k if br == 0 else t_ik
            voff = 2 * br
            em.wait("te", t_qs)
            lu[("w5", "h")] = lu.get("hb_used")
            t_h = proj_into(hb, w1, Qs, "h", bias_col=rb1c, act=AF.Gelu)
            for qt in range(4):
                bank = PS[4]
                em.wait("te", t_h)
                em.wait("te", lu.get(("bank", 4)))
                def f_lg(te, qt=qt, bank=bank):
                    for rhc in range(8):
                        ins = te.matmul(bank[:, 0:8], hb[:, rhc, 128 * qt:128 * (qt + 1)],
                                        rw2[:, rhc], start=(rhc == 0), stop=(rhc == 7))
                    return ins
                t_lgm = em.op("te", f_lg)
                em.wait("ve", t_lgm)
                em.wait("ve", t_ebs)
                em.wait("ve", lu.get("lg_used"))
                em.op("ve", lambda ve, qt=qt, bank=bank: ve.tensor_add(
                    lg[:, qt], bank[:, 0:8], ebc[:, 0]))
                lu[("bank", 4)] = em.tok("ve")
                em.op("ve", lambda ve, qt=qt: ve.reduce_max(
                    lcol[:, 0:1], lg[:, qt], axis=AX.X, negate=True))
                t_nm = em.op("ve", lambda ve: ve.engine_nop()) if False else None
                em.wait("sc", em.tok("ve"))
                t_e = em.op("sc", lambda sc, qt=qt: sc.activation(
                    out=lg[:, qt], in_=lg[:, qt], func=AF.Exp, bias=lcol[:, 0:1]))
                em.wait("ve", t_e)
                em.op("ve", lambda ve, qt=qt: ve.reduce_sum(
                    lcol[:, 1:2], lg[:, qt], axis=AX.X))
                em.op("ve", lambda ve: ve.reciprocal(lcol[:, 2:3], lcol[:, 1:2]))
                t_wg = em.op("ve", lambda ve, qt=qt: ve.tensor_scalar_mul(
                    lg[:, qt], lg[:, qt], lcol[:, 2:3]))
                em.wait("ve", lu.get("gwb2"))
                t_gc = em.op("ve", lambda ve, qt=qt: ve.tensor_copy(gwb2[:], lg[:, qt]))
                em.wait("te", t_gc)
                em.wait("te", id_tok)
                em.wait("te", lu.get("wtps5"))
                t_wt = em.op("te", lambda te, qt=qt: te.transpose(
                    pbf[0][0:8, 0:128], gwb2[:], identb[:]))
                lu["gwb2"] = t_wt
                em.wait("ve", t_wt)
                em.wait("ve", lu.get(("wT_used", qt)))
                t_wte = em.op("ve", lambda ve, qt=qt: ve.tensor_copy(
                    wT[:, qt], pbf[0][0:8, 0:128]))
                lu["wtps5"] = t_wte
                lu["lg_used"] = t_wte
            t_wT = t_wte
            em.wait("sy", t_wT)
            t_wd = em.dma("sy", lambda sy, br=br, t0=t0: sy.dma_start(
                out=scr["wgt_scr"][br][:, t0:t0 + 512],
                in_=wT[:].rearrange("k q f -> k (q f)")))
            lu[("wT_used", 0)] = t_wd
            lu[("wT_used", 1)] = t_wd
            wg_toks = []
            for krc in range(4):
                for kk in range(2):
                    em.wait("sy", t_wd)
                    em.wait("sy", lu.get(("wexp", krc, kk)))
                    wsrc = scr["wgt_scr"]
                    woff = (br * 8 + 2 * krc + kk) * T + t0
                    wg_toks.append(em.dma("sy", lambda sy, krc=krc, kk=kk, woff=woff,
                                          wsrc=wsrc: sy.dma_start(
                        out=wexp[64 * kk:64 * (kk + 1), krc, :],
                        in_=bass.AP(wsrc.tensor, wsrc.offset + woff, [[0, 64], [1, 512]]))))
            for which, (dst, widx, rhs_buf, t_rhs) in enumerate(
                    ((xv, voff + 0, Qs, t_qs), (yw, voff + 1, Ks, t_ks))):
                for krc in range(4):
                    bank = PS[4 + (krc % 2)]
                    em.wait("te", t_rhs)
                    em.wait("te", lu.get(("bank", 4 + krc % 2)))
                    def f(te, krc=krc, widx=widx, rhs_buf=rhs_buf, bank=bank):
                        for kd in range(8):
                            ins = te.matmul(bank[:, 0:512],
                                            vw[:, widx, kd, 128 * krc:128 * (krc + 1)],
                                            rhs_buf[:, kd], start=(kd == 0), stop=(kd == 7))
                        return ins
                    t_mm = em.op("te", f)
                    em.wait("ve", t_mm)
                    em.wait("ve", lu.get(("xvyw", which, krc)))
                    t_ev = em.op("ve", lambda ve, krc=krc, dst=dst, bank=bank:
                                 ve.tensor_copy(dst[:, krc], bank[:, 0:512]))
                    lu[("bank", 4 + krc % 2)] = t_ev
                    lu[("xvyw2", which, krc)] = t_ev
            if b == 4 and br == 0:
                em.op("ve", lambda ve: ve.memset(carry[:], 0.0))
            for krc in range(4):
                em.wait("ve", lu[("xvyw2", 1, krc)])
                def f_scan(ve, br=br, krc=krc):
                    zb = bufs["zcol128"][:]
                    z0 = bass.AP(zb.tensor, zb.offset, [zb.ap[0], [0, 512]])
                    return ve.tensor_tensor_scan(
                        out=cum[:, krc], data0=yw[:, krc], data1=z0,
                        initial=carry[:, br, krc:krc + 1], op0=ALU.add, op1=ALU.add)
                t_sc = em.op("ve", f_scan)
                t_cu = em.op("ve", lambda ve, br=br, krc=krc: ve.tensor_copy(
                    carry[:, br, krc:krc + 1], cum[:, krc, 511:512]))
                lu[("cum", krc)] = t_cu
            lu["carry_z"] = t_cu
            pw_toks = []
            for krc in range(4):
                em.wait("ve", lu[("cum", krc)])
                em.wait("ve", lu[("xvyw2", 0, krc)])
                em.wait("ve", wg_toks[2 * krc])
                em.wait("ve", wg_toks[2 * krc + 1])
                em.wait("ve", lu.get(("prodw", krc)))
                em.op("ve", lambda ve, krc=krc: ve.tensor_mul(
                    xv[:, krc], xv[:, krc], cum[:, krc]))
                lu[("xvyw", 0, krc)] = em.tok("ve")
                t_pw = em.op("ve", lambda ve, krc=krc: ve.tensor_mul(
                    prodw[:, krc], xv[:, krc], wexp[:, krc]))
                lu[("xvyw", 1, krc)] = t_pw
                lu[("wexp", krc, 0)] = t_pw
                lu[("wexp", krc, 1)] = t_pw
                pw_toks.append(t_pw)
            for dq in range(8):
                bank = PS[4 + dq % 2]
                for tok in pw_toks:
                    em.wait("te", tok)
                em.wait("te", t_wT)
                em.wait("te", lu.get(("bank", 4 + dq % 2)))
                def f_o(te, dq=dq, br=br, bank=bank):
                    for krc in range(4):
                        te.matmul(bank[:, 0:512], us[:, br, krc, 128 * dq:128 * (dq + 1)],
                                  prodw[:, krc], start=(krc == 0), stop=False)
                    return te.matmul(bank[:, 0:512], bb[:, br, 128 * dq:128 * (dq + 1)],
                                     wT[:].rearrange("k q f -> k (q f)"),
                                     start=False, stop=True)
                t_om = em.op("te", f_o)
                if br == 0:
                    em.wait("sc", t_om)
                    em.wait("sc", lu.get(("osum", dq)))
                    t_oe = em.op("sc", lambda sc, dq=dq, bank=bank: sc.activation(
                        out=osum[:, dq], in_=bank[:, 0:512], func=AF.Copy))
                else:
                    em.wait("ve", t_om)
                    em.wait("ve", t_al)
                    em.wait("ve", lu.get(("osum2", dq)))
                    t_oe = em.op("ve", lambda ve, dq=dq, bank=bank: ve.scalar_tensor_tensor(
                        osum[:, dq], bank[:, 0:512], alc[:, 0:1], osum[:, dq],
                        ALU.mult, ALU.add))
                    lu[("osum", dq)] = t_oe
                lu[("bank", 4 + dq % 2)] = t_oe
                lu[("osum2", dq)] = t_oe
            for krc in range(4):
                lu[("prodw", krc)] = t_om
            lu["qb_used"] = t_om if br == 1 else lu.get("qb_used")
        lu["kb_used"] = t_om
        lu["iqb_used"] = t_om
        lu["ikb_used"] = t_om
        lu["qb_used"] = t_om
        lu["hb_used"] = t_om
        lu["xnb_used"] = t_ik
        for qt in range(4):
            sl = 0
            em.wait("sy", lu.get(("tmt5", 0)))
            em.wait("sy", lu.get(("tmt5", 1)))
            em.wait("sy", lu.get(("x1t", sl)))
            t_x1 = em.dma("sy", lambda sy, t0=t0, qt=qt, sl=sl: sy.dma_start(
                out=x1t[:, sl], in_=scr["x1_scr"][t0 + 128 * qt:t0 + 128 * (qt + 1), :]))
            t_s1 = em.dma("sy", lambda sy, t0=t0, qt=qt, sl=sl: sy.dma_start(
                out=s1t[:, sl], in_=scr["s1_scr"][t0 + 128 * qt:t0 + 128 * (qt + 1), :]))
            for dn in range(2):
                bank = PS[dn % 2]
                for dq_ in range(8):
                    em.wait("te", lu[("osum2", dq_)])
                em.wait("te", lu.get(("bank", dn % 2)))
                def f_wo(te, qt=qt, dn=dn, bank=bank):
                    for kd in range(8):
                        ins = te.matmul(bank[:], osum[:, kd, 128 * qt:128 * (qt + 1)],
                                        wo[:, kd, 512 * dn:512 * (dn + 1)],
                                        start=(kd == 0), stop=(kd == 7))
                    return ins
                t_wom = em.op("te", f_wo)
                em.wait("sc", t_wom)
                em.wait("sc", t_invn)
                em.wait("sc", lu.get("g1_used"))
                t_g1 = em.op("sc", lambda sc, dn=dn, bank=bank, ti0=ti0, qt=qt:
                             sc.activation(out=g1b[:, 512 * dn:512 * (dn + 1)],
                                           in_=bank[:], func=AF.Silu,
                                           scale=invn[:, ti0 + qt:ti0 + qt + 1]))
                lu[("bank", dn % 2)] = t_g1
            em.wait("ve", t_g1)
            em.wait("ve", t_s1)
            t_sg = em.op("ve", lambda ve, sl=sl: ve.tensor_mul(
                g1b[:], s1t[:, sl], g1b[:]))
            em.wait("ve", t_x1)
            em.wait("ve", lu.get(("x2", sl)))
            t_x2 = em.op("ve", lambda ve, sl=sl: ve.tensor_add(
                x1t[:, sl], x1t[:, sl], g1b[:]))
            lu["g1_used"] = t_x2
            lu[("x1t", sl)] = t_x2
            em.wait("sy", t_x2)
            t_x2d = em.dma("sy", lambda sy, t0=t0, qt=qt, sl=sl: sy.dma_start(
                out=out_d[t0 + 128 * qt:t0 + 128 * (qt + 1), :], in_=x1t[:, sl]))
            lu[("x2", sl)] = t_x2d
            lu[("tmt5", 0)] = t_x2d
            lu[("tmt5", 1)] = t_x2d
        lu["osum_used"] = t_wom
        for dq in range(8):
            lu[("osum", dq)] = t_wom
    return t_x2d



ARENA_UNITS = 106240


def build_fused_program():
    nc = bass.Bass("TRN2", target_bir_lowering=False)
    wd = {}
    wd["u_d"] = nc.dram_tensor("u_d", [T, D], F32, kind="ExternalInput").ap()
    wd["W_inT"] = nc.dram_tensor("W_inT", [1024, 4368], F32, kind="ExternalInput").ap()
    wd["conv_w"] = nc.dram_tensor("conv_w", [NJC, 128, 4], F32, kind="ExternalInput").ap()
    wd["conv_b"] = nc.dram_tensor("conv_b", [NJC, 128, 1], F32, kind="ExternalInput").ap()
    wd["dt_bias"] = nc.dram_tensor("dt_bias", [16, 1], F32, kind="ExternalInput").ap()
    wd["A_log"] = nc.dram_tensor("A_log", [16, 1], F32, kind="ExternalInput").ap()
    wd["Dh"] = nc.dram_tensor("Dh", [1, 16], F32, kind="ExternalInput").ap()
    wd["norm_weight"] = nc.dram_tensor("norm_weight", [1, DIN], F32, kind="ExternalInput").ap()
    wd["W_outT"] = nc.dram_tensor("W_outT", [DIN, D], F32, kind="ExternalInput").ap()
    wd["Wg"] = nc.dram_tensor("Wg", [1024, 8], F32, kind="ExternalInput").ap()
    wd["bg"] = nc.dram_tensor("bg", [1, 8], F32, kind="ExternalInput").ap()
    wd["We"] = nc.dram_tensor("We", [8, 1024, 1024], F32, kind="ExternalInput").ap()
    wd["be"] = nc.dram_tensor("be", [8, 1024], F32, kind="ExternalInput").ap()
    for n_ in ("WqT", "WkT", "WvT", "proj", "W_QT", "W_KT", "W_invT", "rw1T", "W_OT"):
        wd[n_] = nc.dram_tensor(n_, [1024, 1024], F32, kind="ExternalInput").ap()
    for n_ in ("bq", "bk", "bv", "rb1"):
        wd[n_] = nc.dram_tensor(n_, [1024, 1], F32, kind="ExternalInput").ap()
    for n_ in ("VfT", "WfT", "ViT", "WiT"):
        wd[n_] = nc.dram_tensor(n_, [1024, 512], F32, kind="ExternalInput").ap()
    for n_ in ("Uf", "Ui"):
        wd[n_] = nc.dram_tensor(n_, [512, 1024], F32, kind="ExternalInput").ap()
    for n_ in ("b_fwd", "b_inv"):
        wd[n_] = nc.dram_tensor(n_, [8, 1024], F32, kind="ExternalInput").ap()
    wd["rw2T"] = nc.dram_tensor("rw2T", [1024, 8], F32, kind="ExternalInput").ap()
    wd["rb2"] = nc.dram_tensor("rb2", [1, 8], F32, kind="ExternalInput").ap()
    wd["expert_bias"] = nc.dram_tensor("expert_bias", [1, 8], F32, kind="ExternalInput").ap()
    wd["alpha_bi"] = nc.dram_tensor("alpha_bi", [1, 1], F32, kind="ExternalInput").ap()
    out_d = nc.dram_tensor("x2", [T, D], F32, kind="ExternalOutput").ap()

    scr = {
        "xn_scr": nc.dram_tensor("xn_scr", [D, T], BF16, kind="Internal").ap(),
        "z_scr": nc.dram_tensor("z_scr", [T, DIN], BF16, kind="Internal").ap(),
        "xfm_scr": nc.dram_tensor("xfm_scr", [DIN, T], BF16, kind="Internal").ap(),
        "bfm_scr": nc.dram_tensor("bfm_scr", [128, T], BF16, kind="Internal").ap(),
        "cfm_scr": nc.dram_tensor("cfm_scr", [128, T], BF16, kind="Internal").ap(),
        "dtr_scr": nc.dram_tensor("dtr_scr", [16, T], F32, kind="Internal").ap(),
        "acs_scr": nc.dram_tensor("acs_scr", [16, T], F32, kind="Internal").ap(),
        "cdec_scr": nc.dram_tensor("cdec_scr", [2, 16, 8], F32, kind="Internal").ap(),
        "s_scr": nc.dram_tensor("s_scr", [T, D], BF16, kind="Internal").ap(),
        "x1_scr": nc.dram_tensor("x1_scr", [T, D], F32, kind="Internal").ap(),
        "xn1_scr": nc.dram_tensor("xn1_scr", [T, D], BF16, kind="Internal").ap(),
        "s1_scr": nc.dram_tensor("s1_scr", [T, D], BF16, kind="Internal").ap(),
        "wgt_scr": nc.dram_tensor("wgt_scr", [2, 8, T], BF16, kind="Internal").ap(),
    }

    em = Em()
    with contextlib.ExitStack() as stx:
        bufs = {}
        def sb(name, shape, dtype):
            return stx.enter_context(nc.sbuf_tensor(name, shape, dtype))
        bufs["ident"] = sb("ident", [128, 128], BF16)
        bufs["ln_col"] = sb("ln_col", [128, 16], F32)
        AR = sb("AR", [128, ARENA_UNITS], BF16)
        ar = Arena(AR, ARENA_UNITS)
        mk_psum_pool(nc, stx, bufs)
        sems = em.make_sems(nc, stx)
        blk = stx.enter_context(nc.Block())

        bufs["ident_tok"] = build_identity(nc, em, bufs)

        xn = ar.alloc([128, 8, T], BF16)
        bufs["xn_fm"] = xn
        m_xn = ar.mark()
        bufs["ln_x"] = ar.alloc([128, 2, D], F32)
        bufs["ln_sq"] = ar.alloc([128, D], F32)
        bufs["ln_xn"] = ar.alloc([128, 2, D], BF16)
        t_fm = stage_ln(nc, em, bufs, wd["u_d"], xn, key="ln")
        em.barrier()
        dump_fm(em, xn, scr["xn_scr"], 8, T)
        ar.release(m_xn)
        mk_s1_bufs(nc, stx, bufs, alloc=ar.alloc)
        stage_s1(nc, em, bufs, wd, scr, t_fm)
        em.barrier()
        ar.release(0)

        mk_s2_bufs(nc, stx, bufs, alloc=ar.alloc)
        bufs["identf_tok"] = build_identity_f32(nc, em, bufs)
        stage_s2(nc, em, bufs, wd, scr)
        em.barrier()
        ar.release(0)

        xn2 = ar.alloc([128, 8, T], BF16)
        bufs["xn_fm"] = xn2
        xn_tok = load_fm(em, xn2, scr["xn_scr"], 8, T)
        mk_s3_bufs(nc, stx, bufs, alloc=ar.alloc)
        bufs["s1_acc"] = ar.alloc([128, 1024], F32)
        stage_s3(nc, em, bufs, wd, scr, xn_tok)
        em.barrier()
        ar.release(0)

        mk_s4_bufs(nc, stx, bufs, alloc=ar.alloc)
        stage_s4(nc, em, bufs, wd, scr)
        em.barrier()
        ar.release(0)

        mk_s5_bufs(nc, stx, bufs, alloc=ar.alloc)
        out_tok = stage_s5(nc, em, bufs, wd, scr, out_d)
        em.barrier()
        em.wait("sy", out_tok)
        em.emit(nc, blk, sems)
    return nc


_FUSED_NC = None


def kernel_fused(**inputs):
    global _FUSED_NC
    w = host_prep(inputs)
    u = np.asarray(inputs["L0_attn_u"])
    cores = list(range(NCORES))
    if _FUSED_NC is None:
        _FUSED_NC = build_fused_program()
    in_maps = []
    for c in cores:
        m = dict(w)
        m["u_d"] = np.ascontiguousarray(u[2 * c:2 * c + 2].reshape(T, D))
        in_maps.append(m)
    res = run_bass_kernel_spmd(_FUSED_NC, in_maps, core_ids=cores)
    out = np.stack([np.asarray(res.results[c]["x2"]).reshape(2, SEQ, D)
                    for c in cores])
    return out.reshape(16, SEQ, D)



_S0 = None
_S1 = None
_S2 = None
_S3 = None
_S4 = None
_S5 = None


def _programs():
    global _S0, _S1, _S2, _S3, _S4, _S5
    if _S0 is None:
        _S0 = build_s0_program()
        _S1 = build_s1_program()
        _S2 = build_s2_program()
        _S3 = build_s3_program()
        _S4 = build_s4_program()
        _S5 = build_s5_program()
    return _S0, _S1, _S2, _S3, _S4, _S5


FUSED = True


def kernel(**inputs):
    if FUSED:
        return kernel_fused(**inputs)
    w = host_prep(inputs)
    u = np.asarray(inputs["L0_attn_u"])
    cores = list(range(NCORES))
    s0, s1, s2, s3, s4, s5 = _programs()

    u_c = [np.ascontiguousarray(u[2 * c:2 * c + 2].reshape(T, D)) for c in cores]

    r0 = run_bass_kernel_spmd(s0, [dict(u=u_c[c]) for c in cores], core_ids=cores)
    xn = [r0.results[c]["xn_fm"] for c in cores]

    r1 = run_bass_kernel_spmd(s1, [dict(
        xn_fm=xn[c], W_inT=w["W_inT"], conv_w=w["conv_w"], conv_b=w["conv_b"])
        for c in cores], core_ids=cores)

    r2 = run_bass_kernel_spmd(s2, [dict(
        dt_bias=w["dt_bias"], A_log=w["A_log"], Dh=w["Dh"],
        norm_weight=w["norm_weight"], W_outT=w["W_outT"],
        xfm_scr=r1.results[c]["xfm_scr"], bfm_scr=r1.results[c]["bfm_scr"],
        cfm_scr=r1.results[c]["cfm_scr"], dtr_scr=r1.results[c]["dtr_scr"],
        z_scr=r1.results[c]["z_scr"]) for c in cores], core_ids=cores)

    r3 = run_bass_kernel_spmd(s3, [dict(
        xn_fm=xn[c], Wg=w["Wg"], bg=w["bg"], We=w["We"], be=w["be"],
        u_d=u_c[c], s_scr=r2.results[c]["s_scr"]) for c in cores], core_ids=cores)

    r4 = run_bass_kernel_spmd(s4, [dict(
        WqT=w["WqT"], WkT=w["WkT"], WvT=w["WvT"], bq=w["bq"], bk=w["bk"],
        bv=w["bv"], proj=w["proj"], xn1_scr=r3.results[c]["xn1_scr"])
        for c in cores], core_ids=cores)

    in5 = dict(W_QT=w["W_QT"], W_KT=w["W_KT"], W_invT=w["W_invT"], rw1T=w["rw1T"],
               W_OT=w["W_OT"], VfT=w["VfT"], WfT=w["WfT"], ViT=w["ViT"],
               WiT=w["WiT"], Uf=w["Uf"], Ui=w["Ui"], b_fwd=w["b_fwd"],
               b_inv=w["b_inv"], rw2T=w["rw2T"], rb1=w["rb1"], rb2=w["rb2"],
               expert_bias=w["expert_bias"], alpha_bi=w["alpha_bi"])
    r5 = run_bass_kernel_spmd(s5, [dict(
        xn1_scr=r3.results[c]["xn1_scr"],
        x1_scr=r3.results[c]["x1_scr"], s1_scr=r4.results[c]["s1_scr"], **in5)
        for c in cores], core_ids=cores)

    out = np.stack([np.asarray(r5.results[c]["x2"]).reshape(2, SEQ, D)
                    for c in cores])
    return out.reshape(16, SEQ, D)



def dump_fm(em, sbuf_fm, dram_fm, nchunks, ncols, q="sy"):
    dst = dram_fm.rearrange("(c p) t -> p c t", p=128)
    return em.dma(q, lambda e: e.dma_start(out=dst, in_=sbuf_fm[:, 0:nchunks, 0:ncols]))


def build_identity_f32(nc, em, bufs):
    identf = bufs["ident_f32"]
    em.op("gp", lambda gp: gp.memset(identf[:], 1.0))
    return em.op("gp", lambda gp: gp.affine_select(
        out=identf[:], in_=identf[:], pattern=[[-1, 128]],
        compare_op=ALU.is_equal, fill=0.0, base=0, channel_multiplier=1))


def mk_s2_bufs(nc, stx, bufs, alloc=None):
    def sb(name, shape, dtype):
        if alloc is not None:
            return alloc(shape, dtype)
        return stx.enter_context(nc.sbuf_tensor(name, shape, dtype))
    bufs["ident_f32"] = sb("ident_f32", [128, 128], F32)
    regA = sb("s2_regA", [128, 8192], BF16)
    bufs["s2_dtr"] = carve(regA, 0, [16, T], F32)
    bufs["s2_carry"] = carve(regA, 0, [128, 2, 16, 128], F32)
    region = sb("s2_region", [128, 16384], BF16)
    bufs["s2_dt"] = carve(region, 0, [16, T], F32)
    bufs["s2_t16"] = carve(region, 8192, [16, T], F32)
    bufs["s2_wo"] = carve(region, 0, [128, 16, 1024], BF16)
    bufs["s2_acs"] = bufs["s2_dtr"]
    bufs["s2_bcol"] = sb("s2_bcol", [16, 4], F32)
    bufs["zcol16"] = sb("zcol16", [16, 1], F32)
    bufs["s2_cd16"] = sb("s2_cd16", [16, 8], F32)
    bufs["s2_decfm"] = sb("s2_decfm", [16, 256], F32)
    bufs["s2_dtT"] = sb("s2_dtT", [128, 32, 16], F32)
    bufs["s2_nacsT"] = sb("s2_nacsT", [128, 32, 16], F32)
    bufs["s2_eacsT"] = sb("s2_eacsT", [128, 32, 16], F32)
    bufs["s2_sfacT"] = sb("s2_sfacT", [128, 32, 16], F32)
    bufs["s2_cdbc"] = sb("s2_cdbc", [128, 2, 128], F32)
    bufs["s2_dbc"] = sb("s2_dbc", [128, 16], F32)
    bufs["s2_nwbc"] = sb("s2_nwbc", [128, 2048], BF16)
    bufs["s2_xch"] = sb("s2_xch", [128, 2, 16, 256], BF16)
    bufs["s2_xtm"] = sb("s2_xtm", [128, 2, 2, 2048], BF16)
    bufs["s2_bc"] = sb("s2_bc", [128, 2, 256], BF16)
    bufs["s2_cc"] = sb("s2_cc", [128, 2, 256], BF16)
    bufs["s2_btm"] = sb("s2_btm", [128, 2, 2, 128], BF16)
    bufs["s2_bsa"] = sb("s2_bsa", [128, 2, 16, 128], BF16)
    bufs["s2_cbt"] = sb("s2_cbt", [128, 2, 2, 256], BF16)
    bufs["s2_seg1"] = sb("s2_seg1", [128, 2, 8, 256], F32)
    bufs["s2_mta"] = sb("s2_mta", [128, 2, 2, 16, 256], BF16)
    bufs["s2_di"] = sb("s2_di", [128, 16, 128], BF16)
    bufs["s2_carrybf"] = sb("s2_carrybf", [128, 2, 16, 128], BF16)
    bufs["s2_yo"] = sb("s2_yo", [128, 2, 2, 128], F32)
    bufs["s2_ytm"] = sb("s2_ytm", [128, 2, 2048], BF16)
    bufs["s2_ztm"] = sb("s2_ztm", [128, 2, 2048], BF16)
    bufs["s2_yfm"] = sb("s2_yfm", [128, 16, 256], BF16)
    bufs["s2_rcol"] = sb("s2_rcol", [128, 8], F32)
    bufs["s2_sout"] = sb("s2_sout", [128, 2, 1024], BF16)


def build_s2_program():
    nc = bass.Bass("TRN2", target_bir_lowering=False)
    wd = {
        "dt_bias": nc.dram_tensor("dt_bias", [16, 1], F32, kind="ExternalInput").ap(),
        "A_log": nc.dram_tensor("A_log", [16, 1], F32, kind="ExternalInput").ap(),
        "Dh": nc.dram_tensor("Dh", [1, 16], F32, kind="ExternalInput").ap(),
        "norm_weight": nc.dram_tensor("norm_weight", [1, DIN], F32, kind="ExternalInput").ap(),
        "W_outT": nc.dram_tensor("W_outT", [DIN, D], F32, kind="ExternalInput").ap(),
    }
    scr = {
        "xfm_scr": nc.dram_tensor("xfm_scr", [DIN, T], BF16, kind="ExternalInput").ap(),
        "bfm_scr": nc.dram_tensor("bfm_scr", [128, T], BF16, kind="ExternalInput").ap(),
        "cfm_scr": nc.dram_tensor("cfm_scr", [128, T], BF16, kind="ExternalInput").ap(),
        "dtr_scr": nc.dram_tensor("dtr_scr", [16, T], F32, kind="ExternalInput").ap(),
        "z_scr": nc.dram_tensor("z_scr", [T, DIN], BF16, kind="ExternalInput").ap(),
        "acs_scr": nc.dram_tensor("acs_scr", [16, T], F32, kind="Internal").ap(),
        "cdec_scr": nc.dram_tensor("cdec_scr", [2, 16, 8], F32, kind="Internal").ap(),
        "s_scr": nc.dram_tensor("s_scr", [T, D], BF16, kind="ExternalOutput").ap(),
    }
    em = Em()
    with contextlib.ExitStack() as stx:
        bufs = {}
        def sb(name, shape, dtype):
            return stx.enter_context(nc.sbuf_tensor(name, shape, dtype))
        bufs["ident"] = sb("ident", [128, 128], BF16)
        mk_s2_bufs(nc, stx, bufs)
        mk_psum_pool(nc, stx, bufs)
        sems = em.make_sems(nc, stx)
        blk = stx.enter_context(nc.Block())

        bufs["ident_tok"] = build_identity(nc, em, bufs)
        bufs["identf_tok"] = build_identity_f32(nc, em, bufs)
        stage_s2(nc, em, bufs, wd, scr)
        em.fence("sy")
        em.emit(nc, blk, sems)
    return nc


def load_fm(em, sbuf_fm, dram_fm, nchunks, ncols, q="sy"):
    src = dram_fm.rearrange("(c p) t -> p c t", p=128)
    return em.dma(q, lambda e: e.dma_start(out=sbuf_fm[:, 0:nchunks, 0:ncols], in_=src))


def mk_s1_bufs(nc, stx, bufs, alloc=None):
    def sb(name, shape, dtype):
        if alloc is not None:
            return alloc(shape, dtype)
        return stx.enter_context(nc.sbuf_tensor(name, shape, dtype))
    bufs["s1_w"] = sb("s1_w", [128, 8, 2304], BF16)
    bufs["s1_wdt"] = sb("s1_wdt", [128, 8, 16], BF16)
    bufs["s1_z"] = sb("s1_z", [128, 2, 2048], BF16)
    bufs["s1_raw"] = sb("s1_raw", [128, 4102], F32)
    bufs["s1_acc"] = sb("s1_acc", [128, 4096], F32)
    bufs["s1_act"] = sb("s1_act", [128, 4096], BF16)
    bufs["s1_cw"] = sb("s1_cw", [128, 2, 4], F32)
    bufs["s1_cb"] = sb("s1_cb", [128, 2, 1], F32)
    bufs["s1_dtr"] = sb("s1_dtr", [16, T], F32)


def build_s1_program():
    nc = bass.Bass("TRN2", target_bir_lowering=False)
    xn_d = nc.dram_tensor("xn_fm", [D, T], BF16, kind="ExternalInput").ap()
    wd = {
        "W_inT": nc.dram_tensor("W_inT", [1024, 4368], F32, kind="ExternalInput").ap(),
        "conv_w": nc.dram_tensor("conv_w", [NJC, 128, 4], F32, kind="ExternalInput").ap(),
        "conv_b": nc.dram_tensor("conv_b", [NJC, 128, 1], F32, kind="ExternalInput").ap(),
    }
    scr = {
        "z_scr": nc.dram_tensor("z_scr", [T, DIN], BF16, kind="ExternalOutput").ap(),
        "xfm_scr": nc.dram_tensor("xfm_scr", [DIN, T], BF16, kind="ExternalOutput").ap(),
        "bfm_scr": nc.dram_tensor("bfm_scr", [128, T], BF16, kind="ExternalOutput").ap(),
        "cfm_scr": nc.dram_tensor("cfm_scr", [128, T], BF16, kind="ExternalOutput").ap(),
        "dtr_scr": nc.dram_tensor("dtr_scr", [16, T], F32, kind="ExternalOutput").ap(),
    }
    em = Em()
    with contextlib.ExitStack() as stx:
        bufs = {}
        def sb(name, shape, dtype):
            return stx.enter_context(nc.sbuf_tensor(name, shape, dtype))
        bufs["xn_fm"] = sb("xn_fm_sb", [128, 8, T], BF16)
        mk_s1_bufs(nc, stx, bufs)
        mk_psum_pool(nc, stx, bufs)
        sems = em.make_sems(nc, stx)
        blk = stx.enter_context(nc.Block())

        xn_tok = load_fm(em, bufs["xn_fm"], xn_d, 8, T)
        for e in ("te", "ve", "sc", "gp"):
            em.wait(e, xn_tok)
        stage_s1(nc, em, bufs, wd, scr, xn_tok)
        em.fence("sy")
        em.emit(nc, blk, sems)
    return nc


def mk_s3_bufs(nc, stx, bufs, alloc=None):
    def sb(name, shape, dtype):
        if alloc is not None:
            return alloc(shape, dtype)
        return stx.enter_context(nc.sbuf_tensor(name, shape, dtype))
    bufs["s3_wg"] = sb("s3_wg", [128, 8, 8], BF16)
    bufs["s3_bgbc"] = sb("s3_bgbc", [128, 8], F32)
    bufs["s3_be"] = sb("s3_be", [8, 1024], BF16)
    bufs["s3_gcl"] = sb("s3_gcl", [128, 4, 32], F32)
    bufs["s3_gsc"] = sb("s3_gsc", [128, 32, 8], F32)
    bufs["s3_gax1"] = sb("s3_gax1", [128, 32, 8], F32)
    bufs["s3_gax2"] = sb("s3_gax2", [128, 32, 8], F32)
    bufs["s3_gw"] = bufs["s3_gax2"]
    bufs["s3_gwb"] = sb("s3_gwb", [128, 32, 8], BF16)
    bufs["s3_wt"] = sb("s3_wt", [8, 32, 128], BF16)
    bufs["s3_we"] = sb("s3_we", [128, 2, 8, 1024], BF16)
    bufs["s3_acc"] = sb("s3_acc", [128, 16, 1024], F32)
    bufs["s3_u"] = sb("s3_u", [128, 1, 1024], F32)
    bufs["s3_s"] = sb("s3_s", [128, 2, 1024], BF16)
    bufs["s3_g"] = sb("s3_g", [128, 1024], BF16)
    bufs["s3_sg"] = sb("s3_sg", [128, 1024], F32)
    bufs["s3_x1"] = sb("s3_x1", [128, 2, 1024], F32)
    bufs["s3_xn1"] = sb("s3_xn1", [128, 2, 1024], BF16)


def build_s3_program():
    nc = bass.Bass("TRN2", target_bir_lowering=False)
    xn_d = nc.dram_tensor("xn_fm", [D, T], BF16, kind="ExternalInput").ap()
    wd = {
        "Wg": nc.dram_tensor("Wg", [1024, 8], F32, kind="ExternalInput").ap(),
        "bg": nc.dram_tensor("bg", [1, 8], F32, kind="ExternalInput").ap(),
        "We": nc.dram_tensor("We", [8, 1024, 1024], F32, kind="ExternalInput").ap(),
        "be": nc.dram_tensor("be", [8, 1024], F32, kind="ExternalInput").ap(),
        "u_d": nc.dram_tensor("u_d", [T, D], F32, kind="ExternalInput").ap(),
    }
    scr = {
        "s_scr": nc.dram_tensor("s_scr", [T, D], BF16, kind="ExternalInput").ap(),
        "x1_scr": nc.dram_tensor("x1_scr", [T, D], F32, kind="ExternalOutput").ap(),
        "xn1_scr": nc.dram_tensor("xn1_scr", [T, D], BF16, kind="ExternalOutput").ap(),
    }
    em = Em()
    with contextlib.ExitStack() as stx:
        bufs = {}
        def sb(name, shape, dtype):
            return stx.enter_context(nc.sbuf_tensor(name, shape, dtype))
        bufs["ident"] = sb("ident", [128, 128], BF16)
        bufs["xn_fm"] = sb("xn_fm_sb", [128, 8, T], BF16)
        bufs["ln_col"] = sb("ln_col", [128, 16], F32)
        bufs["s1_acc"] = sb("s1_acc", [128, 1024], F32)
        mk_s3_bufs(nc, stx, bufs)
        mk_psum_pool(nc, stx, bufs)
        sems = em.make_sems(nc, stx)
        blk = stx.enter_context(nc.Block())

        bufs["ident_tok"] = build_identity(nc, em, bufs)
        xn_tok = load_fm(em, bufs["xn_fm"], xn_d, 8, T)
        stage_s3(nc, em, bufs, wd, scr, xn_tok)
        em.fence("sy")
        em.emit(nc, blk, sems)
    return nc


def mk_s4_bufs(nc, stx, bufs, alloc=None):
    def sb(name, shape, dtype):
        if alloc is not None:
            return alloc(shape, dtype)
        return stx.enter_context(nc.sbuf_tensor(name, shape, dtype))
    bufs["s4_xn1"] = sb("s4_xn1", [128, 8, 2048], BF16)
    bufs["s4_tm"] = sb("s4_tm", [128, 2, 1024], BF16)
    bufs["s4_w"] = sb("s4_w", [128, 2, 3, 8, 128], BF16)
    bufs["s4_b"] = sb("s4_b", [128, 3, 8], F32)
    bufs["s4_q"] = sb("s4_q", [128, 2, 2048], BF16)
    bufs["s4_k"] = sb("s4_k", [128, 2, 2048], BF16)
    bufs["s4_vf"] = sb("s4_vf", [128, 2, 2048], BF16)
    bufs["s4_vt"] = sb("s4_vt", [128, 2, 16, 130], BF16)
    bufs["s4_pt"] = sb("s4_pt", [128, 2, 2, 16, 512], BF16)
    bufs["s4_zfm"] = sb("s4_zfm", [128, 8, 2048], BF16)
    bufs["s4_proj"] = sb("s4_proj", [128, 8, 1024], BF16)
    bufs["s4_ones"] = sb("s4_ones", [1, 64], BF16)
    bufs["s4_r"] = sb("s4_r", [1, 512], F32)
    bufs["s4_rbf"] = sb("s4_rbf", [1, 2, 512], BF16)
    bufs["s4_bc"] = sb("s4_bc", [64, 2, 512], F32)
    bufs["s4_s1"] = sb("s4_s1", [128, 2, 1024], BF16)


def build_s4_program():
    nc = bass.Bass("TRN2", target_bir_lowering=False)
    wd = {
        "WqT": nc.dram_tensor("WqT", [1024, 1024], F32, kind="ExternalInput").ap(),
        "WkT": nc.dram_tensor("WkT", [1024, 1024], F32, kind="ExternalInput").ap(),
        "WvT": nc.dram_tensor("WvT", [1024, 1024], F32, kind="ExternalInput").ap(),
        "bq": nc.dram_tensor("bq", [1024, 1], F32, kind="ExternalInput").ap(),
        "bk": nc.dram_tensor("bk", [1024, 1], F32, kind="ExternalInput").ap(),
        "bv": nc.dram_tensor("bv", [1024, 1], F32, kind="ExternalInput").ap(),
        "proj": nc.dram_tensor("proj", [1024, 1024], F32, kind="ExternalInput").ap(),
    }
    scr = {
        "xn1_scr": nc.dram_tensor("xn1_scr", [T, D], BF16, kind="ExternalInput").ap(),
        "s1_scr": nc.dram_tensor("s1_scr", [T, D], BF16, kind="ExternalOutput").ap(),
    }
    em = Em()
    with contextlib.ExitStack() as stx:
        bufs = {}
        def sb(name, shape, dtype):
            return stx.enter_context(nc.sbuf_tensor(name, shape, dtype))
        bufs["ident"] = sb("ident", [128, 128], BF16)
        mk_s4_bufs(nc, stx, bufs)
        mk_psum_pool(nc, stx, bufs)
        sems = em.make_sems(nc, stx)
        blk = stx.enter_context(nc.Block())

        bufs["ident_tok"] = build_identity(nc, em, bufs)
        stage_s4(nc, em, bufs, wd, scr)
        em.fence("sy")
        em.emit(nc, blk, sems)
    return nc


def mk_s5_bufs(nc, stx, bufs, alloc=None):
    def sb(name, shape, dtype):
        if alloc is not None:
            return alloc(shape, dtype)
        return stx.enter_context(nc.sbuf_tensor(name, shape, dtype))
    bufs["s5_wq"] = sb("s5_wq", [128, 8, 1024], BF16)
    bufs["s5_wk"] = sb("s5_wk", [128, 8, 1024], BF16)
    bufs["s5_wi"] = sb("s5_wi", [128, 8, 1024], BF16)
    bufs["s5_w1"] = sb("s5_w1", [128, 8, 1024], BF16)
    bufs["s5_wo"] = sb("s5_wo", [128, 8, 1024], BF16)
    bufs["s5_vw"] = sb("s5_vw", [128, 4, 8, 512], BF16)
    bufs["s5_u"] = sb("s5_u", [128, 2, 4, 1024], BF16)
    bufs["s5_bb"] = sb("s5_bb", [8, 2, 1024], BF16)
    bufs["s5_rw2"] = sb("s5_rw2", [128, 8, 8], BF16)
    bufs["s5_rb1"] = sb("s5_rb1", [128, 8], F32)
    bufs["s5_ebc"] = sb("s5_ebc", [128, 2, 8], F32)
    bufs["s5_al"] = sb("s5_al", [128, 1], F32)
    bufs["s5_invn"] = sb("s5_invn", [128, 32], F32)
    reg5 = sb("s5_reg5", [128, 4096], BF16)
    bufs["s5_xnb"] = carve(reg5, 0, [128, 8, 512], BF16)
    bufs["s5_wexp"] = carve(reg5, 0, [128, 4, 512], BF16)
    bufs["s5_prodw"] = carve(reg5, 2048, [128, 4, 512], BF16)
    bufs["s5_qb"] = sb("s5_qb", [128, 8, 512], BF16)
    bufs["s5_kb"] = sb("s5_kb", [128, 8, 512], BF16)
    bufs["s5_iqb"] = sb("s5_iqb", [128, 8, 512], BF16)
    bufs["s5_ikb"] = bufs["s5_qb"]
    bufs["s5_xv"] = sb("s5_xv", [128, 4, 512], F32)
    bufs["s5_yw"] = sb("s5_yw", [128, 4, 512], F32)
    bufs["s5_cum"] = bufs["s5_yw"]
    bufs["s5_carry"] = sb("s5_carry", [128, 2, 4], F32)
    bufs["s5_hb"] = sb("s5_hb", [128, 8, 512], BF16)
    bufs["s5_lg"] = sb("s5_lg", [128, 4, 8], F32)
    bufs["s5_lcol"] = sb("s5_lcol", [128, 8], F32)
    bufs["s5_gwb"] = sb("s5_gwb", [128, 8], BF16)
    bufs["s5_wt"] = sb("s5_wt", [8, 4, 128], BF16)

    bufs["s5_osum"] = sb("s5_osum", [128, 8, 512], BF16)
    reg6 = sb("s5_reg6", [128, 2048], BF16)
    bufs["s4_tm"] = carve(reg6, 0, [128, 2, 1024], BF16)
    bufs["s5_x1t"] = carve(reg6, 0, [128, 1, 1024], F32)
    bufs["s5_s1t"] = sb("s5_s1t", [128, 1, 1024], BF16)
    bufs["s5_g1"] = sb("s5_g1", [128, 1024], BF16)

    bufs["zcol128"] = sb("zcol128", [128, 1], F32)


def build_s5_program():
    nc = bass.Bass("TRN2", target_bir_lowering=False)
    wd = {}
    for name in ("W_QT", "W_KT", "W_invT", "rw1T", "W_OT"):
        wd[name] = nc.dram_tensor(name, [1024, 1024], F32, kind="ExternalInput").ap()
    for name in ("VfT", "WfT", "ViT", "WiT"):
        wd[name] = nc.dram_tensor(name, [1024, 512], F32, kind="ExternalInput").ap()
    for name in ("Uf", "Ui"):
        wd[name] = nc.dram_tensor(name, [512, 1024], F32, kind="ExternalInput").ap()
    for name in ("b_fwd", "b_inv"):
        wd[name] = nc.dram_tensor(name, [8, 1024], F32, kind="ExternalInput").ap()
    wd["rw2T"] = nc.dram_tensor("rw2T", [1024, 8], F32, kind="ExternalInput").ap()
    wd["rb1"] = nc.dram_tensor("rb1", [1024, 1], F32, kind="ExternalInput").ap()
    wd["rb2"] = nc.dram_tensor("rb2", [1, 8], F32, kind="ExternalInput").ap()
    wd["expert_bias"] = nc.dram_tensor("expert_bias", [1, 8], F32, kind="ExternalInput").ap()
    wd["alpha_bi"] = nc.dram_tensor("alpha_bi", [1, 1], F32, kind="ExternalInput").ap()
    scr = {
        "xn1_scr": nc.dram_tensor("xn1_scr", [T, D], BF16, kind="ExternalInput").ap(),
        "x1_scr": nc.dram_tensor("x1_scr", [T, D], F32, kind="ExternalInput").ap(),
        "s1_scr": nc.dram_tensor("s1_scr", [T, D], BF16, kind="ExternalInput").ap(),
        "wgt_scr": nc.dram_tensor("wgt_scr", [2, 8, T], BF16, kind="Internal").ap(),
    }
    out_d = nc.dram_tensor("x2", [T, D], F32, kind="ExternalOutput").ap()
    em = Em()
    with contextlib.ExitStack() as stx:
        bufs = {}
        def sb(name, shape, dtype):
            return stx.enter_context(nc.sbuf_tensor(name, shape, dtype))
        bufs["ident"] = sb("ident", [128, 128], BF16)
        mk_s5_bufs(nc, stx, bufs)
        mk_psum_pool(nc, stx, bufs)
        sems = em.make_sems(nc, stx)
        blk = stx.enter_context(nc.Block())

        bufs["ident_tok"] = build_identity(nc, em, bufs)
        stage_s5(nc, em, bufs, wd, scr, out_d)
        em.fence("sy")
        em.emit(nc, blk, sems)
    return nc


def build_s0_program():
    nc = bass.Bass("TRN2", target_bir_lowering=False)
    u = nc.dram_tensor("u", [T, D], F32, kind="ExternalInput").ap()
    xn_fm_d = nc.dram_tensor("xn_fm", [D, T], BF16, kind="ExternalOutput").ap()
    em = Em()
    with contextlib.ExitStack() as stx:
        bufs = {}
        def sb(name, shape, dtype):
            h = stx.enter_context(nc.sbuf_tensor(name, shape, dtype))
            return h
        bufs["ident"] = sb("ident", [128, 128], BF16)
        bufs["ln_x"] = sb("ln_x", [128, 2, D], F32)
        bufs["ln_sq"] = sb("ln_sq", [128, D], F32)
        bufs["ln_col"] = sb("ln_col", [128, 16], F32)
        bufs["ln_xn"] = sb("ln_xn", [128, 2, D], BF16)
        xn_fm = sb("xn_fm_sb", [128, 8, T], BF16)
        mk_psum_pool(nc, stx, bufs)
        sems = em.make_sems(nc, stx)
        blk = stx.enter_context(nc.Block())

        bufs["ident_tok"] = build_identity(nc, em, bufs)
        t_fm = stage_ln(nc, em, bufs, u, xn_fm)
        em.fence("sy")
        out_tok = dump_fm(em, xn_fm, xn_fm_d, 8, T)
        em.wait("sy", out_tok)
        em.emit(nc, blk, sems)
    return nc


# version 07 (final)
# speedup vs baseline: 1.0154x; 1.0154x over previous
import contextlib
import numpy as np
import concourse.bass as bass
import concourse.mybir as mybir
from concourse.bass_utils import run_bass_kernel_spmd

F32 = mybir.dt.float32
BF16 = mybir.dt.bfloat16
AF = mybir.ActivationFunctionType
ALU = mybir.AluOpType
AX = mybir.AxisListType

NCORES = 8
SEQ = 2048
T = 4096
D = 1024
DIN = 2048
DSTATE = 128
NH = 16
HDIM = 128
CHUNK = 256
NCH = SEQ // CHUNK
CONVD = DIN + 2 * DSTATE
NJC = CONVD // 128
EPS = 1e-5

ENGS = ("te", "ve", "sc", "gp", "sy")
S2_PIPE = False
S2_ILV = True
DMAQS = ("gp", "sy", "sc")


NDMASEM = 8


class Em:

    def __init__(self):
        self.ops = {e: [] for e in ENGS}
        self.n = {e: 0 for e in ENGS}
        self.nd = {}
        self.rr = {q: 0 for q in DMAQS}
        self.last_wait = {}

    def op(self, eng, fn):
        if eng in ("ve", "sc", "gp") and self.n[eng] > 0:
            self.wait(eng, (eng, self.n[eng]))
        self.n[eng] += 1
        self.ops[eng].append(("op", fn))
        return (eng, self.n[eng])

    def dma(self, q, fn, slot=None):
        if q == "gp" and self.n["gp"] > 0:
            self.wait(q, ("gp", self.n["gp"]))
        if slot is None:
            slot = self.rr[q]
            self.rr[q] = (self.rr[q] + 1) % NDMASEM
        key = f"{q}{slot}_dma"
        prev = self.nd.get(key, 0)
        self.wait(q, (key, prev))
        self.nd[key] = prev + 16
        self.ops[q].append(("dma", fn, key))
        return (key, prev + 16)

    def wait(self, eng, tok):
        if tok is None:
            return
        key, cnt = tok
        if cnt <= 0:
            return
        if self.last_wait.get((eng, key), 0) >= cnt:
            return
        self.last_wait[(eng, key)] = cnt
        self.ops[eng].append(("wait", key, cnt))

    def fence(self, eng):
        for e in ENGS:
            if e != eng and self.n[e]:
                self.wait(eng, (e, self.n[e]))
        for key, cnt in self.nd.items():
            self.wait(eng, (key, cnt))

    def barrier(self):
        for e in ENGS:
            self.fence(e)

    def tok(self, eng):
        return (eng, self.n[eng])

    def emit(self, nc, blk, sems):
        def run_stream(eng_obj, eng_name):
            for item in self.ops[eng_name]:
                if item[0] == "op":
                    ins = item[1](eng_obj)
                    ins.then_inc(sems[eng_name], 1)
                elif item[0] == "dma":
                    ins = item[1](eng_obj)
                    ins.then_inc(sems[item[2]], 16)
                else:
                    eng_obj.wait_ge(sems[item[1]], item[2])
        blk.tensor(lambda e: run_stream(e, "te"))
        blk.vector(lambda e: run_stream(e, "ve"))
        blk.scalar(lambda e: run_stream(e, "sc"))
        blk.gpsimd(lambda e: run_stream(e, "gp"))
        blk.sync(lambda e: run_stream(e, "sy"))

    def make_sems(self, nc, stx):
        sems = {}
        for e in ENGS:
            sems[e] = stx.enter_context(nc.semaphore("sem_" + e))
        for q in DMAQS:
            for k in range(NDMASEM):
                sems[f"{q}{k}_dma"] = stx.enter_context(
                    nc.semaphore(f"sem_{q}{k}_dma"))
        return sems


def host_prep(inputs):
    I = inputs
    w = {}
    w["W_inT"] = np.ascontiguousarray(I["L0_attn_W_in"].T)
    w["conv_w"] = np.ascontiguousarray(I["L0_attn_conv_w"][:, 0, :].reshape(NJC, 128, 4))
    w["conv_b"] = np.ascontiguousarray(I["L0_attn_conv_b"].reshape(NJC, 128, 1))
    w["dt_bias"] = np.ascontiguousarray(I["L0_attn_dt_bias"].reshape(16, 1))
    w["A_log"] = np.ascontiguousarray(I["L0_attn_A_log"].reshape(16, 1))
    w["Dh"] = np.ascontiguousarray(I["L0_attn_D"].reshape(1, 16))
    w["norm_weight"] = np.ascontiguousarray(I["L0_attn_norm_weight"].reshape(1, DIN))
    w["W_outT"] = np.ascontiguousarray(I["L0_attn_W_out"].T)
    w["Wg"] = np.ascontiguousarray(I["L0_ffn_Wg"])
    w["bg"] = np.ascontiguousarray(I["L0_ffn_bg"].reshape(1, 8))
    w["We"] = np.ascontiguousarray(I["L0_ffn_We"])
    w["be"] = np.ascontiguousarray(I["L0_ffn_be"])
    w["WqT"] = np.ascontiguousarray(I["L1_attn_Wq"].reshape(1024, 1024).T)
    w["WkT"] = np.ascontiguousarray(I["L1_attn_Wk"].reshape(1024, 1024).T)
    w["WvT"] = np.ascontiguousarray(I["L1_attn_Wv"].reshape(1024, 1024).T)
    w["bq"] = np.ascontiguousarray(I["L1_attn_bq"].reshape(1024, 1))
    w["bk"] = np.ascontiguousarray(I["L1_attn_bk"].reshape(1024, 1))
    w["bv"] = np.ascontiguousarray(I["L1_attn_bv"].reshape(1024, 1))
    w["proj"] = np.ascontiguousarray(I["L1_attn_proj"])
    w["W_QT"] = np.ascontiguousarray(I["L1_ffn_W_Q"].T)
    w["W_KT"] = np.ascontiguousarray(I["L1_ffn_W_K"].T)
    w["W_OT"] = np.ascontiguousarray(I["L1_ffn_W_O"].T)
    w["W_invT"] = np.ascontiguousarray(I["L1_ffn_W_inv"].T)
    def flatT(a):
        return np.ascontiguousarray(a.transpose(1, 0, 2).reshape(1024, 512))
    def flat(a):
        return np.ascontiguousarray(a.transpose(0, 2, 1).reshape(512, 1024))
    w["VfT"] = flatT(I["L1_ffn_V_fwd"])
    w["WfT"] = flatT(I["L1_ffn_W_fwd"])
    w["ViT"] = flatT(I["L1_ffn_W_inv_exp"])
    w["WiT"] = flatT(I["L1_ffn_V_inv"])
    w["Uf"] = flat(I["L1_ffn_U_fwd"])
    w["Ui"] = flat(I["L1_ffn_U_inv"])
    w["b_fwd"] = np.ascontiguousarray(I["L1_ffn_b_fwd"])
    w["b_inv"] = np.ascontiguousarray(I["L1_ffn_b_inv"])
    w["rw1T"] = np.ascontiguousarray(I["L1_ffn_router_w1"].T)
    w["rb1"] = np.ascontiguousarray(I["L1_ffn_router_b1"].reshape(1024, 1))
    w["rw2T"] = np.ascontiguousarray(I["L1_ffn_router_w2"].T)
    w["rb2"] = np.ascontiguousarray(I["L1_ffn_router_b2"].reshape(1, 8))
    w["expert_bias"] = np.ascontiguousarray(I["L1_ffn_expert_bias"].reshape(1, 8))
    w["alpha_bi"] = np.ascontiguousarray(I["L1_ffn_alpha_bi"].reshape(1, 1))
    return w


class Arena:

    def __init__(self, ar_handle, n_units):
        self.ar = ar_handle
        self.n_units = n_units
        self.off = 0
        self.peak = 0

    def alloc(self, shape, dtype):
        p = shape[0]
        nfree = 1
        for d in shape[1:]:
            nfree *= d
        units = nfree * (2 if dtype == F32 else 1)
        units = (units + 63) // 64 * 64
        assert self.off + units <= self.n_units, (
            f"arena overflow: need {units} at {self.off}, cap {self.n_units}")
        base = self.ar[:, self.off:self.off + units]
        self.off += units
        self.peak = max(self.peak, self.off)
        if dtype != BF16:
            base = base.bitcast(dtype)
        strides = []
        st = 1
        for d in reversed(shape[1:]):
            strides.append([st, d])
            st *= d
        strides.reverse()
        pdim = [base.ap[0][0], p]
        return bass.AP(base.tensor, base.offset, [pdim] + strides)

    def mark(self):
        return self.off

    def release(self, m):
        self.off = m


def mk_psum_pool(nc, stx, bufs):
    ps2 = [stx.enter_context(nc.psum_tensor(f"ps2_{i}", [128, 1024], F32))
           for i in range(3)]
    bufs["ps2"] = [t[:] for t in ps2]
    bufs["ps"] = [ps2[i // 2][:, 512 * (i % 2):512 * (i % 2) + 512] for i in range(6)]
    bufs["ps_bf"] = [stx.enter_context(nc.psum_tensor(f"pbf{i}", [128, 1024], BF16))
                     for i in range(2)]


def carve(region, unit_off, shape, dtype):
    ap2 = region[:] if not isinstance(region, bass.AP) else region
    p = shape[0]
    nfree = 1
    for d in shape[1:]:
        nfree *= d
    units = nfree * (2 if dtype == F32 else 1)
    base = bass.AP(ap2.tensor, ap2.offset + unit_off, [list(ap2.ap[0]), [1, units]])
    if dtype != BF16:
        base = base.bitcast(dtype)
    strides = []
    st = 1
    for d in reversed(shape[1:]):
        strides.append([st, d])
        st *= d
    strides.reverse()
    return bass.AP(base.tensor, base.offset, [[base.ap[0][0], p]] + strides)


def flatT_note():
    pass



def build_identity(nc, em, bufs):
    ident = bufs["ident"]

    em.op("gp", lambda gp: gp.memset(ident[:], 1.0))
    return em.op("gp", lambda gp: gp.affine_select(
        out=ident[:], in_=ident[:], pattern=[[-1, 128]],
        compare_op=ALU.is_equal, fill=0.0, base=0, channel_multiplier=1))


def stage_ln(nc, em, bufs, x_dram, xn_fm, key="ln"):
    xt = bufs[key + "_x"]
    sq = bufs[key + "_sq"]
    col = bufs[key + "_col"]
    xn_tm = bufs[key + "_xn"]
    pbf = bufs["ps_bf"]
    ident = bufs["ident"]
    id_tok = bufs["ident_tok"]

    NTT = T // 128
    last_use = {}
    t_ev = None
    for i in range(NTT):
        sl = i % 2
        cb = 8 * sl

        em.wait("sy", last_use.get(("x", sl)))
        ld = em.dma("sy", lambda sy, i=i, sl=sl: sy.dma_start(
            out=xt[:, sl], in_=x_dram[128 * i:128 * (i + 1), :]))
        em.wait("sc", ld)
        em.wait("sc", last_use.get(("col", sl)))
        t_sq = em.op("sc", lambda sc, sl=sl, cb=cb: sc.activation(
            out=sq[:], in_=xt[:, sl], func=AF.Square, accum_out=col[:, cb:cb + 1]))
        em.wait("ve", ld)
        em.wait("ve", last_use.get(("col", sl)))
        em.op("ve", lambda ve, sl=sl, cb=cb: ve.reduce_sum(
            col[:, cb + 1:cb + 2], xt[:, sl], axis=AX.X))
        em.wait("ve", t_sq)
        em.op("ve", lambda ve, cb=cb: ve.tensor_scalar_mul(
            col[:, cb + 2:cb + 3], col[:, cb + 1:cb + 2], 1.0 / D))
        em.op("ve", lambda ve, cb=cb: ve.tensor_tensor(
            col[:, cb + 3:cb + 4], col[:, cb + 2:cb + 3], col[:, cb + 2:cb + 3], ALU.mult))
        em.op("ve", lambda ve, cb=cb: ve.tensor_scalar_mul(
            col[:, cb + 4:cb + 5], col[:, cb:cb + 1], 1.0 / D))
        em.op("ve", lambda ve, cb=cb: ve.tensor_sub(
            col[:, cb + 5:cb + 6], col[:, cb + 4:cb + 5], col[:, cb + 3:cb + 4]))
        em.op("ve", lambda ve, cb=cb: ve.tensor_scalar_add(
            col[:, cb + 6:cb + 7], col[:, cb + 5:cb + 6], EPS))
        t_rec = em.op("ve", lambda ve, cb=cb: ve.reciprocal(
            col[:, cb + 7:cb + 8], col[:, cb + 6:cb + 7]))
        em.wait("sc", t_rec)
        t_rstd = em.op("sc", lambda sc, cb=cb: sc.activation(
            out=col[:, cb + 6:cb + 7], in_=col[:, cb + 7:cb + 8], func=AF.Sqrt))
        em.wait("ve", t_rstd)
        em.wait("ve", last_use.get(("xn", sl)))
        t_xn = em.op("ve", lambda ve, sl=sl, cb=cb: ve.tensor_scalar(
            xn_tm[:, sl], xt[:, sl], col[:, cb + 2:cb + 3], col[:, cb + 6:cb + 7],
            ALU.subtract, ALU.mult))
        last_use[("x", sl)] = t_xn
        last_use[("col", sl)] = t_xn
        em.wait("te", t_xn)
        em.wait("te", id_tok)
        pb = pbf[i % 2]
        em.wait("te", last_use.get(("pb", i % 2)))
        def f_tr(te, sl=sl, pb=pb):
            for c in range(8):
                ins = te.transpose(pb[:, 128 * c:128 * (c + 1)],
                                   xn_tm[:, sl, 128 * c:128 * (c + 1)], ident[:])
            return ins
        t_tr = em.op("te", f_tr)
        em.wait("sc", t_tr)
        def f_ev(sc, i=i, pb=pb):
            return sc.activation(
                out=xn_fm[:, :, 128 * i:128 * (i + 1)],
                in_=pb[:].rearrange("p (c f) -> p c f", f=128), func=AF.Copy)
        t_ev = em.op("sc", f_ev)
        last_use[("pb", i % 2)] = t_ev
        last_use[("xn", sl)] = t_ev
    return t_ev


def stage_s1(nc, em, bufs, wd, scr, xn_tok):
    xn = bufs["xn_fm"]
    W = bufs["s1_w"]
    Wdt = bufs["s1_wdt"]
    zbuf = bufs["s1_z"]
    raw = bufs["s1_raw"]
    acc = bufs["s1_acc"]
    act = bufs["s1_act"]
    cw = bufs["s1_cw"]
    cb = bufs["s1_cb"]
    dtr = bufs["s1_dtr"]
    PS = bufs["ps"]

    wz_toks = []
    for kd in range(8):
        wz_toks.append(em.dma("gp", lambda gp, kd=kd: gp.dma_start(
            out=W[:, kd, 0:2048], in_=wd["W_inT"][128 * kd:128 * (kd + 1), 0:2048])))
    wdt_toks = []
    for kd in range(8):
        wdt_toks.append(em.dma("gp", lambda gp, kd=kd: gp.dma_start(
            out=Wdt[:, kd], in_=wd["W_inT"][128 * kd:128 * (kd + 1), 4352:4368])))

    em.wait("te", xn_tok)
    z_done = []
    last_ev = {}
    for ti in range(T // 128):
        sl = ti % 2
        ev_toks = []
        for jn in range(4):
            bank = PS[jn % 2]
            em.wait("te", last_ev.get(("zps", jn % 2)))
            for tok in wz_toks:
                em.wait("te", tok)
            def f_mm(te, ti=ti, jn=jn, bank=bank):
                for kd in range(8):
                    ins = te.matmul(bank[:], xn[:, kd, 128 * ti:128 * (ti + 1)],
                                    W[:, kd, 512 * jn:512 * (jn + 1)],
                                    start=(kd == 0), stop=(kd == 7))
                return ins
            t_mm = em.op("te", f_mm)
            em.wait("sc", t_mm)
            em.wait("sc", last_ev.get(("zbuf", sl)))
            t_ev = em.op("sc", lambda sc, sl=sl, jn=jn, bank=bank: sc.activation(
                out=zbuf[:, sl, 512 * jn:512 * (jn + 1)], in_=bank[:], func=AF.Copy))
            last_ev[("zps", jn % 2)] = t_ev
            ev_toks.append(t_ev)
        for tok in ev_toks:
            em.wait("sy", tok)
        st = em.dma("sy", lambda sy, ti=ti, sl=sl: sy.dma_start(
            out=scr["z_scr"][128 * ti:128 * (ti + 1), :], in_=zbuf[:, sl]))
        last_ev[("zbuf", sl)] = st
        z_done.append(st)

    t_zmm_done = em.tok("te")
    wx_toks = []
    for kd in range(8):
        em.wait("gp", t_zmm_done)
        wx_toks.append(em.dma("gp", lambda gp, kd=kd: gp.dma_start(
            out=W[:, kd, 0:2304],
            in_=wd["W_inT"][128 * kd:128 * (kd + 1), 2048:4352])))

    bank_dt = PS[4]
    dt_evs = []
    for tb in range(8):
        for tok in wdt_toks:
            em.wait("te", tok)
        def f_mmdt(te, tb=tb):
            for kd in range(8):
                ins = te.matmul(bank_dt[0:16, 0:512],
                                Wdt[:, kd], xn[:, kd, 512 * tb:512 * (tb + 1)],
                                start=(kd == 0), stop=(kd == 7))
            return ins
        em.wait("te", dt_evs[-1] if dt_evs else None)
        t_mm = em.op("te", f_mmdt)
        em.wait("ve", t_mm)
        t_ev = em.op("ve", lambda ve, tb=tb: ve.tensor_copy(
            dtr[:, 512 * tb:512 * (tb + 1)], bank_dt[0:16, 0:512]))
        dt_evs.append(t_ev)
    em.wait("sy", dt_evs[-1])
    st_dtr = em.dma("sy", lambda sy: sy.dma_start(out=scr["dtr_scr"][:, :], in_=dtr[:]))

    em.op("ve", lambda ve: ve.memset(raw[:], 0.0))
    t_pad = em.tok("ve")
    last = {}
    for jc in range(NJC):
        csl = jc % 2
        bsl = 0
        em.wait("sy", last.get(("cw", csl)))
        t_cw = em.dma("sy", lambda sy, jc=jc, csl=csl: sy.dma_start(
            out=cw[:, csl], in_=wd["conv_w"][jc]))
        t_cb = em.dma("sy", lambda sy, jc=jc, csl=csl: sy.dma_start(
            out=cb[:, csl], in_=wd["conv_b"][jc]))
        raw_evs = []
        for tb in range(8):
            bank = PS[2 + tb % 2]
            em.wait("te", last.get(("xps", tb % 2)))
            for tok in wx_toks:
                em.wait("te", tok)
            def f_mm(te, jc=jc, tb=tb, bank=bank):
                for kd in range(8):
                    ins = te.matmul(bank[:], W[:, kd, 128 * jc:128 * (jc + 1)],
                                    xn[:, kd, 512 * tb:512 * (tb + 1)],
                                    start=(kd == 0), stop=(kd == 7))
                return ins
            t_mm = em.op("te", f_mm)
            em.wait("ve", t_mm)
            em.wait("ve", last.get(("rawfull", bsl)))
            col = 3 + 512 * tb if tb < 4 else 2054 + 512 * (tb - 4)
            t_ev = em.op("ve", lambda ve, col=col, bank=bank, bsl=bsl: ve.tensor_copy(
                raw[:, bsl, col:col + 512], bank[:]))
            last[("xps", tb % 2)] = t_ev
            raw_evs.append(t_ev)
        em.wait("ve", t_pad)
        for tok in raw_evs:
            em.wait("ve", tok)
        em.wait("ve", t_cw)
        em.wait("ve", last.get(("accfull", bsl)))
        def rawview(j, bsl=bsl):
            r = raw[:, bsl]
            return bass.AP(r.tensor, r.offset + j, [r.ap[0], [2051, 2], [1, 2048]])
        def f_c0(ve, csl=csl, bsl=bsl):
            a3 = acc[:, bsl].rearrange("p (s c) -> p s c", s=2)
            return ve.tensor_scalar(a3, rawview(0), cw[:, csl, 0:1], None, ALU.mult)
        em.op("ve", f_c0)
        for j in range(1, 4):
            def f_cj(ve, j=j, csl=csl, bsl=bsl):
                a3 = acc[:, bsl].rearrange("p (s c) -> p s c", s=2)
                return ve.scalar_tensor_tensor(a3, rawview(j), cw[:, csl, j:j + 1],
                                               a3, ALU.mult, ALU.add)
            em.op("ve", f_cj)
        t_conv = em.tok("ve")
        last[("rawfull", bsl)] = t_conv
        em.wait("sc", t_conv)
        em.wait("sc", t_cb)
        em.wait("sc", last.get(("actfull", bsl)))
        t_act = em.op("sc", lambda sc, csl=csl, bsl=bsl: sc.activation(
            out=act[:, bsl], in_=acc[:, bsl], func=AF.Silu, bias=cb[:, csl]))
        last[("accfull", bsl)] = t_act
        last[("cw", csl)] = t_act
        em.wait("sy", t_act)
        if jc < 16:
            st = em.dma("sy", lambda sy, jc=jc, bsl=bsl: sy.dma_start(
                out=scr["xfm_scr"][128 * jc:128 * (jc + 1), :], in_=act[:, bsl]))
        elif jc == 16:
            st = em.dma("sy", lambda sy, bsl=bsl: sy.dma_start(
                out=scr["bfm_scr"][:, :], in_=act[:, bsl]))
        else:
            st = em.dma("sy", lambda sy, bsl=bsl: sy.dma_start(
                out=scr["cfm_scr"][:, :], in_=act[:, bsl]))
        last[("actfull", bsl)] = st
    return st_dtr


def stage_s2(nc, em, bufs, wd, scr, dbg=None):
    PS = bufs["ps"]
    pbf = bufs["ps_bf"]
    identb = bufs["ident"]
    identf = bufs["ident_f32"]
    id_tok = bufs["ident_tok"]
    idf_tok = bufs["identf_tok"]

    dtr = bufs["s2_dtr"]
    dt = bufs["s2_dt"]
    t16 = bufs["s2_t16"]
    acs = bufs["s2_acs"]
    bcol = bufs["s2_bcol"]
    dtT = bufs["s2_dtT"]
    nacsT = bufs["s2_nacsT"]
    eacsT = bufs["s2_eacsT"]
    sfacT = bufs["s2_sfacT"]
    decfm = bufs["s2_decfm"]
    cd16 = bufs["s2_cd16"]
    cdbc = bufs["s2_cdbc"]
    dbc = bufs["s2_dbc"]
    nwbc = bufs["s2_nwbc"]
    Wo = bufs["s2_wo"]
    xch = bufs["s2_xch"]
    xtm = bufs["s2_xtm"]
    Bc = bufs["s2_bc"]
    Cc = bufs["s2_cc"]
    Btm = bufs["s2_btm"]
    bsa = bufs["s2_bsa"]
    cbt = bufs["s2_cbt"]
    seg1 = bufs["s2_seg1"]
    mta = bufs["s2_mta"]
    DI = bufs["s2_di"]
    carry = bufs["s2_carry"]
    carrybf = bufs["s2_carrybf"]
    yo = bufs["s2_yo"]
    ytm = bufs["s2_ytm"]
    ztm = bufs["s2_ztm"]
    yfm = bufs["s2_yfm"]
    rcol = bufs["s2_rcol"]
    sout = bufs["s2_sout"]

    lu = {}

    def bcast_h(ap2, nh, nq):
        return bass.AP(ap2.tensor, ap2.offset, [ap2.ap[0], [0, nh], list(ap2.ap[1])])

    def bcast_q(ap2, nq):
        return bass.AP(ap2.tensor, ap2.offset, [ap2.ap[0], list(ap2.ap[1]), [0, nq]])

    t_db = em.dma("sy", lambda sy: sy.dma_start(out=bcol[:, 0:1], in_=wd["dt_bias"]))
    t_al = em.dma("sy", lambda sy: sy.dma_start(out=bcol[:, 1:2], in_=wd["A_log"]))
    dsrc = wd["Dh"]
    t_dbc = em.dma("sy", lambda sy: sy.dma_start(
        out=dbc[:], in_=bass.AP(dsrc.tensor, dsrc.offset, [[0, 128], [1, 16]])))
    nsrc = wd["norm_weight"]
    t_nw = em.dma("gp", lambda gp: gp.dma_start(
        out=nwbc[:], in_=bass.AP(nsrc.tensor, nsrc.offset, [[0, 128], [1, 2048]])))
    t_dtr = em.dma("sy", lambda sy: sy.dma_start(out=dtr[:], in_=scr["dtr_scr"][:, :]))
    em.wait("ve", t_dbc)
    em.wait("ve", id_tok)
    t_DI = em.op("ve", lambda ve: ve.tensor_tensor(
        DI[:],
        bass.AP(identb[:].tensor, identb[:].offset,
                [identb[:].ap[0], [0, 16], [1, 128]]),
        bass.AP(dbc[:].tensor, dbc[:].offset, [dbc[:].ap[0], [1, 16], [0, 128]]),
        ALU.mult))

    em.wait("sc", t_dtr)
    em.wait("sc", t_db)
    em.op("sc", lambda sc: sc.activation(out=t16[:], in_=dtr[:], func=AF.Abs,
                                         bias=bcol[:, 0:1]))
    em.op("sc", lambda sc: sc.activation(out=t16[:], in_=t16[:], func=AF.Exp, scale=-1.0))
    em.op("sc", lambda sc: sc.activation(out=t16[:], in_=t16[:], func=AF.Ln, bias=1.0))
    t_sp1 = em.op("sc", lambda sc: sc.activation(out=dt[:], in_=dtr[:], func=AF.Relu,
                                                 bias=bcol[:, 0:1]))
    em.wait("ve", t_sp1)
    t_dt = em.op("ve", lambda ve: ve.tensor_add(dt[:], dt[:], t16[:]))
    em.wait("sc", t_al)
    t_a = em.op("sc", lambda sc: sc.activation(out=bcol[:, 2:3], in_=bcol[:, 1:2], func=AF.Exp))
    em.wait("ve", t_a)
    em.wait("ve", t_dt)
    t_dA = em.op("ve", lambda ve: ve.tensor_scalar(
        t16[:], dt[:], bcol[:, 2:3], -1.0, ALU.mult, ALU.mult))
    zc = bufs["zcol16"]
    t_zc = em.op("ve", lambda ve: ve.memset(zc[:], 0.0))
    scan_toks = []
    for s in range(2):
        for c in range(NCH):
            t0 = SEQ * s + CHUNK * c
            def f_scan(ve, t0=t0):
                zb = bass.AP(zc[:].tensor, zc[:].offset, [zc[:].ap[0], [0, CHUNK]])
                return ve.tensor_tensor_scan(
                    out=acs[:, t0:t0 + CHUNK], data0=t16[:, t0:t0 + CHUNK],
                    data1=zb, initial=0.0, op0=ALU.add, op1=ALU.add)
            scan_toks.append(em.op("ve", f_scan))
    t_acs = scan_toks[-1]
    em.wait("sy", t_acs)
    t_acsd = em.dma("sy", lambda sy: sy.dma_start(out=scr["acs_scr"][:, :], in_=acs[:]))
    t_cdbc = []
    for s in range(2):
        a0 = acs[:]
        ends = bass.AP(a0.tensor, a0.offset + SEQ * s + CHUNK - 1, [a0.ap[0], [CHUNK, NCH]])
        em.wait("sc", t_acs)
        em.wait("sc", lu.get("cd16"))
        t_e = em.op("sc", lambda sc, ends=ends: sc.activation(
            out=cd16[:], in_=ends, func=AF.Exp))
        em.wait("sy", t_e)
        t_w = em.dma("sy", lambda sy, s=s: sy.dma_start(
            out=scr["cdec_scr"][s], in_=cd16[:]))
        em.wait("sy", t_w)
        csrc = scr["cdec_scr"]
        t_b = em.dma("sy", lambda sy, s=s, csrc=csrc: sy.dma_start(
            out=cdbc[:, s], in_=bass.AP(csrc.tensor, csrc.offset + 128 * s,
                                        [[0, 128], [1, 128]])))
        lu["cd16"] = t_w
        t_cdbc.append(t_b)

    em.wait("te", idf_tok)
    for ti in range(32):
        em.wait("te", t_dt)
        em.wait("te", t_acs)
        em.wait("te", lu.get(("tps", ti % 2)))
        em.wait("te", lu.get(("tpsv", ti % 2)))
        bank = PS[ti % 2]
        def f_tr(te, ti=ti, bank=bank):
            te.transpose(bank[0:128, 0:16], dt[:, 128 * ti:128 * (ti + 1)],
                         identf[0:16, 0:16])
            return te.transpose(bank[0:128, 16:32], acs[:, 128 * ti:128 * (ti + 1)],
                                identf[0:16, 0:16])
        t_tr = em.op("te", f_tr)
        em.wait("ve", t_tr)
        t_c1 = em.op("ve", lambda ve, ti=ti, bank=bank: ve.tensor_copy(
            dtT[:, ti], bank[0:128, 0:16]))
        t_c2 = em.op("ve", lambda ve, ti=ti, bank=bank: ve.tensor_scalar_mul(
            nacsT[:, ti], bank[0:128, 16:32], -1.0))
        em.wait("sc", t_tr)
        t_c3 = em.op("sc", lambda sc, ti=ti, bank=bank: sc.activation(
            out=eacsT[:, ti], in_=bank[0:128, 16:32], func=AF.Exp))
        lu[("tps", ti % 2)] = t_c3
        lu[("tpsv", ti % 2)] = t_c2
    t_cols = (t_c1, t_c2, t_c3)
    for s in range(2):
        for c in range(NCH):
            t0 = SEQ * s + CHUNK * c
            em.wait("ve", t_acs)
            em.wait("ve", lu.get("decfm"))
            t_d1 = em.op("ve", lambda ve, t0=t0: ve.tensor_scalar(
                decfm[:], acs[:, t0:t0 + CHUNK], acs[:, t0 + CHUNK - 1:t0 + CHUNK],
                None, ALU.subtract))
            em.wait("sc", t_d1)
            t_d2 = em.op("sc", lambda sc: sc.activation(
                out=decfm[:], in_=decfm[:], func=AF.Exp, scale=-1.0))
            ti0 = 2 * (NCH * s + c)
            em.wait("te", t_d2)
            em.wait("te", lu.get("decps"))
            def f_tr2(te):
                te.transpose(PS[2][0:128, 0:16], decfm[:, 0:128], identf[0:16, 0:16])
                return te.transpose(PS[2][0:128, 16:32], decfm[:, 128:256],
                                    identf[0:16, 0:16])
            t_tr2 = em.op("te", f_tr2)
            for st in range(2):
                em.wait("ve", t_tr2)
                t_sf = em.op("ve", lambda ve, ti0=ti0, st=st: ve.tensor_mul(
                    sfacT[:, ti0 + st], dtT[:, ti0 + st],
                    PS[2][0:128, 16 * st:16 * (st + 1)]))
            lu["decps"] = t_sf
            lu["decfm"] = t_tr2

    em.wait("gp", em.tok("te"))
    em.wait("gp", em.tok("ve"))
    wo_toks = [em.dma("gp", lambda gp, kc=kc: gp.dma_start(
        out=Wo[:, kc], in_=wd["W_outT"][128 * kc:128 * (kc + 1), :]))
        for kc in range(16)]


    def P2(cc):
        s, c = (cc % 2, cc // 2) if S2_ILV else (cc // NCH, cc % NCH)
        t0 = SEQ * s + CHUNK * c
        ti0 = 2 * (NCH * s + c)
        g = cc % 2
        em.wait("gp", lu.get(("xchg", g)))
        xsrc = scr["xfm_scr"].rearrange("(c p) t -> p c t", p=128)
        t_xch = em.dma("gp", lambda gp, t0=t0, g=g, xsrc=xsrc: gp.dma_start(
            out=xch[:, g], in_=xsrc[:, :, t0:t0 + CHUNK]))
        em.wait("gp", lu.get(("bcg", g)))
        t_bc = em.dma("gp", lambda gp, t0=t0, g=g: gp.dma_start(
            out=Bc[:, g], in_=scr["bfm_scr"][:, t0:t0 + CHUNK]))
        t_cc_ = em.dma("gp", lambda gp, t0=t0, g=g: gp.dma_start(
            out=Cc[:, g], in_=scr["cfm_scr"][:, t0:t0 + CHUNK]))
        lu[("ccld", g)] = t_cc_
        for gg in range(4):
            pb = pbf[gg % 2]
            em.wait("te", t_xch)
            em.wait("te", id_tok)
            em.wait("te", lu.get(("pbf", gg % 2)))
            em.wait("te", lu.get(("xtmg", g)))
            def f_xtr(te, gg=gg, pb=pb, g=g):
                for k in range(8):
                    idx = 8 * gg + k
                    st, c16 = idx // 16, idx % 16
                    ins = te.transpose(pb[:, 128 * k:128 * (k + 1)],
                                       xch[:, g, c16, 128 * st:128 * (st + 1)],
                                       identb[:])
                return ins
            t_xtr = em.op("te", f_xtr)
            em.wait("sc", t_xtr)
            em.wait("sc", lu.get(("xtmg", g)))
            def f_xev(sc, gg=gg, pb=pb, g=g):
                st, c0 = (8 * gg) // 16, (8 * gg) % 16
                return sc.activation(
                    out=xtm[:, g, st, 128 * c0:128 * c0 + 1024],
                    in_=pb[:], func=AF.Copy)
            t_xev = em.op("sc", f_xev)
            lu[("pbf", gg % 2)] = t_xev
        lu[("xchg", g)] = t_xtr
        lu[("xtm", cc)] = t_xev
        em.wait("te", t_bc)
        em.wait("te", lu.get(("btmg", g)))
        em.wait("te", lu.get(("pbf", 0)))
        def f_btr(te, g=g):
            te.transpose(pbf[0][:, 0:128], Bc[:, g, 0:128], identb[:])
            return te.transpose(pbf[0][:, 128:256], Bc[:, g, 128:256], identb[:])
        t_btr = em.op("te", f_btr)
        em.wait("ve", t_btr)
        t_bev = em.op("ve", lambda ve, g=g: ve.tensor_copy(
            Btm[:, g].rearrange("p a b -> p (a b)"), pbf[0][:, 0:256]))
        lu[("pbf", 0)] = t_bev
        lu[("bcg", g)] = t_btr
        lu[("btm", cc)] = t_bev
        em.wait("te", t_cc_)
        em.wait("te", lu.get(("bank", 5)))
        def f_cbt(te, g=g):
            for st in range(2):
                ins = te.matmul(PS[5][:, 256 * st:256 * (st + 1)],
                                Bc[:, g, 128 * st:128 * (st + 1)], Cc[:, g],
                                start=True, stop=True)
            return ins
        t_cbtm = em.op("te", f_cbt)
        em.wait("ve", t_cbtm)
        em.wait("ve", lu.get(("cbtg", g)))
        t_cbt = em.op("ve", lambda ve, g=g: ve.tensor_copy(
            cbt[:, g].rearrange("p a b -> p (a b)"), PS[5][:]))
        lu[("bank", 5)] = t_cbt
        mta_toks = []
        asrc = scr["acs_scr"]
        for st in range(2):
            t_exs = []
            for hh in range(2):
                sl = (2 * st + hh) % 2
                em.wait("gp", lu.get(("seg1", sl)))
                em.wait("gp", t_acsd)
                t_ab = em.dma("gp", lambda gp, t0=t0, hh=hh, sl=sl, asrc=asrc:
                              gp.dma_start(
                    out=seg1[:, sl],
                    in_=bass.AP(asrc.tensor, asrc.offset + 8 * hh * T + t0,
                                [[0, 128], [T, 8], [1, 256]])))
                em.wait("gp", t_ab)
                em.wait("gp", t_cols[1])
                a1 = nacsT[:, ti0 + st, 8 * hh:8 * (hh + 1)]
                t_s1 = em.op("gp", lambda gp, sl=sl, a1=a1: gp.tensor_tensor(
                    seg1[:, sl], seg1[:, sl], bcast_q(a1, 256), ALU.add))
                em.wait("sc", t_s1)
                t_rl = em.op("sc", lambda sc, sl=sl: sc.activation(
                    out=seg1[:, sl], in_=seg1[:, sl], func=AF.Relu, scale=-1.0))
                em.wait("sc", lu.get(("mtag", g, st)))
                t_ex = em.op("sc", lambda sc, st=st, hh=hh, sl=sl, g=g: sc.activation(
                    out=mta[:, g, st, 8 * hh:8 * (hh + 1), :], in_=seg1[:, sl],
                    func=AF.Exp, scale=-1.0))
                lu[("seg1", sl)] = t_ex
                t_exs.append(t_ex)
            for t_ex in t_exs:
                em.wait("ve", t_ex)
            d1 = dtT[:, ti0 + st, :]
            em.op("ve", lambda ve, st=st, d1=d1, g=g: ve.tensor_tensor(
                mta[:, g, st], mta[:, g, st], bcast_q(d1, 256), ALU.mult))
            em.wait("ve", t_cbt)
            c1 = cbt[:, g, st]
            t_m2 = em.op("ve", lambda ve, st=st, c1=c1, g=g: ve.tensor_tensor(
                mta[:, g, st], mta[:, g, st], bcast_h(c1, 16, 256), ALU.mult))
            em.wait("gp", t_m2)
            t_msk = em.op("gp", lambda gp, st=st, g=g: gp.affine_select(
                out=mta[:, g, st], in_=mta[:, g, st], pattern=[[0, 16], [1, 256]],
                compare_op=ALU.is_ge, fill=0.0, base=-128 * st,
                channel_multiplier=-1))
            em.wait("ve", t_msk)
            em.wait("ve", t_DI)
            t_di = em.op("ve", lambda ve, st=st, g=g: ve.tensor_add(
                mta[:, g, st, :, 128 * st:128 * (st + 1)],
                mta[:, g, st, :, 128 * st:128 * (st + 1)], DI[:]))
            mta_toks.append(t_di)
        lu[("cbtg", g)] = t_m2
        lu[("mta_ready", cc)] = mta_toks

    def P3(cc):
        s, c = (cc % 2, cc // 2) if S2_ILV else (cc // NCH, cc % NCH)
        ti0 = 2 * (NCH * s + c)
        g = cc % 2
        if cc < 2:
            em.wait("ve", t_acsd)
            em.wait("ve", t_cols[2])
            t_cz = em.op("ve", lambda ve, s=s: ve.memset(carry[:, s], 0.0))
            lu["carry"] = t_cz
        em.wait("ve", lu.get("bsa_used"))
        em.wait("ve", lu.get(("btm", cc)))
        bsa_toks = []
        for st in range(2):
            b1 = Btm[:, g, st]
            s1_ = sfacT[:, ti0 + st, :]
            t_bs = em.op("ve", lambda ve, st=st, b1=b1, s1_=s1_: ve.tensor_tensor(
                bsa[:, st], bcast_h(b1, 16, 128), bcast_q(s1_, 128), ALU.mult))
            bsa_toks.append(t_bs)
        for h in range(NH):
            par = h % 2
            psA = PS[0 + par]
            psB = PS[2 + par]
            psC = PS[4]
            em.wait("te", lu.get(("bank", par)))
            for tok in lu[("mta_ready", cc)]:
                em.wait("te", tok)
            em.wait("te", lu.get(("xtm", cc)))
            def f_yd(te, h=h, psA=psA, g=g):
                for qt in range(2):
                    for st in range(2):
                        ins = te.matmul(psA[:, 128 * qt:128 * (qt + 1)],
                                        mta[:, g, st, h, 128 * qt:128 * (qt + 1)],
                                        xtm[:, g, st, 128 * h:128 * (h + 1)],
                                        start=(st == 0), stop=(st == 1))
                return ins
            t_yd = em.op("te", f_yd)
            if c > 0:
                em.wait("te", lu.get(("bank", 2 + par)))
                em.wait("te", lu.get(("ccld", g)))
                em.wait("te", lu.get(("carrybf", s)))
                def f_yo(te, h=h, psB=psB, g=g, s=s):
                    for qt in range(2):
                        ins = te.matmul(psB[:, 128 * qt:128 * (qt + 1)],
                                        Cc[:, g, 128 * qt:128 * (qt + 1)],
                                        carrybf[:, s, h], start=True, stop=True)
                    return ins
                t_yo = em.op("te", f_yo)
            em.wait("te", lu.get(("bank", 4)))
            for tok in bsa_toks:
                em.wait("te", tok)
            def f_st(te, h=h, psC=psC, g=g):
                for st in range(2):
                    ins = te.matmul(psC[:, 0:128], bsa[:, st, h],
                                    xtm[:, g, st, 128 * h:128 * (h + 1)],
                                    start=(st == 0), stop=(st == 1))
                return ins
            t_stm = em.op("te", f_st)
            for qt in range(2):
                if c > 0:
                    em.wait("sc", t_yo)
                    em.wait("sc", lu.get(("yo", par)))
                    t_yoev = em.op("sc", lambda sc, qt=qt, ti0=ti0, h=h, psB=psB,
                                   par=par:
                                   sc.activation(out=yo[:, par, qt],
                                                 in_=psB[:, 128 * qt:128 * (qt + 1)],
                                                 func=AF.Copy,
                                                 scale=eacsT[:, ti0 + qt, h:h + 1]))
                    em.wait("ve", t_yoev)
                    em.wait("ve", t_yd)
                    em.wait("ve", lu.get("ytm_used"))
                    t_y2 = em.op("ve", lambda ve, qt=qt, h=h, psA=psA, par=par:
                                 ve.tensor_add(
                                     ytm[:, qt, 128 * h:128 * (h + 1)],
                                     yo[:, par, qt],
                                     psA[:, 128 * qt:128 * (qt + 1)]))
                else:
                    em.wait("sc", t_yd)
                    em.wait("sc", lu.get("ytm_used"))
                    em.wait("sc", lu.get(("yo", par)))
                    t_y2 = em.op("sc", lambda sc, qt=qt, h=h, psA=psA: sc.activation(
                        out=ytm[:, qt, 128 * h:128 * (h + 1)],
                        in_=psA[:, 128 * qt:128 * (qt + 1)], func=AF.Copy))
            lu[("yo", par)] = t_y2
            if c > 0:
                lu[("bank", 2 + par)] = t_y2
            lu[("bank", par)] = t_y2
            em.wait("ve", t_stm)
            em.wait("ve", lu.get("carry"))
            em.wait("ve", t_cdbc[s])
            t_cu = em.op("ve", lambda ve, h=h, c=c, s=s, psC=psC: ve.scalar_tensor_tensor(
                carry[:, s, h], carry[:, s, h], cdbc[:, s, 8 * h + c:8 * h + c + 1],
                psC[:, 0:128], ALU.mult, ALU.add))
            lu[("bank", 4)] = t_cu
            em.wait("gp", t_cu)
            em.wait("gp", lu.get(("cbf", s, h)))
            t_cb2 = em.op("gp", lambda gp, h=h, s=s: gp.tensor_copy(
                carrybf[:, s, h], carry[:, s, h]))
            lu[("cbf", s, h)] = t_cb2
        lu[("carrybf", s)] = t_cb2
        lu["carry_used"] = t_cu
        lu["bsa_used"] = t_stm
        lu[("xtmg", g)] = t_cu
        lu[("mtag", g, 0)] = t_yd
        lu[("mtag", g, 1)] = t_yd
        lu[("btmg", g)] = t_bs
        lu["lasty2"] = t_y2

    def P4(cc):
        s, c = (cc % 2, cc // 2) if S2_ILV else (cc // NCH, cc % NCH)
        t0 = SEQ * s + CHUNK * c
        g = cc % 2
        em.wait("sy", lu.get("ztm_used"))
        t_ztm = em.dma("sy", lambda sy, t0=t0: sy.dma_start(
            out=ztm[:], in_=scr["z_scr"][t0:t0 + CHUNK, :].rearrange(
                "(h p) d -> p h d", p=128)))
        em.wait("sc", t_ztm)
        em.wait("sc", lu.get("ztm_used2"))
        t_sz = em.op("sc", lambda sc: sc.activation(out=ztm[:], in_=ztm[:], func=AF.Silu))
        for qt in range(2):
            em.wait("ve", t_sz)
            em.wait("ve", lu["lasty2"])
            t_g = em.op("ve", lambda ve, qt=qt: ve.tensor_mul(
                ytm[:, qt], ytm[:, qt], ztm[:, qt]))
            t_sq = em.op("ve", lambda ve, qt=qt: ve.scalar_tensor_tensor(
                ztm[:, qt], ytm[:, qt], 1.0, ytm[:, qt], ALU.mult, ALU.mult,
                accum_out=rcol[:, 4 * qt:4 * qt + 1]))
            em.op("ve", lambda ve, qt=qt: ve.tensor_scalar(
                rcol[:, 4 * qt + 1:4 * qt + 2], rcol[:, 4 * qt:4 * qt + 1],
                1.0 / DIN, EPS, ALU.mult, ALU.add))
            t_rc = em.op("ve", lambda ve, qt=qt: ve.reciprocal(
                rcol[:, 4 * qt + 2:4 * qt + 3], rcol[:, 4 * qt + 1:4 * qt + 2]))
            em.wait("sc", t_rc)
            t_rs = em.op("sc", lambda sc, qt=qt: sc.activation(
                out=rcol[:, 4 * qt + 3:4 * qt + 4],
                in_=rcol[:, 4 * qt + 2:4 * qt + 3], func=AF.Sqrt))
            em.wait("ve", t_rs)
            em.wait("ve", t_nw)
            t_y3 = em.op("ve", lambda ve, qt=qt: ve.scalar_tensor_tensor(
                ytm[:, qt], ytm[:, qt], rcol[:, 4 * qt + 3:4 * qt + 4],
                nwbc[:], ALU.mult, ALU.mult))
            lu[("y3", qt)] = t_y3
        lu["ztm_used"] = t_sq
        lu["ztm_used2"] = t_y3
        for gg in range(4):
            pb = pbf[gg % 2]
            em.wait("te", lu[("y3", 0)])
            em.wait("te", lu[("y3", 1)])
            em.wait("te", lu.get(("pbf", gg % 2)))
            em.wait("te", lu.get("yfm_used"))
            def f_ytr(te, gg=gg, pb=pb):
                for k in range(8):
                    idx = 8 * gg + k
                    st, c16 = idx // 16, idx % 16
                    ins = te.transpose(pb[:, 128 * k:128 * (k + 1)],
                                       ytm[:, st, 128 * c16:128 * (c16 + 1)],
                                       identb[:])
                return ins
            t_ytr = em.op("te", f_ytr)
            em.wait("sc", t_ytr)
            em.wait("sc", lu.get("yfm_used"))
            def f_yev(sc, gg=gg, pb=pb):
                st, c0 = (8 * gg) // 16, (8 * gg) % 16
                return sc.activation(
                    out=yfm[:, c0:c0 + 8, 128 * st:128 * (st + 1)],
                    in_=pb[:].rearrange("p (c f) -> p c f", f=128),
                    func=AF.Copy)
            t_yev = em.op("sc", f_yev)
            lu[("pbf", gg % 2)] = t_yev
        lu["ytm_used"] = t_yev
        sev_toks = []
        for qt in range(2):
            for dn in range(2):
                bi = 2 + (qt * 2 + dn) % 2
                bank = PS[bi]
                em.wait("te", t_yev)
                for tok in wo_toks:
                    em.wait("te", tok)
                em.wait("te", lu.get(("bank", bi)))
                def f_op(te, qt=qt, dn=dn, bank=bank):
                    for kc in range(16):
                        ins = te.matmul(bank[:],
                                        yfm[:, kc, 128 * qt:128 * (qt + 1)],
                                        Wo[:, kc, 512 * dn:512 * (dn + 1)],
                                        start=(kc == 0), stop=(kc == 15))
                    return ins
                t_op = em.op("te", f_op)
                em.wait("sc", t_op)
                em.wait("sc", lu.get("sout_used"))
                t_sev = em.op("sc", lambda sc, qt=qt, dn=dn, bank=bank: sc.activation(
                    out=sout[:, qt, 512 * dn:512 * (dn + 1)], in_=bank[:], func=AF.Copy))
                lu[("bank", bi)] = t_sev
                sev_toks.append(t_sev)
        for tok in sev_toks:
            em.wait("sy", tok)
        t_sdma = em.dma("sy", lambda sy, t0=t0: sy.dma_start(
            out=scr["s_scr"][t0:t0 + CHUNK, :].rearrange(
                "(h p) d -> p h d", p=128),
            in_=sout[:]))
        lu["sout_used"] = t_sdma
        lu["yfm_used"] = t_op
        lu["s_done"] = t_sdma

    NCC = 2 * NCH
    PIPE = S2_PIPE
    if PIPE:
        P2(0)
        for cc in range(1, NCC):
            P2(cc)
            P3(cc - 1)
            P4(cc - 1)
        P3(NCC - 1)
        P4(NCC - 1)
    else:
        for cc in range(NCC):
            P2(cc)
            P3(cc)
            P4(cc)
    return lu["s_done"]


def stage_s3(nc, em, bufs, wd, scr, xn_tok):
    xn = bufs["xn_fm"]
    PS = bufs["ps"]
    pbf = bufs["ps_bf"]
    identb = bufs["ident"]
    id_tok = bufs["ident_tok"]
    wg = bufs["s3_wg"]
    bgbc = bufs["s3_bgbc"]
    besb = bufs["s3_be"]
    gcl = bufs["s3_gcl"]
    gsc = bufs["s3_gsc"]
    gax1 = bufs["s3_gax1"]
    gax2 = bufs["s3_gax2"]
    gw_all = bufs["s3_gw"]
    gwb_all = bufs["s3_gwb"]
    wT_all = bufs["s3_wt"]
    We_sb = bufs["s3_we"]
    acc = bufs["s3_acc"]
    ut = bufs["s3_u"]
    st_ = bufs["s3_s"]
    gbf = bufs["s3_g"]
    sg = bufs["s3_sg"]
    x1t = bufs["s3_x1"]
    xn1t = bufs["s3_xn1"]
    sqs = bufs["s1_acc"]
    lu = {}

    t_wg = []
    for kd in range(8):
        t_wg.append(em.dma("gp", lambda gp, kd=kd: gp.dma_start(
            out=wg[:, kd], in_=wd["Wg"][128 * kd:128 * (kd + 1), :])))
    bsrc = wd["bg"]
    t_bg = em.dma("sy", lambda sy: sy.dma_start(
        out=bgbc[:], in_=bass.AP(bsrc.tensor, bsrc.offset, [[0, 128], [1, 8]])))
    t_be = em.dma("gp", lambda gp: gp.dma_start(out=besb[:], in_=wd["be"][:, :]))

    ge_toks = []
    for ti in range(32):
        bank = PS[4 + ti % 2]
        for tok in t_wg:
            em.wait("te", tok)
        em.wait("te", xn_tok)
        em.wait("te", lu.get(("gps", ti % 2)))
        def f_g(te, ti=ti, bank=bank):
            for kd in range(8):
                ins = te.matmul(bank[0:128, 0:8], xn[:, kd, 128 * ti:128 * (ti + 1)],
                                wg[:, kd], start=(kd == 0), stop=(kd == 7))
            return ins
        t_gm = em.op("te", f_g)
        em.wait("ve", t_gm)
        em.wait("ve", t_bg)
        t_ge = em.op("ve", lambda ve, ti=ti, bank=bank: ve.tensor_add(
            gsc[:, ti], bank[0:128, 0:8], bgbc[:]))
        lu[("gps", ti % 2)] = t_ge
        ge_toks.append(t_ge)

    def bmid(ap3, n_last):
        return bass.AP(ap3.tensor, ap3.offset,
                       [ap3.ap[0], list(ap3.ap[1]), [0, n_last]])

    em.wait("ve", ge_toks[-1])
    em.op("ve", lambda ve: ve.reduce_max(gcl[:, 0], gsc[:], axis=AX.X))
    em.op("ve", lambda ve: ve.tensor_tensor(
        gax1[:], gsc[:], bmid(gcl[:, 0], 8), ALU.is_ge))
    em.op("ve", lambda ve: ve.scalar_tensor_tensor(
        gax2[:], gax1[:], -1e30, gsc[:], ALU.mult, ALU.add))
    em.op("ve", lambda ve: ve.reduce_max(gcl[:, 1], gax2[:], axis=AX.X))
    em.op("ve", lambda ve: ve.tensor_tensor(
        gax1[:], gsc[:], bmid(gcl[:, 1], 8), ALU.is_ge))
    t_d = em.op("ve", lambda ve: ve.tensor_tensor(
        gax2[:], gsc[:], bmid(gcl[:, 0], 8), ALU.subtract))
    em.wait("sc", t_d)
    t_e1 = em.op("sc", lambda sc: sc.activation(out=gax2[:], in_=gax2[:], func=AF.Exp))
    em.wait("ve", t_e1)
    em.op("ve", lambda ve: ve.tensor_mul(gax2[:], gax2[:], gax1[:]))
    em.op("ve", lambda ve: ve.reduce_sum(gcl[:, 2], gax2[:], axis=AX.X))
    em.op("ve", lambda ve: ve.reciprocal(gcl[:, 3], gcl[:, 2]))
    em.op("ve", lambda ve: ve.tensor_tensor(
        gw_all[:], gax2[:], bmid(gcl[:, 3], 8), ALU.mult))
    t_gwb = em.op("ve", lambda ve: ve.tensor_copy(gwb_all[:], gw_all[:]))
    lu["gw_done"] = t_gwb
    for tg in range(4):
        em.wait("te", t_gwb)
        em.wait("te", id_tok)
        em.wait("te", lu.get("wtps"))
        def f_wt(te, tg=tg):
            for j in range(8):
                ins = te.transpose(pbf[0][0:8, 128 * j:128 * (j + 1)],
                                   gwb_all[:, 8 * tg + j, :], identb[:])
            return ins
        t_wt = em.op("te", f_wt)
        em.wait("ve", t_wt)
        t_wte = em.op("ve", lambda ve, tg=tg: ve.tensor_copy(
            wT_all[:, 8 * tg:8 * (tg + 1), :],
            pbf[0][0:8, 0:1024].rearrange("p (j f) -> p j f", f=128)))
        lu["wtps"] = t_wte
        for j in range(8):
            lu[("wtok", 8 * tg + j)] = t_wte
    t_gate_done = t_wte

    x1_toks = []
    for half in range(2):
        tis = range(16 * half, 16 * (half + 1))
        for ti in tis:
            tloc = ti - 16 * half
            for dn in range(2):
                bank = PS[4 + (2 * tloc + dn) % 2]
                em.wait("te", lu[("wtok", ti)])
                em.wait("te", t_be)
                em.wait("te", lu.get(("bps", (2 * tloc + dn) % 2)))
                t_bm = em.op("te", lambda te, ti=ti, dn=dn, bank=bank: te.matmul(
                    bank[:], wT_all[:, ti], besb[:, 512 * dn:512 * (dn + 1)],
                    start=True, stop=True))
                em.wait("sc", t_bm)
                em.wait("sc", lu.get(("accw", tloc)))
                t_ai = em.op("sc", lambda sc, tloc=tloc, dn=dn, bank=bank: sc.activation(
                    out=acc[:, tloc, 512 * dn:512 * (dn + 1)], in_=bank[:], func=AF.Copy))
                lu[("bps", (2 * tloc + dn) % 2)] = t_ai
            lu[("acci", tloc)] = t_ai
        for e in range(8):
            sl = e % 2
            we_toks = []
            em.wait("gp", lu.get(("wesl", sl)))
            for kd in range(8):
                we_toks.append(em.dma("gp", lambda gp, e=e, sl=sl, kd=kd: gp.dma_start(
                    out=We_sb[:, sl, kd], in_=wd["We"][e, 128 * kd:128 * (kd + 1), :])))
            for ti in tis:
                tloc = ti - 16 * half
                for dn in range(2):
                    bank = PS[(2 * tloc + dn) % 4]
                    for tok in we_toks:
                        em.wait("te", tok)
                    em.wait("te", lu.get(("eps", (2 * tloc + dn) % 4)))
                    def f_e(te, ti=ti, sl=sl, dn=dn, bank=bank):
                        for kd in range(8):
                            ins = te.matmul(bank[:], xn[:, kd, 128 * ti:128 * (ti + 1)],
                                            We_sb[:, sl, kd, 512 * dn:512 * (dn + 1)],
                                            start=(kd == 0), stop=(kd == 7))
                        return ins
                    t_em = em.op("te", f_e)
                    em.wait("ve", t_em)
                    em.wait("ve", lu.get(("acci", tloc)))
                    t_up = em.op("ve", lambda ve, tloc=tloc, dn=dn, e=e, bank=bank,
                                 ti=ti: ve.scalar_tensor_tensor(
                                     acc[:, tloc, 512 * dn:512 * (dn + 1)],
                                     bank[:], gw_all[:, ti, e:e + 1],
                                     acc[:, tloc, 512 * dn:512 * (dn + 1)],
                                     ALU.mult, ALU.add))
                    lu[("eps", (2 * tloc + dn) % 4)] = t_up
                    lu[("accu", tloc)] = t_up
            lu[("wesl", sl)] = em.tok("te")
        for ti in tis:
            tloc = ti - 16 * half
            sl2 = ti % 2
            em.wait("sy", lu.get(("ut", 0)))
            t_u = em.dma("sy", lambda sy, ti=ti: sy.dma_start(
                out=ut[:, 0], in_=wd["u_d"][128 * ti:128 * (ti + 1), :]))
            t_s = em.dma("sy", lambda sy, ti=ti, sl2=sl2: sy.dma_start(
                out=st_[:, sl2], in_=scr["s_scr"][128 * ti:128 * (ti + 1), :]))
            em.wait("sc", lu.get(("accu", tloc)))
            em.wait("sc", lu.get("gbf"))
            t_g = em.op("sc", lambda sc, tloc=tloc: sc.activation(
                out=gbf[:], in_=acc[:, tloc], func=AF.Silu))
            lu[("accw", tloc)] = t_g
            em.wait("ve", t_g)
            em.wait("ve", t_s)
            em.wait("ve", lu.get("sg"))
            t_sg = em.op("ve", lambda ve, sl2=sl2: ve.tensor_mul(sg[:], st_[:, sl2], gbf[:]))
            lu["gbf"] = t_sg
            em.wait("ve", t_u)
            em.wait("ve", lu.get(("x1", sl2)))
            em.wait("ve", lu.get(("x1d", sl2)))
            t_x1 = em.op("ve", lambda ve, sl2=sl2: ve.tensor_add(
                x1t[:, sl2], sg[:], ut[:, 0]))
            lu["sg"] = t_x1
            lu[("ut", 0)] = t_x1
            em.wait("sy", t_x1)
            t_x1d = em.dma("sy", lambda sy, ti=ti, sl2=sl2: sy.dma_start(
                out=scr["x1_scr"][128 * ti:128 * (ti + 1), :], in_=x1t[:, sl2]))
            lu[("x1d", sl2)] = t_x1d
            x1_toks.append(t_x1d)
            cb = 8 * sl2
            col = bufs["ln_col"]
            em.wait("sc", t_x1)
            em.wait("sc", lu.get(("lncol", sl2)))
            t_sq = em.op("sc", lambda sc, sl2=sl2, cb=cb: sc.activation(
                out=sqs[:, 0:1024], in_=x1t[:, sl2], func=AF.Square,
                accum_out=col[:, cb:cb + 1]))
            em.wait("ve", t_sq)
            em.op("ve", lambda ve, sl2=sl2, cb=cb: ve.reduce_sum(
                col[:, cb + 1:cb + 2], x1t[:, sl2], axis=AX.X))
            em.op("ve", lambda ve, cb=cb: ve.tensor_scalar_mul(
                col[:, cb + 2:cb + 3], col[:, cb + 1:cb + 2], 1.0 / D))
            em.op("ve", lambda ve, cb=cb: ve.tensor_tensor(
                col[:, cb + 3:cb + 4], col[:, cb + 2:cb + 3], col[:, cb + 2:cb + 3],
                ALU.mult))
            em.op("ve", lambda ve, cb=cb: ve.tensor_scalar_mul(
                col[:, cb + 4:cb + 5], col[:, cb:cb + 1], 1.0 / D))
            em.op("ve", lambda ve, cb=cb: ve.tensor_sub(
                col[:, cb + 5:cb + 6], col[:, cb + 4:cb + 5], col[:, cb + 3:cb + 4]))
            em.op("ve", lambda ve, cb=cb: ve.tensor_scalar_add(
                col[:, cb + 6:cb + 7], col[:, cb + 5:cb + 6], EPS))
            t_rec = em.op("ve", lambda ve, cb=cb: ve.reciprocal(
                col[:, cb + 7:cb + 8], col[:, cb + 6:cb + 7]))
            em.wait("sc", t_rec)
            t_rstd = em.op("sc", lambda sc, cb=cb: sc.activation(
                out=col[:, cb + 6:cb + 7], in_=col[:, cb + 7:cb + 8], func=AF.Sqrt))
            em.wait("ve", t_rstd)
            em.wait("ve", lu.get(("xn1", sl2)))
            t_xn1 = em.op("ve", lambda ve, sl2=sl2, cb=cb: ve.tensor_scalar(
                xn1t[:, sl2], x1t[:, sl2], col[:, cb + 2:cb + 3], col[:, cb + 6:cb + 7],
                ALU.subtract, ALU.mult))
            lu[("x1", sl2)] = t_xn1
            lu[("lncol", sl2)] = t_xn1
            em.wait("sy", t_xn1)
            t_xn1d = em.dma("sy", lambda sy, ti=ti, sl2=sl2: sy.dma_start(
                out=scr["xn1_scr"][128 * ti:128 * (ti + 1), :], in_=xn1t[:, sl2]))
            lu[("xn1", sl2)] = t_xn1d
            x1_toks.append(t_xn1d)
    return x1_toks[-1]


def stage_s4(nc, em, bufs, wd, scr):
    PS = bufs["ps"]
    PS2 = bufs["ps2"]
    pbf = bufs["ps_bf"]
    identb = bufs["ident"]
    id_tok = bufs["ident_tok"]
    xn1f = bufs["s4_xn1"]
    tmt = bufs["s4_tm"]
    wsb = bufs["s4_w"]
    bsb = bufs["s4_b"]
    q_sb = bufs["s4_q"]
    k_sb = bufs["s4_k"]
    v_fm = bufs["s4_vf"]
    v_tm = bufs["s4_vt"]
    pt = bufs["s4_pt"]
    zfm = bufs["s4_zfm"]
    projsb = bufs["s4_proj"]
    ones64 = bufs["s4_ones"]
    rrow = bufs["s4_r"]
    rbf = bufs["s4_rbf"]
    bcsb = bufs["s4_bc"]
    s1b = bufs["s4_s1"]
    lu = {}

    bias_toks = []
    for x, name in ((0, "bq"), (1, "bk"), (2, "bv")):
        src = wd[name].rearrange("(c p) o -> p c o", p=128)
        bias_toks.append(em.dma("sy", lambda sy, x=x, src=src: sy.dma_start(
            out=bsb[:, x, :], in_=src[:, :, 0], allow_slow_non_contiguous=True)))
    proj_toks = [em.dma("gp", lambda gp, kc=kc: gp.dma_start(
        out=projsb[:, kc], in_=wd["proj"][128 * kc:128 * (kc + 1), :]))
        for kc in range(8)]
    t_ones = em.op("gp", lambda gp: gp.memset(ones64[:], 1.0))
    em.op("gp", lambda gp: gp.memset(v_tm[:, :, :, 64:65], 1.0))
    t_vones = em.op("gp", lambda gp: gp.memset(v_tm[:, :, :, 129:130], 1.0))

    for s in range(2):
        soff = SEQ * s
        for ti in range(16):
            sl = ti % 2
            em.wait("sy", lu.get(("tmt", sl)))
            t_ld = em.dma("sy", lambda sy, ti=ti, sl=sl, soff=soff: sy.dma_start(
                out=tmt[:, sl], in_=scr["xn1_scr"][soff + 128 * ti:soff + 128 * (ti + 1), :]))
            em.wait("te", t_ld)
            em.wait("te", id_tok)
            em.wait("te", lu.get(("pbt", ti % 2)))
            pb = pbf[ti % 2]
            def f_tr(te, sl=sl, pb=pb):
                for c in range(8):
                    ins = te.transpose(pb[:, 128 * c:128 * (c + 1)],
                                       tmt[:, sl, 128 * c:128 * (c + 1)], identb[:])
                return ins
            t_tr = em.op("te", f_tr)
            lu[("tmt", sl)] = t_tr
            em.wait("sc", t_tr)
            em.wait("sc", lu.get("xn1f_used"))
            t_ev = em.op("sc", lambda sc, ti=ti, pb=pb: sc.activation(
                out=xn1f[:, :, 128 * ti:128 * (ti + 1)],
                in_=pb[:].rearrange("p (c f) -> p c f", f=128), func=AF.Copy))
            lu[("pbt", ti % 2)] = t_ev
        t_xn1f = t_ev

        for hp in range(8):
            pp = 0
            w_toks = []
            em.wait("gp", lu.get(("wsb", pp)))
            for x, name in ((0, "WqT"), (1, "WkT"), (2, "WvT")):
                src = wd[name].rearrange("(c p) m -> p c m", p=128)
                w_toks.append(em.dma("gp", lambda gp, x=x, src=src, hp=hp, pp=pp:
                                     gp.dma_start(
                    out=wsb[:, pp, x], in_=src[:, :, 128 * hp:128 * (hp + 1)])))
            for x, dst in ((0, q_sb), (1, k_sb), (2, v_fm)):
                for tb in range(4):
                    bank = PS[tb % 2]
                    for tok in w_toks:
                        em.wait("te", tok)
                    em.wait("te", t_xn1f)
                    em.wait("te", lu.get(("bank", tb % 2)))
                    def f_mm(te, x=x, tb=tb, bank=bank, pp=pp):
                        for kd in range(8):
                            ins = te.matmul(bank[:], wsb[:, pp, x, kd],
                                            xn1f[:, kd, 512 * tb:512 * (tb + 1)],
                                            start=(kd == 0), stop=(kd == 7))
                        return ins
                    t_mm = em.op("te", f_mm)
                    em.wait("ve", t_mm)
                    for tok in bias_toks:
                        em.wait("ve", tok)
                    em.wait("ve", lu.get(("qkv", x, pp)))
                    t_qe = em.op("ve", lambda ve, x=x, tb=tb, dst=dst, bank=bank,
                                 hp=hp, pp=pp:
                                 ve.tensor_scalar_add(dst[:, pp, 512 * tb:512 * (tb + 1)],
                                                      bank[:], bsb[:, x, hp:hp + 1]))
                    lu[("bank", tb % 2)] = t_qe
                lu[("qkv", x, pp)] = t_qe
            t_qkv = t_qe
            for kc in range(16):
                pb = pbf[kc % 2]
                em.wait("te", t_qkv)
                em.wait("te", lu.get(("pbt", kc % 2)))
                em.wait("te", lu.get(("vtm_used", pp)))
                t_vt = em.op("te", lambda te, kc=kc, pb=pb, pp=pp: te.transpose(
                    pb[:, 0:128], v_fm[:, pp, 128 * kc:128 * (kc + 1)], identb[:]))
                em.wait("sc", t_vt)
                em.wait("sc", lu.get(("vtm_used", pp)))
                vt = v_tm[:, pp, kc]
                vdst = bass.AP(vt.tensor, vt.offset, [vt.ap[0], [65, 2], [1, 64]])
                t_ve = em.op("sc", lambda sc, pb=pb, vdst=vdst: sc.activation(
                    out=vdst, in_=pb[:, 0:128].rearrange("p (a b) -> p a b", b=64),
                    func=AF.Copy))
                lu[("pbt", kc % 2)] = t_ve
                lu[("qkv", 2, pp)] = t_vt
            t_vtm = t_ve
            for qg in range(4):
                nkc = 4 * qg + 4
                for kc in range(nkc):
                    wb = PS2[kc % 2]
                    em.wait("te", t_qkv)
                    em.wait("te", lu.get(("stps", kc % 2)))
                    em.wait("te", lu.get(("bank", 2 * (kc % 2))))
                    em.wait("te", lu.get(("bank", 2 * (kc % 2) + 1)))
                    def f_qk(te, kc=kc, qg=qg, wb=wb, pp=pp):
                        te.matmul(wb[:, 0:512], k_sb[0:64, pp, 128 * kc:128 * (kc + 1)],
                                  q_sb[0:64, pp, 512 * qg:512 * (qg + 1)],
                                  start=True, stop=True, tile_position=(0, 0))
                        return te.matmul(wb[:, 512:1024],
                                         k_sb[64:128, pp, 128 * kc:128 * (kc + 1)],
                                         q_sb[64:128, pp, 512 * qg:512 * (qg + 1)],
                                         start=True, stop=True, tile_position=(64, 0))
                    t_st = em.op("te", f_qk)
                    em.wait("sc", t_st)
                    em.wait("sc", lu.get(("ptw", pp, kc)))
                    def f_ex(sc, kc=kc, wb=wb, pp=pp):
                        return sc.activation(
                            out=pt[:, pp, :, kc, :],
                            in_=wb.rearrange("p (h q) -> p h q", q=512),
                            func=AF.Exp, scale=0.125)
                    t_ex = em.op("sc", f_ex)
                    lu[("stps", kc % 2)] = t_ex
                    lu[("bank", 2 * (kc % 2))] = t_ex
                    lu[("bank", 2 * (kc % 2) + 1)] = t_ex
                    if kc >= 4 * qg:
                        em.wait("gp", t_ex)
                        def f_msk(gp, kc=kc, qg=qg, pp=pp):
                            base = 512 * qg - 128 * kc
                            return gp.affine_select(
                                out=pt[:, pp, :, kc, :], in_=pt[:, pp, :, kc, :],
                                pattern=[[0, 2], [1, 512]], compare_op=ALU.is_ge,
                                fill=0.0, base=base, channel_multiplier=-1)
                        t_msk = em.op("gp", f_msk)
                        lu[("ptm", pp, kc)] = t_msk
                    else:
                        lu[("ptm", pp, kc)] = t_ex
                for hd in range(2):
                    psz = PS[4 + hd]
                    em.wait("te", lu.get(("zps", hd)))
                    em.wait("te", t_vtm)
                    for kc in range(nkc):
                        em.wait("te", lu[("ptm", pp, kc)])
                    def f_av(te, hd=hd, nkc=nkc, psz=psz, pp=pp):
                        for kc in range(nkc):
                            ins = te.matmul(psz[0:65, :],
                                            v_tm[:, pp, kc, 65 * hd:65 * (hd + 1)],
                                            pt[:, pp, hd, kc], start=(kc == 0),
                                            stop=(kc == nkc - 1))
                        return ins
                    t_av = em.op("te", f_av)
                    for kc in range(nkc):
                        lu[("ptw", pp, kc)] = t_av
                    em.wait("sc", t_av)
                    em.wait("sc", lu.get("rrow"))
                    em.op("sc", lambda sc, psz=psz: sc.activation(
                        out=rrow[:], in_=psz[64:65, :], func=AF.Ln, scale=8.0))
                    t_rb = em.op("sc", lambda sc, pp=pp: sc.activation(
                        out=rbf[:, pp], in_=rrow[:], func=AF.Exp, scale=-1.0))
                    em.wait("te", t_rb)
                    em.wait("te", t_ones)
                    t_bc = em.op("te", lambda te, psz=psz, pp=pp: te.matmul(
                        psz[64:128, :], ones64[:], rbf[:, pp], start=True, stop=True,
                        tile_position=(0, 64)))
                    em.wait("sc", t_bc)
                    em.wait("sc", lu.get(("bcsb", pp)))
                    t_bce = em.op("sc", lambda sc, psz=psz, pp=pp: sc.activation(
                        out=bcsb[:, pp], in_=psz[64:128, :][0:64, :], func=AF.Copy))
                    lu["rrow"] = t_bc
                    em.wait("ve", t_bce)
                    em.wait("ve", lu.get("zfm_used"))
                    t_zf = em.op("ve", lambda ve, hd=hd, hp=hp, qg=qg, psz=psz, pp=pp:
                                 ve.tensor_mul(
                                     zfm[64 * hd:64 * (hd + 1), hp, 512 * qg:512 * (qg + 1)],
                                     psz[0:64, :], bcsb[:, pp]))
                    lu[("bcsb", pp)] = t_zf
                    lu[("zps", hd)] = t_zf
            lu[("vtm_used", pp)] = t_av
            lu[("wsb", pp)] = t_st
            lu[("qkv", 0, pp)] = t_st
            lu[("qkv", 1, pp)] = t_st
        lu["xn1f_used"] = t_st
        for ti in range(16):
            for dn in range(2):
                bank = PS[(2 * ti + dn) % 2]
                for tok in proj_toks:
                    em.wait("te", tok)
                em.wait("te", t_zf)
                em.wait("te", lu.get(("bank", (2 * ti + dn) % 2)))
                def f_pr(te, ti=ti, dn=dn, bank=bank):
                    for kc in range(8):
                        ins = te.matmul(bank[:], zfm[:, kc, 128 * ti:128 * (ti + 1)],
                                        projsb[:, kc, 512 * dn:512 * (dn + 1)],
                                        start=(kc == 0), stop=(kc == 7))
                    return ins
                t_pr = em.op("te", f_pr)
                em.wait("sc", t_pr)
                em.wait("sc", lu.get(("s1b", ti % 2)))
                t_se = em.op("sc", lambda sc, ti=ti, dn=dn, bank=bank: sc.activation(
                    out=s1b[:, ti % 2, 512 * dn:512 * (dn + 1)], in_=bank[:], func=AF.Copy))
                lu[("bank", (2 * ti + dn) % 2)] = t_se
            em.wait("sy", t_se)
            t_sd = em.dma("sy", lambda sy, ti=ti, s=s, soff=soff: sy.dma_start(
                out=scr["s1_scr"][soff + 128 * ti:soff + 128 * (ti + 1), :],
                in_=s1b[:, ti % 2]))
            lu[("s1b", ti % 2)] = t_sd
        lu["zfm_used"] = t_pr
    return t_sd


def stage_s5(nc, em, bufs, wd, scr, out_d):
    PS = bufs["ps"]
    pbf = bufs["ps_bf"]
    identb = bufs["ident"]
    id_tok = bufs["ident_tok"]
    wq = bufs["s5_wq"]; wk = bufs["s5_wk"]; wi = bufs["s5_wi"]; w1 = bufs["s5_w1"]
    wo = bufs["s5_wo"]
    vw = bufs["s5_vw"]
    us = bufs["s5_u"]
    bb = bufs["s5_bb"]
    rw2 = bufs["s5_rw2"]
    rb1c = bufs["s5_rb1"]
    ebc = bufs["s5_ebc"]
    alc = bufs["s5_al"]
    invn = bufs["s5_invn"]
    tmt = bufs["s4_tm"]
    xnb = bufs["s5_xnb"]
    qb = bufs["s5_qb"]
    kb = bufs["s5_kb"]
    iqb = bufs["s5_iqb"]
    ikb = bufs["s5_ikb"]
    xv = bufs["s5_xv"]
    yw = bufs["s5_yw"]
    cum = bufs["s5_cum"]
    carry = bufs["s5_carry"]
    hb = bufs["s5_hb"]
    lg = bufs["s5_lg"]
    lcol = bufs["s5_lcol"]
    gwb2 = bufs["s5_gwb"]
    wT = bufs["s5_wt"]
    wexp = bufs["s5_wexp"]
    prodw = bufs["s5_prodw"]
    osum = bufs["s5_osum"]
    x1t = bufs["s5_x1t"]
    s1t = bufs["s5_s1t"]
    g1b = bufs["s5_g1"]
    lu = {}

    wtoks = []
    for dst, name in ((wq, "W_QT"), (wk, "W_KT"), (wi, "W_invT"), (w1, "rw1T"),
                      (wo, "W_OT")):
        for kd in range(8):
            wtoks.append(em.dma("gp", lambda gp, dst=dst, name=name, kd=kd:
                                 gp.dma_start(out=dst[:, kd],
                                              in_=wd[name][128 * kd:128 * (kd + 1), :])))
    for i, name in enumerate(("VfT", "WfT", "ViT", "WiT")):
        for kd in range(8):
            wtoks.append(em.dma("gp", lambda gp, i=i, name=name, kd=kd:
                                 gp.dma_start(out=vw[:, i, kd],
                                              in_=wd[name][128 * kd:128 * (kd + 1), :])))
    for i, name in enumerate(("Uf", "Ui")):
        for kc in range(4):
            wtoks.append(em.dma("gp", lambda gp, i=i, name=name, kc=kc:
                                 gp.dma_start(out=us[:, i, kc],
                                              in_=wd[name][128 * kc:128 * (kc + 1), :])))
    for i, name in enumerate(("b_fwd", "b_inv")):
        wtoks.append(em.dma("gp", lambda gp, i=i, name=name: gp.dma_start(
            out=bb[:, i], in_=wd[name][:, :])))
    for kd in range(8):
        wtoks.append(em.dma("gp", lambda gp, kd=kd: gp.dma_start(
            out=rw2[:, kd], in_=wd["rw2T"][128 * kd:128 * (kd + 1), :])))
    src = wd["rb1"].rearrange("(c p) o -> p c o", p=128)
    t_rb1 = em.dma("sy", lambda sy, src=src: sy.dma_start(
        out=rb1c[:], in_=src[:, :, 0], allow_slow_non_contiguous=True))
    em.wait("sc", t_rb1)
    e0 = wd["rb2"]
    t_eb1 = em.dma("sy", lambda sy, e0=e0: sy.dma_start(
        out=ebc[:, 0], in_=bass.AP(e0.tensor, e0.offset, [[0, 128], [1, 8]])))
    e1 = wd["expert_bias"]
    t_eb2 = em.dma("sy", lambda sy, e1=e1: sy.dma_start(
        out=ebc[:, 1], in_=bass.AP(e1.tensor, e1.offset, [[0, 128], [1, 8]])))
    a0 = wd["alpha_bi"]
    t_al = em.dma("sy", lambda sy, a0=a0: sy.dma_start(
        out=alc[:], in_=bass.AP(a0.tensor, a0.offset, [[0, 128], [1, 1]])))
    em.wait("ve", t_eb1)
    em.wait("ve", t_eb2)
    t_ebs = em.op("ve", lambda ve: ve.tensor_add(ebc[:, 0], ebc[:, 0], ebc[:, 1]))
    t_io = em.op("gp", lambda gp: gp.iota(
        invn[:], pattern=[[0, 2], [128, 16]], base=1, channel_multiplier=1,
        allow_small_or_imprecise_dtypes=True))
    em.wait("ve", t_io)
    t_invn = em.op("ve", lambda ve: ve.reciprocal(invn[:], invn[:]))
    t_cz = em.op("ve", lambda ve: ve.memset(carry[:], 0.0))
    em.op("ve", lambda ve: ve.memset(bufs["zcol128"][:], 0.0))
    lu["carry_z"] = t_cz

    def proj_into(dst, wsb, rhs_buf, tag, nk=8, bias_col=None, act=AF.Copy):
        for dq in range(8):
            bank = PS[dq % 4]
            em.wait("te", lu.get(("bank", dq % 4)))
            def f(te, dq=dq, bank=bank, wsb=wsb, rhs_buf=rhs_buf):
                for kd in range(8):
                    ins = te.matmul(bank[:, 0:512], wsb[:, kd, 128 * dq:128 * (dq + 1)],
                                    rhs_buf[:, kd], start=(kd == 0), stop=(kd == 7))
                return ins
            t_mm = em.op("te", f)
            em.wait("sc", t_mm)
            em.wait("sc", lu.get(("w5", tag)))
            kw = {}
            if bias_col is not None:
                kw["bias"] = bias_col[:, dq:dq + 1]
            t_ev = em.op("sc", lambda sc, dq=dq, bank=bank, dst=dst, kw=kw, act=act:
                         sc.activation(out=dst[:, dq], in_=bank[:, 0:512], func=act, **kw))
            lu[("bank", dq % 4)] = t_ev
        return t_ev

    t_x2d = None
    for b in range(8):
        t0 = 512 * b
        ti0 = 4 * b
        for j in range(4):
            sl = j % 2
            em.wait("sy", lu.get(("tmt5", sl)))
            t_ld = em.dma("sy", lambda sy, t0=t0, j=j, sl=sl: sy.dma_start(
                out=tmt[:, sl], in_=scr["xn1_scr"][t0 + 128 * j:t0 + 128 * (j + 1), :]))
            em.wait("te", t_ld)
            em.wait("te", id_tok)
            em.wait("te", lu.get(("pb5", j % 2)))
            em.wait("te", lu.get("xnb_used"))
            pb = pbf[j % 2]
            def f_tr(te, sl=sl, pb=pb):
                for c in range(8):
                    ins = te.transpose(pb[:, 128 * c:128 * (c + 1)],
                                       tmt[:, sl, 128 * c:128 * (c + 1)], identb[:])
                return ins
            t_tr = em.op("te", f_tr)
            lu[("tmt5", sl)] = t_tr
            em.wait("sc", t_tr)
            em.wait("sc", lu.get("xnb_used"))
            t_xe = em.op("sc", lambda sc, j=j, pb=pb: sc.activation(
                out=xnb[:, :, 128 * j:128 * (j + 1)],
                in_=pb[:].rearrange("p (c f) -> p c f", f=128), func=AF.Copy))
            lu[("pb5", j % 2)] = t_xe
        t_xnb = t_xe
        for tok in wtoks:
            em.wait("te", tok)
        em.wait("te", t_xnb)
        lu[("w5", "q")] = lu.get("qb_used")
        t_q = proj_into(qb, wq, xnb, "q")
        lu[("w5", "k")] = lu.get("kb_used")
        t_k = proj_into(kb, wk, xnb, "k")

        for br in range(2):
            if br == 1:
                em.wait("te", t_q)
                lu[("w5", "iq")] = lu.get("iqb_used")
                t_iq = proj_into(iqb, wi, qb, "iq")
                em.wait("te", t_k)
                lu[("w5", "ik")] = lu.get("br0_om")
                t_ik = proj_into(ikb, wi, kb, "ik")
            Qs = qb if br == 0 else iqb
            Ks = kb if br == 0 else ikb
            t_qs = t_q if br == 0 else t_iq
            t_ks = t_k if br == 0 else t_ik
            voff = 2 * br
            em.wait("te", t_qs)
            lu[("w5", "h")] = lu.get("hb_used")
            t_h = proj_into(hb, w1, Qs, "h", bias_col=rb1c, act=AF.Gelu)
            for qt in range(4):
                bank = PS[4]
                em.wait("te", t_h)
                em.wait("te", lu.get(("bank", 4)))
                def f_lg(te, qt=qt, bank=bank):
                    for rhc in range(8):
                        ins = te.matmul(bank[:, 0:8], hb[:, rhc, 128 * qt:128 * (qt + 1)],
                                        rw2[:, rhc], start=(rhc == 0), stop=(rhc == 7))
                    return ins
                t_lgm = em.op("te", f_lg)
                em.wait("ve", t_lgm)
                em.wait("ve", t_ebs)
                em.wait("ve", lu.get("lg_used"))
                em.op("ve", lambda ve, qt=qt, bank=bank: ve.tensor_add(
                    lg[:, qt], bank[:, 0:8], ebc[:, 0]))
                lu[("bank", 4)] = em.tok("ve")
                em.op("ve", lambda ve, qt=qt: ve.reduce_max(
                    lcol[:, 0:1], lg[:, qt], axis=AX.X, negate=True))
                t_nm = em.op("ve", lambda ve: ve.engine_nop()) if False else None
                em.wait("sc", em.tok("ve"))
                t_e = em.op("sc", lambda sc, qt=qt: sc.activation(
                    out=lg[:, qt], in_=lg[:, qt], func=AF.Exp, bias=lcol[:, 0:1]))
                em.wait("ve", t_e)
                em.op("ve", lambda ve, qt=qt: ve.reduce_sum(
                    lcol[:, 1:2], lg[:, qt], axis=AX.X))
                em.op("ve", lambda ve: ve.reciprocal(lcol[:, 2:3], lcol[:, 1:2]))
                t_wg = em.op("ve", lambda ve, qt=qt: ve.tensor_scalar_mul(
                    lg[:, qt], lg[:, qt], lcol[:, 2:3]))
                em.wait("ve", lu.get("gwb2"))
                t_gc = em.op("ve", lambda ve, qt=qt: ve.tensor_copy(gwb2[:], lg[:, qt]))
                em.wait("te", t_gc)
                em.wait("te", id_tok)
                em.wait("te", lu.get("wtps5"))
                t_wt = em.op("te", lambda te, qt=qt: te.transpose(
                    pbf[0][0:8, 0:128], gwb2[:], identb[:]))
                lu["gwb2"] = t_wt
                em.wait("ve", t_wt)
                em.wait("ve", lu.get(("wT_used", qt)))
                t_wte = em.op("ve", lambda ve, qt=qt: ve.tensor_copy(
                    wT[:, qt], pbf[0][0:8, 0:128]))
                lu["wtps5"] = t_wte
                lu["lg_used"] = t_wte
            t_wT = t_wte
            em.wait("sy", t_wT)
            t_wd = em.dma("sy", lambda sy, br=br, t0=t0: sy.dma_start(
                out=scr["wgt_scr"][br][:, t0:t0 + 512],
                in_=wT[:].rearrange("k q f -> k (q f)")))
            lu[("wT_used", 0)] = t_wd
            lu[("wT_used", 1)] = t_wd
            wg_toks = []
            for krc in range(4):
                for kk in range(2):
                    em.wait("sy", t_wd)
                    em.wait("sy", lu.get(("wexp", krc, kk)))
                    wsrc = scr["wgt_scr"]
                    woff = (br * 8 + 2 * krc + kk) * T + t0
                    wg_toks.append(em.dma("sy", lambda sy, krc=krc, kk=kk, woff=woff,
                                          wsrc=wsrc: sy.dma_start(
                        out=wexp[64 * kk:64 * (kk + 1), krc, :],
                        in_=bass.AP(wsrc.tensor, wsrc.offset + woff, [[0, 64], [1, 512]]))))
            for which, (dst, widx, rhs_buf, t_rhs) in enumerate(
                    ((xv, voff + 0, Qs, t_qs), (yw, voff + 1, Ks, t_ks))):
                for krc in range(4):
                    bank = PS[4 + (krc % 2)]
                    em.wait("te", t_rhs)
                    em.wait("te", lu.get(("bank", 4 + krc % 2)))
                    def f(te, krc=krc, widx=widx, rhs_buf=rhs_buf, bank=bank):
                        for kd in range(8):
                            ins = te.matmul(bank[:, 0:512],
                                            vw[:, widx, kd, 128 * krc:128 * (krc + 1)],
                                            rhs_buf[:, kd], start=(kd == 0), stop=(kd == 7))
                        return ins
                    t_mm = em.op("te", f)
                    em.wait("ve", t_mm)
                    em.wait("ve", lu.get(("xvyw", which, krc)))
                    t_ev = em.op("ve", lambda ve, krc=krc, dst=dst, bank=bank:
                                 ve.tensor_copy(dst[:, krc], bank[:, 0:512]))
                    lu[("bank", 4 + krc % 2)] = t_ev
                    lu[("xvyw2", which, krc)] = t_ev
            if b == 4 and br == 0:
                em.op("ve", lambda ve: ve.memset(carry[:], 0.0))
            for krc in range(4):
                em.wait("ve", lu[("xvyw2", 1, krc)])
                def f_scan(ve, br=br, krc=krc):
                    zb = bufs["zcol128"][:]
                    z0 = bass.AP(zb.tensor, zb.offset, [zb.ap[0], [0, 512]])
                    return ve.tensor_tensor_scan(
                        out=cum[:, krc], data0=yw[:, krc], data1=z0,
                        initial=carry[:, br, krc:krc + 1], op0=ALU.add, op1=ALU.add)
                t_sc = em.op("ve", f_scan)
                t_cu = em.op("ve", lambda ve, br=br, krc=krc: ve.tensor_copy(
                    carry[:, br, krc:krc + 1], cum[:, krc, 511:512]))
                lu[("cum", krc)] = t_cu
            lu["carry_z"] = t_cu
            pw_toks = []
            for krc in range(4):
                em.wait("ve", lu[("cum", krc)])
                em.wait("ve", lu[("xvyw2", 0, krc)])
                em.wait("ve", wg_toks[2 * krc])
                em.wait("ve", wg_toks[2 * krc + 1])
                em.wait("ve", lu.get(("prodw", krc)))
                em.op("ve", lambda ve, krc=krc: ve.tensor_mul(
                    xv[:, krc], xv[:, krc], cum[:, krc]))
                lu[("xvyw", 0, krc)] = em.tok("ve")
                t_pw = em.op("ve", lambda ve, krc=krc: ve.tensor_mul(
                    prodw[:, krc], xv[:, krc], wexp[:, krc]))
                lu[("xvyw", 1, krc)] = t_pw
                lu[("wexp", krc, 0)] = t_pw
                lu[("wexp", krc, 1)] = t_pw
                pw_toks.append(t_pw)
            for dq in range(8):
                bank = PS[4 + dq % 2]
                for tok in pw_toks:
                    em.wait("te", tok)
                em.wait("te", t_wT)
                em.wait("te", lu.get(("bank", 4 + dq % 2)))
                def f_o(te, dq=dq, br=br, bank=bank):
                    for krc in range(4):
                        te.matmul(bank[:, 0:512], us[:, br, krc, 128 * dq:128 * (dq + 1)],
                                  prodw[:, krc], start=(krc == 0), stop=False)
                    return te.matmul(bank[:, 0:512], bb[:, br, 128 * dq:128 * (dq + 1)],
                                     wT[:].rearrange("k q f -> k (q f)"),
                                     start=False, stop=True)
                t_om = em.op("te", f_o)
                if br == 0:
                    em.wait("sc", t_om)
                    em.wait("sc", lu.get(("osum", dq)))
                    t_oe = em.op("sc", lambda sc, dq=dq, bank=bank: sc.activation(
                        out=osum[:, dq], in_=bank[:, 0:512], func=AF.Copy))
                else:
                    em.wait("ve", t_om)
                    em.wait("ve", t_al)
                    em.wait("ve", lu.get(("osum2", dq)))
                    t_oe = em.op("ve", lambda ve, dq=dq, bank=bank: ve.scalar_tensor_tensor(
                        osum[:, dq], bank[:, 0:512], alc[:, 0:1], osum[:, dq],
                        ALU.mult, ALU.add))
                    lu[("osum", dq)] = t_oe
                lu[("bank", 4 + dq % 2)] = t_oe
                lu[("osum2", dq)] = t_oe
            for krc in range(4):
                lu[("prodw", krc)] = t_om
            lu["qb_used"] = t_om if br == 1 else lu.get("qb_used")
        lu["kb_used"] = t_om
        lu["iqb_used"] = t_om
        lu["ikb_used"] = t_om
        lu["qb_used"] = t_om
        lu["hb_used"] = t_om
        lu["xnb_used"] = t_ik
        for qt in range(4):
            sl = 0
            em.wait("sy", lu.get(("tmt5", 0)))
            em.wait("sy", lu.get(("tmt5", 1)))
            em.wait("sy", lu.get(("x1t", sl)))
            t_x1 = em.dma("sy", lambda sy, t0=t0, qt=qt, sl=sl: sy.dma_start(
                out=x1t[:, sl], in_=scr["x1_scr"][t0 + 128 * qt:t0 + 128 * (qt + 1), :]))
            t_s1 = em.dma("sy", lambda sy, t0=t0, qt=qt, sl=sl: sy.dma_start(
                out=s1t[:, sl], in_=scr["s1_scr"][t0 + 128 * qt:t0 + 128 * (qt + 1), :]))
            for dn in range(2):
                bank = PS[dn % 2]
                for dq_ in range(8):
                    em.wait("te", lu[("osum2", dq_)])
                em.wait("te", lu.get(("bank", dn % 2)))
                def f_wo(te, qt=qt, dn=dn, bank=bank):
                    for kd in range(8):
                        ins = te.matmul(bank[:], osum[:, kd, 128 * qt:128 * (qt + 1)],
                                        wo[:, kd, 512 * dn:512 * (dn + 1)],
                                        start=(kd == 0), stop=(kd == 7))
                    return ins
                t_wom = em.op("te", f_wo)
                em.wait("sc", t_wom)
                em.wait("sc", t_invn)
                em.wait("sc", lu.get("g1_used"))
                t_g1 = em.op("sc", lambda sc, dn=dn, bank=bank, ti0=ti0, qt=qt:
                             sc.activation(out=g1b[:, 512 * dn:512 * (dn + 1)],
                                           in_=bank[:], func=AF.Silu,
                                           scale=invn[:, ti0 + qt:ti0 + qt + 1]))
                lu[("bank", dn % 2)] = t_g1
            em.wait("ve", t_g1)
            em.wait("ve", t_s1)
            t_sg = em.op("ve", lambda ve, sl=sl: ve.tensor_mul(
                g1b[:], s1t[:, sl], g1b[:]))
            em.wait("ve", t_x1)
            em.wait("ve", lu.get(("x2", sl)))
            t_x2 = em.op("ve", lambda ve, sl=sl: ve.tensor_add(
                x1t[:, sl], x1t[:, sl], g1b[:]))
            lu["g1_used"] = t_x2
            lu[("x1t", sl)] = t_x2
            em.wait("sy", t_x2)
            t_x2d = em.dma("sy", lambda sy, t0=t0, qt=qt, sl=sl: sy.dma_start(
                out=out_d[t0 + 128 * qt:t0 + 128 * (qt + 1), :], in_=x1t[:, sl]))
            lu[("x2", sl)] = t_x2d
            lu[("tmt5", 0)] = t_x2d
            lu[("tmt5", 1)] = t_x2d
        lu["osum_used"] = t_wom
        for dq in range(8):
            lu[("osum", dq)] = t_wom
    return t_x2d



ARENA_UNITS = 106240


def build_fused_program():
    nc = bass.Bass("TRN2", target_bir_lowering=False)
    wd = {}
    wd["u_d"] = nc.dram_tensor("u_d", [T, D], F32, kind="ExternalInput").ap()
    wd["W_inT"] = nc.dram_tensor("W_inT", [1024, 4368], F32, kind="ExternalInput").ap()
    wd["conv_w"] = nc.dram_tensor("conv_w", [NJC, 128, 4], F32, kind="ExternalInput").ap()
    wd["conv_b"] = nc.dram_tensor("conv_b", [NJC, 128, 1], F32, kind="ExternalInput").ap()
    wd["dt_bias"] = nc.dram_tensor("dt_bias", [16, 1], F32, kind="ExternalInput").ap()
    wd["A_log"] = nc.dram_tensor("A_log", [16, 1], F32, kind="ExternalInput").ap()
    wd["Dh"] = nc.dram_tensor("Dh", [1, 16], F32, kind="ExternalInput").ap()
    wd["norm_weight"] = nc.dram_tensor("norm_weight", [1, DIN], F32, kind="ExternalInput").ap()
    wd["W_outT"] = nc.dram_tensor("W_outT", [DIN, D], F32, kind="ExternalInput").ap()
    wd["Wg"] = nc.dram_tensor("Wg", [1024, 8], F32, kind="ExternalInput").ap()
    wd["bg"] = nc.dram_tensor("bg", [1, 8], F32, kind="ExternalInput").ap()
    wd["We"] = nc.dram_tensor("We", [8, 1024, 1024], F32, kind="ExternalInput").ap()
    wd["be"] = nc.dram_tensor("be", [8, 1024], F32, kind="ExternalInput").ap()
    for n_ in ("WqT", "WkT", "WvT", "proj", "W_QT", "W_KT", "W_invT", "rw1T", "W_OT"):
        wd[n_] = nc.dram_tensor(n_, [1024, 1024], F32, kind="ExternalInput").ap()
    for n_ in ("bq", "bk", "bv", "rb1"):
        wd[n_] = nc.dram_tensor(n_, [1024, 1], F32, kind="ExternalInput").ap()
    for n_ in ("VfT", "WfT", "ViT", "WiT"):
        wd[n_] = nc.dram_tensor(n_, [1024, 512], F32, kind="ExternalInput").ap()
    for n_ in ("Uf", "Ui"):
        wd[n_] = nc.dram_tensor(n_, [512, 1024], F32, kind="ExternalInput").ap()
    for n_ in ("b_fwd", "b_inv"):
        wd[n_] = nc.dram_tensor(n_, [8, 1024], F32, kind="ExternalInput").ap()
    wd["rw2T"] = nc.dram_tensor("rw2T", [1024, 8], F32, kind="ExternalInput").ap()
    wd["rb2"] = nc.dram_tensor("rb2", [1, 8], F32, kind="ExternalInput").ap()
    wd["expert_bias"] = nc.dram_tensor("expert_bias", [1, 8], F32, kind="ExternalInput").ap()
    wd["alpha_bi"] = nc.dram_tensor("alpha_bi", [1, 1], F32, kind="ExternalInput").ap()
    out_d = nc.dram_tensor("x2", [T, D], F32, kind="ExternalOutput").ap()

    scr = {
        "xn_scr": nc.dram_tensor("xn_scr", [D, T], BF16, kind="Internal").ap(),
        "z_scr": nc.dram_tensor("z_scr", [T, DIN], BF16, kind="Internal").ap(),
        "xfm_scr": nc.dram_tensor("xfm_scr", [DIN, T], BF16, kind="Internal").ap(),
        "bfm_scr": nc.dram_tensor("bfm_scr", [128, T], BF16, kind="Internal").ap(),
        "cfm_scr": nc.dram_tensor("cfm_scr", [128, T], BF16, kind="Internal").ap(),
        "dtr_scr": nc.dram_tensor("dtr_scr", [16, T], F32, kind="Internal").ap(),
        "acs_scr": nc.dram_tensor("acs_scr", [16, T], F32, kind="Internal").ap(),
        "cdec_scr": nc.dram_tensor("cdec_scr", [2, 16, 8], F32, kind="Internal").ap(),
        "s_scr": nc.dram_tensor("s_scr", [T, D], BF16, kind="Internal").ap(),
        "x1_scr": nc.dram_tensor("x1_scr", [T, D], F32, kind="Internal").ap(),
        "xn1_scr": nc.dram_tensor("xn1_scr", [T, D], BF16, kind="Internal").ap(),
        "s1_scr": nc.dram_tensor("s1_scr", [T, D], BF16, kind="Internal").ap(),
        "wgt_scr": nc.dram_tensor("wgt_scr", [2, 8, T], BF16, kind="Internal").ap(),
    }

    em = Em()
    with contextlib.ExitStack() as stx:
        bufs = {}
        def sb(name, shape, dtype):
            return stx.enter_context(nc.sbuf_tensor(name, shape, dtype))
        bufs["ident"] = sb("ident", [128, 128], BF16)
        bufs["ln_col"] = sb("ln_col", [128, 16], F32)
        AR = sb("AR", [128, ARENA_UNITS], BF16)
        ar = Arena(AR, ARENA_UNITS)
        mk_psum_pool(nc, stx, bufs)
        sems = em.make_sems(nc, stx)
        blk = stx.enter_context(nc.Block())

        bufs["ident_tok"] = build_identity(nc, em, bufs)

        xn = ar.alloc([128, 8, T], BF16)
        bufs["xn_fm"] = xn
        m_xn = ar.mark()
        bufs["ln_x"] = ar.alloc([128, 2, D], F32)
        bufs["ln_sq"] = ar.alloc([128, D], F32)
        bufs["ln_xn"] = ar.alloc([128, 2, D], BF16)
        t_fm = stage_ln(nc, em, bufs, wd["u_d"], xn, key="ln")
        em.barrier()
        dump_fm(em, xn, scr["xn_scr"], 8, T)
        ar.release(m_xn)
        mk_s1_bufs(nc, stx, bufs, alloc=ar.alloc)
        stage_s1(nc, em, bufs, wd, scr, t_fm)
        em.barrier()
        ar.release(0)

        mk_s2_bufs(nc, stx, bufs, alloc=ar.alloc)
        bufs["identf_tok"] = build_identity_f32(nc, em, bufs)
        stage_s2(nc, em, bufs, wd, scr)
        em.barrier()
        ar.release(0)

        xn2 = ar.alloc([128, 8, T], BF16)
        bufs["xn_fm"] = xn2
        xn_tok = load_fm(em, xn2, scr["xn_scr"], 8, T)
        mk_s3_bufs(nc, stx, bufs, alloc=ar.alloc)
        bufs["s1_acc"] = ar.alloc([128, 1024], F32)
        stage_s3(nc, em, bufs, wd, scr, xn_tok)
        em.barrier()
        ar.release(0)

        mk_s4_bufs(nc, stx, bufs, alloc=ar.alloc)
        stage_s4(nc, em, bufs, wd, scr)
        em.barrier()
        ar.release(0)

        mk_s5_bufs(nc, stx, bufs, alloc=ar.alloc)
        out_tok = stage_s5(nc, em, bufs, wd, scr, out_d)
        em.barrier()
        em.wait("sy", out_tok)
        em.emit(nc, blk, sems)
    return nc


_FUSED_NC = None


def kernel_fused(**inputs):
    global _FUSED_NC
    w = host_prep(inputs)
    u = np.asarray(inputs["L0_attn_u"])
    cores = list(range(NCORES))
    if _FUSED_NC is None:
        _FUSED_NC = build_fused_program()
    in_maps = []
    for c in cores:
        m = dict(w)
        m["u_d"] = np.ascontiguousarray(u[2 * c:2 * c + 2].reshape(T, D))
        in_maps.append(m)
    res = run_bass_kernel_spmd(_FUSED_NC, in_maps, core_ids=cores)
    out = np.stack([np.asarray(res.results[c]["x2"]).reshape(2, SEQ, D)
                    for c in cores])
    return out.reshape(16, SEQ, D)



_S0 = None
_S1 = None
_S2 = None
_S3 = None
_S4 = None
_S5 = None


def _programs():
    global _S0, _S1, _S2, _S3, _S4, _S5
    if _S0 is None:
        _S0 = build_s0_program()
        _S1 = build_s1_program()
        _S2 = build_s2_program()
        _S3 = build_s3_program()
        _S4 = build_s4_program()
        _S5 = build_s5_program()
    return _S0, _S1, _S2, _S3, _S4, _S5


FUSED = True


def kernel(**inputs):
    if FUSED:
        return kernel_fused(**inputs)
    w = host_prep(inputs)
    u = np.asarray(inputs["L0_attn_u"])
    cores = list(range(NCORES))
    s0, s1, s2, s3, s4, s5 = _programs()

    u_c = [np.ascontiguousarray(u[2 * c:2 * c + 2].reshape(T, D)) for c in cores]

    r0 = run_bass_kernel_spmd(s0, [dict(u=u_c[c]) for c in cores], core_ids=cores)
    xn = [r0.results[c]["xn_fm"] for c in cores]

    r1 = run_bass_kernel_spmd(s1, [dict(
        xn_fm=xn[c], W_inT=w["W_inT"], conv_w=w["conv_w"], conv_b=w["conv_b"])
        for c in cores], core_ids=cores)

    r2 = run_bass_kernel_spmd(s2, [dict(
        dt_bias=w["dt_bias"], A_log=w["A_log"], Dh=w["Dh"],
        norm_weight=w["norm_weight"], W_outT=w["W_outT"],
        xfm_scr=r1.results[c]["xfm_scr"], bfm_scr=r1.results[c]["bfm_scr"],
        cfm_scr=r1.results[c]["cfm_scr"], dtr_scr=r1.results[c]["dtr_scr"],
        z_scr=r1.results[c]["z_scr"]) for c in cores], core_ids=cores)

    r3 = run_bass_kernel_spmd(s3, [dict(
        xn_fm=xn[c], Wg=w["Wg"], bg=w["bg"], We=w["We"], be=w["be"],
        u_d=u_c[c], s_scr=r2.results[c]["s_scr"]) for c in cores], core_ids=cores)

    r4 = run_bass_kernel_spmd(s4, [dict(
        WqT=w["WqT"], WkT=w["WkT"], WvT=w["WvT"], bq=w["bq"], bk=w["bk"],
        bv=w["bv"], proj=w["proj"], xn1_scr=r3.results[c]["xn1_scr"])
        for c in cores], core_ids=cores)

    in5 = dict(W_QT=w["W_QT"], W_KT=w["W_KT"], W_invT=w["W_invT"], rw1T=w["rw1T"],
               W_OT=w["W_OT"], VfT=w["VfT"], WfT=w["WfT"], ViT=w["ViT"],
               WiT=w["WiT"], Uf=w["Uf"], Ui=w["Ui"], b_fwd=w["b_fwd"],
               b_inv=w["b_inv"], rw2T=w["rw2T"], rb1=w["rb1"], rb2=w["rb2"],
               expert_bias=w["expert_bias"], alpha_bi=w["alpha_bi"])
    r5 = run_bass_kernel_spmd(s5, [dict(
        xn1_scr=r3.results[c]["xn1_scr"],
        x1_scr=r3.results[c]["x1_scr"], s1_scr=r4.results[c]["s1_scr"], **in5)
        for c in cores], core_ids=cores)

    out = np.stack([np.asarray(r5.results[c]["x2"]).reshape(2, SEQ, D)
                    for c in cores])
    return out.reshape(16, SEQ, D)



def dump_fm(em, sbuf_fm, dram_fm, nchunks, ncols, q="sy"):
    dst = dram_fm.rearrange("(c p) t -> p c t", p=128)
    return em.dma(q, lambda e: e.dma_start(out=dst, in_=sbuf_fm[:, 0:nchunks, 0:ncols]))


def build_identity_f32(nc, em, bufs):
    identf = bufs["ident_f32"]
    em.op("gp", lambda gp: gp.memset(identf[:], 1.0))
    return em.op("gp", lambda gp: gp.affine_select(
        out=identf[:], in_=identf[:], pattern=[[-1, 128]],
        compare_op=ALU.is_equal, fill=0.0, base=0, channel_multiplier=1))


def mk_s2_bufs(nc, stx, bufs, alloc=None):
    def sb(name, shape, dtype):
        if alloc is not None:
            return alloc(shape, dtype)
        return stx.enter_context(nc.sbuf_tensor(name, shape, dtype))
    bufs["ident_f32"] = sb("ident_f32", [128, 128], F32)
    regA = sb("s2_regA", [128, 8192], BF16)
    bufs["s2_dtr"] = carve(regA, 0, [16, T], F32)
    bufs["s2_carry"] = carve(regA, 0, [128, 2, 16, 128], F32)
    region = sb("s2_region", [128, 16384], BF16)
    bufs["s2_dt"] = carve(region, 0, [16, T], F32)
    bufs["s2_t16"] = carve(region, 8192, [16, T], F32)
    bufs["s2_wo"] = carve(region, 0, [128, 16, 1024], BF16)
    bufs["s2_acs"] = bufs["s2_dtr"]
    bufs["s2_bcol"] = sb("s2_bcol", [16, 4], F32)
    bufs["zcol16"] = sb("zcol16", [16, 1], F32)
    bufs["s2_cd16"] = sb("s2_cd16", [16, 8], F32)
    bufs["s2_decfm"] = sb("s2_decfm", [16, 256], F32)
    bufs["s2_dtT"] = sb("s2_dtT", [128, 32, 16], F32)
    bufs["s2_nacsT"] = sb("s2_nacsT", [128, 32, 16], F32)
    bufs["s2_eacsT"] = sb("s2_eacsT", [128, 32, 16], F32)
    bufs["s2_sfacT"] = sb("s2_sfacT", [128, 32, 16], F32)
    bufs["s2_cdbc"] = sb("s2_cdbc", [128, 2, 128], F32)
    bufs["s2_dbc"] = sb("s2_dbc", [128, 16], F32)
    bufs["s2_nwbc"] = sb("s2_nwbc", [128, 2048], BF16)
    bufs["s2_xch"] = sb("s2_xch", [128, 2, 16, 256], BF16)
    bufs["s2_xtm"] = sb("s2_xtm", [128, 2, 2, 2048], BF16)
    bufs["s2_bc"] = sb("s2_bc", [128, 2, 256], BF16)
    bufs["s2_cc"] = sb("s2_cc", [128, 2, 256], BF16)
    bufs["s2_btm"] = sb("s2_btm", [128, 2, 2, 128], BF16)
    bufs["s2_bsa"] = sb("s2_bsa", [128, 2, 16, 128], BF16)
    bufs["s2_cbt"] = sb("s2_cbt", [128, 2, 2, 256], BF16)
    bufs["s2_seg1"] = sb("s2_seg1", [128, 2, 8, 256], F32)
    bufs["s2_mta"] = sb("s2_mta", [128, 2, 2, 16, 256], BF16)
    bufs["s2_di"] = sb("s2_di", [128, 16, 128], BF16)
    bufs["s2_carrybf"] = sb("s2_carrybf", [128, 2, 16, 128], BF16)
    bufs["s2_yo"] = sb("s2_yo", [128, 2, 2, 128], F32)
    bufs["s2_ytm"] = sb("s2_ytm", [128, 2, 2048], BF16)
    bufs["s2_ztm"] = sb("s2_ztm", [128, 2, 2048], BF16)
    bufs["s2_yfm"] = sb("s2_yfm", [128, 16, 256], BF16)
    bufs["s2_rcol"] = sb("s2_rcol", [128, 8], F32)
    bufs["s2_sout"] = sb("s2_sout", [128, 2, 1024], BF16)


def build_s2_program():
    nc = bass.Bass("TRN2", target_bir_lowering=False)
    wd = {
        "dt_bias": nc.dram_tensor("dt_bias", [16, 1], F32, kind="ExternalInput").ap(),
        "A_log": nc.dram_tensor("A_log", [16, 1], F32, kind="ExternalInput").ap(),
        "Dh": nc.dram_tensor("Dh", [1, 16], F32, kind="ExternalInput").ap(),
        "norm_weight": nc.dram_tensor("norm_weight", [1, DIN], F32, kind="ExternalInput").ap(),
        "W_outT": nc.dram_tensor("W_outT", [DIN, D], F32, kind="ExternalInput").ap(),
    }
    scr = {
        "xfm_scr": nc.dram_tensor("xfm_scr", [DIN, T], BF16, kind="ExternalInput").ap(),
        "bfm_scr": nc.dram_tensor("bfm_scr", [128, T], BF16, kind="ExternalInput").ap(),
        "cfm_scr": nc.dram_tensor("cfm_scr", [128, T], BF16, kind="ExternalInput").ap(),
        "dtr_scr": nc.dram_tensor("dtr_scr", [16, T], F32, kind="ExternalInput").ap(),
        "z_scr": nc.dram_tensor("z_scr", [T, DIN], BF16, kind="ExternalInput").ap(),
        "acs_scr": nc.dram_tensor("acs_scr", [16, T], F32, kind="Internal").ap(),
        "cdec_scr": nc.dram_tensor("cdec_scr", [2, 16, 8], F32, kind="Internal").ap(),
        "s_scr": nc.dram_tensor("s_scr", [T, D], BF16, kind="ExternalOutput").ap(),
    }
    em = Em()
    with contextlib.ExitStack() as stx:
        bufs = {}
        def sb(name, shape, dtype):
            return stx.enter_context(nc.sbuf_tensor(name, shape, dtype))
        bufs["ident"] = sb("ident", [128, 128], BF16)
        mk_s2_bufs(nc, stx, bufs)
        mk_psum_pool(nc, stx, bufs)
        sems = em.make_sems(nc, stx)
        blk = stx.enter_context(nc.Block())

        bufs["ident_tok"] = build_identity(nc, em, bufs)
        bufs["identf_tok"] = build_identity_f32(nc, em, bufs)
        stage_s2(nc, em, bufs, wd, scr)
        em.fence("sy")
        em.emit(nc, blk, sems)
    return nc


def load_fm(em, sbuf_fm, dram_fm, nchunks, ncols, q="sy"):
    src = dram_fm.rearrange("(c p) t -> p c t", p=128)
    return em.dma(q, lambda e: e.dma_start(out=sbuf_fm[:, 0:nchunks, 0:ncols], in_=src))


def mk_s1_bufs(nc, stx, bufs, alloc=None):
    def sb(name, shape, dtype):
        if alloc is not None:
            return alloc(shape, dtype)
        return stx.enter_context(nc.sbuf_tensor(name, shape, dtype))
    bufs["s1_w"] = sb("s1_w", [128, 8, 2304], BF16)
    bufs["s1_wdt"] = sb("s1_wdt", [128, 8, 16], BF16)
    bufs["s1_z"] = sb("s1_z", [128, 2, 2048], BF16)
    bufs["s1_raw"] = sb("s1_raw", [128, 1, 4102], F32)
    bufs["s1_acc"] = sb("s1_acc", [128, 1, 4096], F32)
    bufs["s1_act"] = sb("s1_act", [128, 1, 4096], BF16)
    bufs["s1_cw"] = sb("s1_cw", [128, 2, 4], F32)
    bufs["s1_cb"] = sb("s1_cb", [128, 2, 1], F32)
    bufs["s1_dtr"] = sb("s1_dtr", [16, T], F32)


def build_s1_program():
    nc = bass.Bass("TRN2", target_bir_lowering=False)
    xn_d = nc.dram_tensor("xn_fm", [D, T], BF16, kind="ExternalInput").ap()
    wd = {
        "W_inT": nc.dram_tensor("W_inT", [1024, 4368], F32, kind="ExternalInput").ap(),
        "conv_w": nc.dram_tensor("conv_w", [NJC, 128, 4], F32, kind="ExternalInput").ap(),
        "conv_b": nc.dram_tensor("conv_b", [NJC, 128, 1], F32, kind="ExternalInput").ap(),
    }
    scr = {
        "z_scr": nc.dram_tensor("z_scr", [T, DIN], BF16, kind="ExternalOutput").ap(),
        "xfm_scr": nc.dram_tensor("xfm_scr", [DIN, T], BF16, kind="ExternalOutput").ap(),
        "bfm_scr": nc.dram_tensor("bfm_scr", [128, T], BF16, kind="ExternalOutput").ap(),
        "cfm_scr": nc.dram_tensor("cfm_scr", [128, T], BF16, kind="ExternalOutput").ap(),
        "dtr_scr": nc.dram_tensor("dtr_scr", [16, T], F32, kind="ExternalOutput").ap(),
    }
    em = Em()
    with contextlib.ExitStack() as stx:
        bufs = {}
        def sb(name, shape, dtype):
            return stx.enter_context(nc.sbuf_tensor(name, shape, dtype))
        bufs["xn_fm"] = sb("xn_fm_sb", [128, 8, T], BF16)
        mk_s1_bufs(nc, stx, bufs)
        mk_psum_pool(nc, stx, bufs)
        sems = em.make_sems(nc, stx)
        blk = stx.enter_context(nc.Block())

        xn_tok = load_fm(em, bufs["xn_fm"], xn_d, 8, T)
        for e in ("te", "ve", "sc", "gp"):
            em.wait(e, xn_tok)
        stage_s1(nc, em, bufs, wd, scr, xn_tok)
        em.fence("sy")
        em.emit(nc, blk, sems)
    return nc


def mk_s3_bufs(nc, stx, bufs, alloc=None):
    def sb(name, shape, dtype):
        if alloc is not None:
            return alloc(shape, dtype)
        return stx.enter_context(nc.sbuf_tensor(name, shape, dtype))
    bufs["s3_wg"] = sb("s3_wg", [128, 8, 8], BF16)
    bufs["s3_bgbc"] = sb("s3_bgbc", [128, 8], F32)
    bufs["s3_be"] = sb("s3_be", [8, 1024], BF16)
    bufs["s3_gcl"] = sb("s3_gcl", [128, 4, 32], F32)
    bufs["s3_gsc"] = sb("s3_gsc", [128, 32, 8], F32)
    bufs["s3_gax1"] = sb("s3_gax1", [128, 32, 8], F32)
    bufs["s3_gax2"] = sb("s3_gax2", [128, 32, 8], F32)
    bufs["s3_gw"] = bufs["s3_gax2"]
    bufs["s3_gwb"] = sb("s3_gwb", [128, 32, 8], BF16)
    bufs["s3_wt"] = sb("s3_wt", [8, 32, 128], BF16)
    bufs["s3_we"] = sb("s3_we", [128, 2, 8, 1024], BF16)
    bufs["s3_acc"] = sb("s3_acc", [128, 16, 1024], F32)
    bufs["s3_u"] = sb("s3_u", [128, 1, 1024], F32)
    bufs["s3_s"] = sb("s3_s", [128, 2, 1024], BF16)
    bufs["s3_g"] = sb("s3_g", [128, 1024], BF16)
    bufs["s3_sg"] = sb("s3_sg", [128, 1024], F32)
    bufs["s3_x1"] = sb("s3_x1", [128, 2, 1024], F32)
    bufs["s3_xn1"] = sb("s3_xn1", [128, 2, 1024], BF16)


def build_s3_program():
    nc = bass.Bass("TRN2", target_bir_lowering=False)
    xn_d = nc.dram_tensor("xn_fm", [D, T], BF16, kind="ExternalInput").ap()
    wd = {
        "Wg": nc.dram_tensor("Wg", [1024, 8], F32, kind="ExternalInput").ap(),
        "bg": nc.dram_tensor("bg", [1, 8], F32, kind="ExternalInput").ap(),
        "We": nc.dram_tensor("We", [8, 1024, 1024], F32, kind="ExternalInput").ap(),
        "be": nc.dram_tensor("be", [8, 1024], F32, kind="ExternalInput").ap(),
        "u_d": nc.dram_tensor("u_d", [T, D], F32, kind="ExternalInput").ap(),
    }
    scr = {
        "s_scr": nc.dram_tensor("s_scr", [T, D], BF16, kind="ExternalInput").ap(),
        "x1_scr": nc.dram_tensor("x1_scr", [T, D], F32, kind="ExternalOutput").ap(),
        "xn1_scr": nc.dram_tensor("xn1_scr", [T, D], BF16, kind="ExternalOutput").ap(),
    }
    em = Em()
    with contextlib.ExitStack() as stx:
        bufs = {}
        def sb(name, shape, dtype):
            return stx.enter_context(nc.sbuf_tensor(name, shape, dtype))
        bufs["ident"] = sb("ident", [128, 128], BF16)
        bufs["xn_fm"] = sb("xn_fm_sb", [128, 8, T], BF16)
        bufs["ln_col"] = sb("ln_col", [128, 16], F32)
        bufs["s1_acc"] = sb("s1_acc", [128, 1024], F32)
        mk_s3_bufs(nc, stx, bufs)
        mk_psum_pool(nc, stx, bufs)
        sems = em.make_sems(nc, stx)
        blk = stx.enter_context(nc.Block())

        bufs["ident_tok"] = build_identity(nc, em, bufs)
        xn_tok = load_fm(em, bufs["xn_fm"], xn_d, 8, T)
        stage_s3(nc, em, bufs, wd, scr, xn_tok)
        em.fence("sy")
        em.emit(nc, blk, sems)
    return nc


def mk_s4_bufs(nc, stx, bufs, alloc=None):
    def sb(name, shape, dtype):
        if alloc is not None:
            return alloc(shape, dtype)
        return stx.enter_context(nc.sbuf_tensor(name, shape, dtype))
    bufs["s4_xn1"] = sb("s4_xn1", [128, 8, 2048], BF16)
    bufs["s4_tm"] = sb("s4_tm", [128, 2, 1024], BF16)
    bufs["s4_w"] = sb("s4_w", [128, 2, 3, 8, 128], BF16)
    bufs["s4_b"] = sb("s4_b", [128, 3, 8], F32)
    bufs["s4_q"] = sb("s4_q", [128, 2, 2048], BF16)
    bufs["s4_k"] = sb("s4_k", [128, 2, 2048], BF16)
    bufs["s4_vf"] = sb("s4_vf", [128, 2, 2048], BF16)
    bufs["s4_vt"] = sb("s4_vt", [128, 2, 16, 130], BF16)
    bufs["s4_pt"] = sb("s4_pt", [128, 2, 2, 16, 512], BF16)
    bufs["s4_zfm"] = sb("s4_zfm", [128, 8, 2048], BF16)
    bufs["s4_proj"] = sb("s4_proj", [128, 8, 1024], BF16)
    bufs["s4_ones"] = sb("s4_ones", [1, 64], BF16)
    bufs["s4_r"] = sb("s4_r", [1, 512], F32)
    bufs["s4_rbf"] = sb("s4_rbf", [1, 2, 512], BF16)
    bufs["s4_bc"] = sb("s4_bc", [64, 2, 512], F32)
    bufs["s4_s1"] = sb("s4_s1", [128, 2, 1024], BF16)


def build_s4_program():
    nc = bass.Bass("TRN2", target_bir_lowering=False)
    wd = {
        "WqT": nc.dram_tensor("WqT", [1024, 1024], F32, kind="ExternalInput").ap(),
        "WkT": nc.dram_tensor("WkT", [1024, 1024], F32, kind="ExternalInput").ap(),
        "WvT": nc.dram_tensor("WvT", [1024, 1024], F32, kind="ExternalInput").ap(),
        "bq": nc.dram_tensor("bq", [1024, 1], F32, kind="ExternalInput").ap(),
        "bk": nc.dram_tensor("bk", [1024, 1], F32, kind="ExternalInput").ap(),
        "bv": nc.dram_tensor("bv", [1024, 1], F32, kind="ExternalInput").ap(),
        "proj": nc.dram_tensor("proj", [1024, 1024], F32, kind="ExternalInput").ap(),
    }
    scr = {
        "xn1_scr": nc.dram_tensor("xn1_scr", [T, D], BF16, kind="ExternalInput").ap(),
        "s1_scr": nc.dram_tensor("s1_scr", [T, D], BF16, kind="ExternalOutput").ap(),
    }
    em = Em()
    with contextlib.ExitStack() as stx:
        bufs = {}
        def sb(name, shape, dtype):
            return stx.enter_context(nc.sbuf_tensor(name, shape, dtype))
        bufs["ident"] = sb("ident", [128, 128], BF16)
        mk_s4_bufs(nc, stx, bufs)
        mk_psum_pool(nc, stx, bufs)
        sems = em.make_sems(nc, stx)
        blk = stx.enter_context(nc.Block())

        bufs["ident_tok"] = build_identity(nc, em, bufs)
        stage_s4(nc, em, bufs, wd, scr)
        em.fence("sy")
        em.emit(nc, blk, sems)
    return nc


def mk_s5_bufs(nc, stx, bufs, alloc=None):
    def sb(name, shape, dtype):
        if alloc is not None:
            return alloc(shape, dtype)
        return stx.enter_context(nc.sbuf_tensor(name, shape, dtype))
    bufs["s5_wq"] = sb("s5_wq", [128, 8, 1024], BF16)
    bufs["s5_wk"] = sb("s5_wk", [128, 8, 1024], BF16)
    bufs["s5_wi"] = sb("s5_wi", [128, 8, 1024], BF16)
    bufs["s5_w1"] = sb("s5_w1", [128, 8, 1024], BF16)
    bufs["s5_wo"] = sb("s5_wo", [128, 8, 1024], BF16)
    bufs["s5_vw"] = sb("s5_vw", [128, 4, 8, 512], BF16)
    bufs["s5_u"] = sb("s5_u", [128, 2, 4, 1024], BF16)
    bufs["s5_bb"] = sb("s5_bb", [8, 2, 1024], BF16)
    bufs["s5_rw2"] = sb("s5_rw2", [128, 8, 8], BF16)
    bufs["s5_rb1"] = sb("s5_rb1", [128, 8], F32)
    bufs["s5_ebc"] = sb("s5_ebc", [128, 2, 8], F32)
    bufs["s5_al"] = sb("s5_al", [128, 1], F32)
    bufs["s5_invn"] = sb("s5_invn", [128, 32], F32)
    reg5 = sb("s5_reg5", [128, 4096], BF16)
    bufs["s5_xnb"] = carve(reg5, 0, [128, 8, 512], BF16)
    bufs["s5_wexp"] = carve(reg5, 0, [128, 4, 512], BF16)
    bufs["s5_prodw"] = carve(reg5, 2048, [128, 4, 512], BF16)
    bufs["s5_qb"] = sb("s5_qb", [128, 8, 512], BF16)
    bufs["s5_kb"] = sb("s5_kb", [128, 8, 512], BF16)
    bufs["s5_iqb"] = sb("s5_iqb", [128, 8, 512], BF16)
    bufs["s5_ikb"] = bufs["s5_qb"]
    bufs["s5_xv"] = sb("s5_xv", [128, 4, 512], F32)
    bufs["s5_yw"] = sb("s5_yw", [128, 4, 512], F32)
    bufs["s5_cum"] = bufs["s5_yw"]
    bufs["s5_carry"] = sb("s5_carry", [128, 2, 4], F32)
    bufs["s5_hb"] = sb("s5_hb", [128, 8, 512], BF16)
    bufs["s5_lg"] = sb("s5_lg", [128, 4, 8], F32)
    bufs["s5_lcol"] = sb("s5_lcol", [128, 8], F32)
    bufs["s5_gwb"] = sb("s5_gwb", [128, 8], BF16)
    bufs["s5_wt"] = sb("s5_wt", [8, 4, 128], BF16)

    bufs["s5_osum"] = sb("s5_osum", [128, 8, 512], BF16)
    reg6 = sb("s5_reg6", [128, 2048], BF16)
    bufs["s4_tm"] = carve(reg6, 0, [128, 2, 1024], BF16)
    bufs["s5_x1t"] = carve(reg6, 0, [128, 1, 1024], F32)
    bufs["s5_s1t"] = sb("s5_s1t", [128, 1, 1024], BF16)
    bufs["s5_g1"] = sb("s5_g1", [128, 1024], BF16)

    bufs["zcol128"] = sb("zcol128", [128, 1], F32)


def build_s5_program():
    nc = bass.Bass("TRN2", target_bir_lowering=False)
    wd = {}
    for name in ("W_QT", "W_KT", "W_invT", "rw1T", "W_OT"):
        wd[name] = nc.dram_tensor(name, [1024, 1024], F32, kind="ExternalInput").ap()
    for name in ("VfT", "WfT", "ViT", "WiT"):
        wd[name] = nc.dram_tensor(name, [1024, 512], F32, kind="ExternalInput").ap()
    for name in ("Uf", "Ui"):
        wd[name] = nc.dram_tensor(name, [512, 1024], F32, kind="ExternalInput").ap()
    for name in ("b_fwd", "b_inv"):
        wd[name] = nc.dram_tensor(name, [8, 1024], F32, kind="ExternalInput").ap()
    wd["rw2T"] = nc.dram_tensor("rw2T", [1024, 8], F32, kind="ExternalInput").ap()
    wd["rb1"] = nc.dram_tensor("rb1", [1024, 1], F32, kind="ExternalInput").ap()
    wd["rb2"] = nc.dram_tensor("rb2", [1, 8], F32, kind="ExternalInput").ap()
    wd["expert_bias"] = nc.dram_tensor("expert_bias", [1, 8], F32, kind="ExternalInput").ap()
    wd["alpha_bi"] = nc.dram_tensor("alpha_bi", [1, 1], F32, kind="ExternalInput").ap()
    scr = {
        "xn1_scr": nc.dram_tensor("xn1_scr", [T, D], BF16, kind="ExternalInput").ap(),
        "x1_scr": nc.dram_tensor("x1_scr", [T, D], F32, kind="ExternalInput").ap(),
        "s1_scr": nc.dram_tensor("s1_scr", [T, D], BF16, kind="ExternalInput").ap(),
        "wgt_scr": nc.dram_tensor("wgt_scr", [2, 8, T], BF16, kind="Internal").ap(),
    }
    out_d = nc.dram_tensor("x2", [T, D], F32, kind="ExternalOutput").ap()
    em = Em()
    with contextlib.ExitStack() as stx:
        bufs = {}
        def sb(name, shape, dtype):
            return stx.enter_context(nc.sbuf_tensor(name, shape, dtype))
        bufs["ident"] = sb("ident", [128, 128], BF16)
        mk_s5_bufs(nc, stx, bufs)
        mk_psum_pool(nc, stx, bufs)
        sems = em.make_sems(nc, stx)
        blk = stx.enter_context(nc.Block())

        bufs["ident_tok"] = build_identity(nc, em, bufs)
        stage_s5(nc, em, bufs, wd, scr, out_d)
        em.fence("sy")
        em.emit(nc, blk, sems)
    return nc


def build_s0_program():
    nc = bass.Bass("TRN2", target_bir_lowering=False)
    u = nc.dram_tensor("u", [T, D], F32, kind="ExternalInput").ap()
    xn_fm_d = nc.dram_tensor("xn_fm", [D, T], BF16, kind="ExternalOutput").ap()
    em = Em()
    with contextlib.ExitStack() as stx:
        bufs = {}
        def sb(name, shape, dtype):
            h = stx.enter_context(nc.sbuf_tensor(name, shape, dtype))
            return h
        bufs["ident"] = sb("ident", [128, 128], BF16)
        bufs["ln_x"] = sb("ln_x", [128, 2, D], F32)
        bufs["ln_sq"] = sb("ln_sq", [128, D], F32)
        bufs["ln_col"] = sb("ln_col", [128, 16], F32)
        bufs["ln_xn"] = sb("ln_xn", [128, 2, D], BF16)
        xn_fm = sb("xn_fm_sb", [128, 8, T], BF16)
        mk_psum_pool(nc, stx, bufs)
        sems = em.make_sems(nc, stx)
        blk = stx.enter_context(nc.Block())

        bufs["ident_tok"] = build_identity(nc, em, bufs)
        t_fm = stage_ln(nc, em, bufs, u, xn_fm)
        em.fence("sy")
        out_tok = dump_fm(em, xn_fm, xn_fm_d, 8, T)
        em.wait("sy", out_tok)
        em.emit(nc, blk, sems)
    return nc


# version 08
# speedup vs baseline: 1.0186x; 1.0032x over previous
import contextlib
import numpy as np
import concourse.bass as bass
import concourse.mybir as mybir
from concourse.bass_utils import run_bass_kernel_spmd

F32 = mybir.dt.float32
BF16 = mybir.dt.bfloat16
AF = mybir.ActivationFunctionType
ALU = mybir.AluOpType
AX = mybir.AxisListType

NCORES = 8
SEQ = 2048
T = 4096
D = 1024
DIN = 2048
DSTATE = 128
NH = 16
HDIM = 128
CHUNK = 256
NCH = SEQ // CHUNK
CONVD = DIN + 2 * DSTATE
NJC = CONVD // 128
EPS = 1e-5

ENGS = ("te", "ve", "sc", "gp", "sy")
S2_PIPE = True
S2_ILV = True
DMAQS = ("gp", "sy", "sc")


NDMASEM = 8


class Em:

    def __init__(self):
        self.ops = {e: [] for e in ENGS}
        self.n = {e: 0 for e in ENGS}
        self.nd = {}
        self.rr = {q: 0 for q in DMAQS}
        self.last_wait = {}

    def op(self, eng, fn):
        if eng in ("ve", "sc", "gp") and self.n[eng] > 0:
            self.wait(eng, (eng, self.n[eng]))
        self.n[eng] += 1
        self.ops[eng].append(("op", fn))
        return (eng, self.n[eng])

    def dma(self, q, fn, slot=None):
        if q == "gp" and self.n["gp"] > 0:
            self.wait(q, ("gp", self.n["gp"]))
        if slot is None:
            slot = self.rr[q]
            self.rr[q] = (self.rr[q] + 1) % NDMASEM
        key = f"{q}{slot}_dma"
        prev = self.nd.get(key, 0)
        self.wait(q, (key, prev))
        self.nd[key] = prev + 16
        self.ops[q].append(("dma", fn, key))
        return (key, prev + 16)

    def wait(self, eng, tok):
        if tok is None:
            return
        key, cnt = tok
        if cnt <= 0:
            return
        if self.last_wait.get((eng, key), 0) >= cnt:
            return
        self.last_wait[(eng, key)] = cnt
        self.ops[eng].append(("wait", key, cnt))

    def fence(self, eng):
        for e in ENGS:
            if e != eng and self.n[e]:
                self.wait(eng, (e, self.n[e]))
        for key, cnt in self.nd.items():
            self.wait(eng, (key, cnt))

    def barrier(self):
        for e in ENGS:
            self.fence(e)

    def tok(self, eng):
        return (eng, self.n[eng])

    def emit(self, nc, blk, sems):
        def run_stream(eng_obj, eng_name):
            for item in self.ops[eng_name]:
                if item[0] == "op":
                    ins = item[1](eng_obj)
                    ins.then_inc(sems[eng_name], 1)
                elif item[0] == "dma":
                    ins = item[1](eng_obj)
                    ins.then_inc(sems[item[2]], 16)
                else:
                    eng_obj.wait_ge(sems[item[1]], item[2])
        blk.tensor(lambda e: run_stream(e, "te"))
        blk.vector(lambda e: run_stream(e, "ve"))
        blk.scalar(lambda e: run_stream(e, "sc"))
        blk.gpsimd(lambda e: run_stream(e, "gp"))
        blk.sync(lambda e: run_stream(e, "sy"))

    def make_sems(self, nc, stx):
        sems = {}
        for e in ENGS:
            sems[e] = stx.enter_context(nc.semaphore("sem_" + e))
        for q in DMAQS:
            for k in range(NDMASEM):
                sems[f"{q}{k}_dma"] = stx.enter_context(
                    nc.semaphore(f"sem_{q}{k}_dma"))
        return sems


def host_prep(inputs):
    I = inputs
    w = {}
    w["W_inT"] = np.ascontiguousarray(I["L0_attn_W_in"].T)
    w["conv_w"] = np.ascontiguousarray(I["L0_attn_conv_w"][:, 0, :].reshape(NJC, 128, 4))
    w["conv_b"] = np.ascontiguousarray(I["L0_attn_conv_b"].reshape(NJC, 128, 1))
    w["dt_bias"] = np.ascontiguousarray(I["L0_attn_dt_bias"].reshape(16, 1))
    w["A_log"] = np.ascontiguousarray(I["L0_attn_A_log"].reshape(16, 1))
    w["Dh"] = np.ascontiguousarray(I["L0_attn_D"].reshape(1, 16))
    w["norm_weight"] = np.ascontiguousarray(I["L0_attn_norm_weight"].reshape(1, DIN))
    w["W_outT"] = np.ascontiguousarray(I["L0_attn_W_out"].T)
    w["Wg"] = np.ascontiguousarray(I["L0_ffn_Wg"])
    w["bg"] = np.ascontiguousarray(I["L0_ffn_bg"].reshape(1, 8))
    w["We"] = np.ascontiguousarray(I["L0_ffn_We"])
    w["be"] = np.ascontiguousarray(I["L0_ffn_be"])
    w["WqT"] = np.ascontiguousarray(I["L1_attn_Wq"].reshape(1024, 1024).T)
    w["WkT"] = np.ascontiguousarray(I["L1_attn_Wk"].reshape(1024, 1024).T)
    w["WvT"] = np.ascontiguousarray(I["L1_attn_Wv"].reshape(1024, 1024).T)
    w["bq"] = np.ascontiguousarray(I["L1_attn_bq"].reshape(1024, 1))
    w["bk"] = np.ascontiguousarray(I["L1_attn_bk"].reshape(1024, 1))
    w["bv"] = np.ascontiguousarray(I["L1_attn_bv"].reshape(1024, 1))
    w["proj"] = np.ascontiguousarray(I["L1_attn_proj"])
    w["W_QT"] = np.ascontiguousarray(I["L1_ffn_W_Q"].T)
    w["W_KT"] = np.ascontiguousarray(I["L1_ffn_W_K"].T)
    w["W_OT"] = np.ascontiguousarray(I["L1_ffn_W_O"].T)
    w["W_invT"] = np.ascontiguousarray(I["L1_ffn_W_inv"].T)
    def flatT(a):
        return np.ascontiguousarray(a.transpose(1, 0, 2).reshape(1024, 512))
    def flat(a):
        return np.ascontiguousarray(a.transpose(0, 2, 1).reshape(512, 1024))
    w["VfT"] = flatT(I["L1_ffn_V_fwd"])
    w["WfT"] = flatT(I["L1_ffn_W_fwd"])
    w["ViT"] = flatT(I["L1_ffn_W_inv_exp"])
    w["WiT"] = flatT(I["L1_ffn_V_inv"])
    w["Uf"] = flat(I["L1_ffn_U_fwd"])
    w["Ui"] = flat(I["L1_ffn_U_inv"])
    w["b_fwd"] = np.ascontiguousarray(I["L1_ffn_b_fwd"])
    w["b_inv"] = np.ascontiguousarray(I["L1_ffn_b_inv"])
    w["rw1T"] = np.ascontiguousarray(I["L1_ffn_router_w1"].T)
    w["rb1"] = np.ascontiguousarray(I["L1_ffn_router_b1"].reshape(1024, 1))
    w["rw2T"] = np.ascontiguousarray(I["L1_ffn_router_w2"].T)
    w["rb2"] = np.ascontiguousarray(I["L1_ffn_router_b2"].reshape(1, 8))
    w["expert_bias"] = np.ascontiguousarray(I["L1_ffn_expert_bias"].reshape(1, 8))
    w["alpha_bi"] = np.ascontiguousarray(I["L1_ffn_alpha_bi"].reshape(1, 1))
    return w


class Arena:

    def __init__(self, ar_handle, n_units):
        self.ar = ar_handle
        self.n_units = n_units
        self.off = 0
        self.peak = 0

    def alloc(self, shape, dtype):
        p = shape[0]
        nfree = 1
        for d in shape[1:]:
            nfree *= d
        units = nfree * (2 if dtype == F32 else 1)
        units = (units + 63) // 64 * 64
        assert self.off + units <= self.n_units, (
            f"arena overflow: need {units} at {self.off}, cap {self.n_units}")
        base = self.ar[:, self.off:self.off + units]
        self.off += units
        self.peak = max(self.peak, self.off)
        if dtype != BF16:
            base = base.bitcast(dtype)
        strides = []
        st = 1
        for d in reversed(shape[1:]):
            strides.append([st, d])
            st *= d
        strides.reverse()
        pdim = [base.ap[0][0], p]
        return bass.AP(base.tensor, base.offset, [pdim] + strides)

    def mark(self):
        return self.off

    def release(self, m):
        self.off = m


def mk_psum_pool(nc, stx, bufs):
    ps2 = [stx.enter_context(nc.psum_tensor(f"ps2_{i}", [128, 1024], F32))
           for i in range(3)]
    bufs["ps2"] = [t[:] for t in ps2]
    bufs["ps"] = [ps2[i // 2][:, 512 * (i % 2):512 * (i % 2) + 512] for i in range(6)]
    bufs["ps_bf"] = [stx.enter_context(nc.psum_tensor(f"pbf{i}", [128, 1024], BF16))
                     for i in range(2)]


def carve(region, unit_off, shape, dtype):
    ap2 = region[:] if not isinstance(region, bass.AP) else region
    p = shape[0]
    nfree = 1
    for d in shape[1:]:
        nfree *= d
    units = nfree * (2 if dtype == F32 else 1)
    base = bass.AP(ap2.tensor, ap2.offset + unit_off, [list(ap2.ap[0]), [1, units]])
    if dtype != BF16:
        base = base.bitcast(dtype)
    strides = []
    st = 1
    for d in reversed(shape[1:]):
        strides.append([st, d])
        st *= d
    strides.reverse()
    return bass.AP(base.tensor, base.offset, [[base.ap[0][0], p]] + strides)


def flatT_note():
    pass



def build_identity(nc, em, bufs):
    ident = bufs["ident"]

    em.op("gp", lambda gp: gp.memset(ident[:], 1.0))
    return em.op("gp", lambda gp: gp.affine_select(
        out=ident[:], in_=ident[:], pattern=[[-1, 128]],
        compare_op=ALU.is_equal, fill=0.0, base=0, channel_multiplier=1))


def stage_ln(nc, em, bufs, x_dram, xn_fm, key="ln"):
    xt = bufs[key + "_x"]
    sq = bufs[key + "_sq"]
    col = bufs[key + "_col"]
    xn_tm = bufs[key + "_xn"]
    pbf = bufs["ps_bf"]
    ident = bufs["ident"]
    id_tok = bufs["ident_tok"]

    NTT = T // 128
    last_use = {}
    t_ev = None
    for i in range(NTT):
        sl = i % 2
        cb = 8 * sl

        em.wait("sy", last_use.get(("x", sl)))
        ld = em.dma("sy", lambda sy, i=i, sl=sl: sy.dma_start(
            out=xt[:, sl], in_=x_dram[128 * i:128 * (i + 1), :]))
        em.wait("sc", ld)
        em.wait("sc", last_use.get(("col", sl)))
        t_sq = em.op("sc", lambda sc, sl=sl, cb=cb: sc.activation(
            out=sq[:], in_=xt[:, sl], func=AF.Square, accum_out=col[:, cb:cb + 1]))
        em.wait("ve", ld)
        em.wait("ve", last_use.get(("col", sl)))
        em.op("ve", lambda ve, sl=sl, cb=cb: ve.reduce_sum(
            col[:, cb + 1:cb + 2], xt[:, sl], axis=AX.X))
        em.wait("ve", t_sq)
        em.op("ve", lambda ve, cb=cb: ve.tensor_scalar_mul(
            col[:, cb + 2:cb + 3], col[:, cb + 1:cb + 2], 1.0 / D))
        em.op("ve", lambda ve, cb=cb: ve.tensor_tensor(
            col[:, cb + 3:cb + 4], col[:, cb + 2:cb + 3], col[:, cb + 2:cb + 3], ALU.mult))
        em.op("ve", lambda ve, cb=cb: ve.tensor_scalar_mul(
            col[:, cb + 4:cb + 5], col[:, cb:cb + 1], 1.0 / D))
        em.op("ve", lambda ve, cb=cb: ve.tensor_sub(
            col[:, cb + 5:cb + 6], col[:, cb + 4:cb + 5], col[:, cb + 3:cb + 4]))
        em.op("ve", lambda ve, cb=cb: ve.tensor_scalar_add(
            col[:, cb + 6:cb + 7], col[:, cb + 5:cb + 6], EPS))
        t_rec = em.op("ve", lambda ve, cb=cb: ve.reciprocal(
            col[:, cb + 7:cb + 8], col[:, cb + 6:cb + 7]))
        em.wait("sc", t_rec)
        t_rstd = em.op("sc", lambda sc, cb=cb: sc.activation(
            out=col[:, cb + 6:cb + 7], in_=col[:, cb + 7:cb + 8], func=AF.Sqrt))
        em.wait("ve", t_rstd)
        em.wait("ve", last_use.get(("xn", sl)))
        t_xn = em.op("ve", lambda ve, sl=sl, cb=cb: ve.tensor_scalar(
            xn_tm[:, sl], xt[:, sl], col[:, cb + 2:cb + 3], col[:, cb + 6:cb + 7],
            ALU.subtract, ALU.mult))
        last_use[("x", sl)] = t_xn
        last_use[("col", sl)] = t_xn
        em.wait("te", t_xn)
        em.wait("te", id_tok)
        pb = pbf[i % 2]
        em.wait("te", last_use.get(("pb", i % 2)))
        def f_tr(te, sl=sl, pb=pb):
            for c in range(8):
                ins = te.transpose(pb[:, 128 * c:128 * (c + 1)],
                                   xn_tm[:, sl, 128 * c:128 * (c + 1)], ident[:])
            return ins
        t_tr = em.op("te", f_tr)
        em.wait("sc", t_tr)
        def f_ev(sc, i=i, pb=pb):
            return sc.activation(
                out=xn_fm[:, :, 128 * i:128 * (i + 1)],
                in_=pb[:].rearrange("p (c f) -> p c f", f=128), func=AF.Copy)
        t_ev = em.op("sc", f_ev)
        last_use[("pb", i % 2)] = t_ev
        last_use[("xn", sl)] = t_ev
    return t_ev


def stage_s1(nc, em, bufs, wd, scr, xn_tok):
    xn = bufs["xn_fm"]
    W = bufs["s1_w"]
    Wdt = bufs["s1_wdt"]
    zbuf = bufs["s1_z"]
    raw = bufs["s1_raw"]
    acc = bufs["s1_acc"]
    act = bufs["s1_act"]
    cw = bufs["s1_cw"]
    cb = bufs["s1_cb"]
    dtr = bufs["s1_dtr"]
    PS = bufs["ps"]

    wz_toks = []
    for kd in range(8):
        wz_toks.append(em.dma("gp", lambda gp, kd=kd: gp.dma_start(
            out=W[:, kd, 0:2048], in_=wd["W_inT"][128 * kd:128 * (kd + 1), 0:2048])))
    wdt_toks = []
    for kd in range(8):
        wdt_toks.append(em.dma("gp", lambda gp, kd=kd: gp.dma_start(
            out=Wdt[:, kd], in_=wd["W_inT"][128 * kd:128 * (kd + 1), 4352:4368])))

    em.wait("te", xn_tok)
    z_done = []
    last_ev = {}
    for ti in range(T // 128):
        sl = ti % 2
        ev_toks = []
        for jn in range(4):
            bank = PS[jn % 2]
            em.wait("te", last_ev.get(("zps", jn % 2)))
            for tok in wz_toks:
                em.wait("te", tok)
            def f_mm(te, ti=ti, jn=jn, bank=bank):
                for kd in range(8):
                    ins = te.matmul(bank[:], xn[:, kd, 128 * ti:128 * (ti + 1)],
                                    W[:, kd, 512 * jn:512 * (jn + 1)],
                                    start=(kd == 0), stop=(kd == 7))
                return ins
            t_mm = em.op("te", f_mm)
            em.wait("sc", t_mm)
            em.wait("sc", last_ev.get(("zbuf", sl)))
            t_ev = em.op("sc", lambda sc, sl=sl, jn=jn, bank=bank: sc.activation(
                out=zbuf[:, sl, 512 * jn:512 * (jn + 1)], in_=bank[:], func=AF.Copy))
            last_ev[("zps", jn % 2)] = t_ev
            ev_toks.append(t_ev)
        for tok in ev_toks:
            em.wait("sy", tok)
        st = em.dma("sy", lambda sy, ti=ti, sl=sl: sy.dma_start(
            out=scr["z_scr"][128 * ti:128 * (ti + 1), :], in_=zbuf[:, sl]))
        last_ev[("zbuf", sl)] = st
        z_done.append(st)

    t_zmm_done = em.tok("te")
    wx_toks = []
    for kd in range(8):
        em.wait("gp", t_zmm_done)
        wx_toks.append(em.dma("gp", lambda gp, kd=kd: gp.dma_start(
            out=W[:, kd, 0:2304],
            in_=wd["W_inT"][128 * kd:128 * (kd + 1), 2048:4352])))

    bank_dt = PS[4]
    dt_evs = []
    for tb in range(8):
        for tok in wdt_toks:
            em.wait("te", tok)
        def f_mmdt(te, tb=tb):
            for kd in range(8):
                ins = te.matmul(bank_dt[0:16, 0:512],
                                Wdt[:, kd], xn[:, kd, 512 * tb:512 * (tb + 1)],
                                start=(kd == 0), stop=(kd == 7))
            return ins
        em.wait("te", dt_evs[-1] if dt_evs else None)
        t_mm = em.op("te", f_mmdt)
        em.wait("ve", t_mm)
        t_ev = em.op("ve", lambda ve, tb=tb: ve.tensor_copy(
            dtr[:, 512 * tb:512 * (tb + 1)], bank_dt[0:16, 0:512]))
        dt_evs.append(t_ev)
    em.wait("sy", dt_evs[-1])
    st_dtr = em.dma("sy", lambda sy: sy.dma_start(out=scr["dtr_scr"][:, :], in_=dtr[:]))

    em.op("ve", lambda ve: ve.memset(raw[:], 0.0))
    t_pad = em.tok("ve")
    last = {}
    for jc in range(NJC):
        csl = jc % 2
        bsl = 0
        em.wait("sy", last.get(("cw", csl)))
        t_cw = em.dma("sy", lambda sy, jc=jc, csl=csl: sy.dma_start(
            out=cw[:, csl], in_=wd["conv_w"][jc]))
        t_cb = em.dma("sy", lambda sy, jc=jc, csl=csl: sy.dma_start(
            out=cb[:, csl], in_=wd["conv_b"][jc]))
        raw_evs = []
        for tb in range(8):
            bank = PS[2 + tb % 2]
            em.wait("te", last.get(("xps", tb % 2)))
            for tok in wx_toks:
                em.wait("te", tok)
            def f_mm(te, jc=jc, tb=tb, bank=bank):
                for kd in range(8):
                    ins = te.matmul(bank[:], W[:, kd, 128 * jc:128 * (jc + 1)],
                                    xn[:, kd, 512 * tb:512 * (tb + 1)],
                                    start=(kd == 0), stop=(kd == 7))
                return ins
            t_mm = em.op("te", f_mm)
            em.wait("ve", t_mm)
            em.wait("ve", last.get(("rawfull", bsl)))
            col = 3 + 512 * tb if tb < 4 else 2054 + 512 * (tb - 4)
            t_ev = em.op("ve", lambda ve, col=col, bank=bank, bsl=bsl: ve.tensor_copy(
                raw[:, bsl, col:col + 512], bank[:]))
            last[("xps", tb % 2)] = t_ev
            raw_evs.append(t_ev)
        em.wait("ve", t_pad)
        for tok in raw_evs:
            em.wait("ve", tok)
        em.wait("ve", t_cw)
        em.wait("ve", last.get(("accfull", bsl)))
        def rawview(j, bsl=bsl):
            r = raw[:, bsl]
            return bass.AP(r.tensor, r.offset + j, [r.ap[0], [2051, 2], [1, 2048]])
        def f_c0(ve, csl=csl, bsl=bsl):
            a3 = acc[:, bsl].rearrange("p (s c) -> p s c", s=2)
            return ve.tensor_scalar(a3, rawview(0), cw[:, csl, 0:1], None, ALU.mult)
        em.op("ve", f_c0)
        for j in range(1, 4):
            def f_cj(ve, j=j, csl=csl, bsl=bsl):
                a3 = acc[:, bsl].rearrange("p (s c) -> p s c", s=2)
                return ve.scalar_tensor_tensor(a3, rawview(j), cw[:, csl, j:j + 1],
                                               a3, ALU.mult, ALU.add)
            em.op("ve", f_cj)
        t_conv = em.tok("ve")
        last[("rawfull", bsl)] = t_conv
        em.wait("sc", t_conv)
        em.wait("sc", t_cb)
        em.wait("sc", last.get(("actfull", bsl)))
        t_act = em.op("sc", lambda sc, csl=csl, bsl=bsl: sc.activation(
            out=act[:, bsl], in_=acc[:, bsl], func=AF.Silu, bias=cb[:, csl]))
        last[("accfull", bsl)] = t_act
        last[("cw", csl)] = t_act
        em.wait("sy", t_act)
        if jc < 16:
            st = em.dma("sy", lambda sy, jc=jc, bsl=bsl: sy.dma_start(
                out=scr["xfm_scr"][128 * jc:128 * (jc + 1), :], in_=act[:, bsl]))
        elif jc == 16:
            st = em.dma("sy", lambda sy, bsl=bsl: sy.dma_start(
                out=scr["bfm_scr"][:, :], in_=act[:, bsl]))
        else:
            st = em.dma("sy", lambda sy, bsl=bsl: sy.dma_start(
                out=scr["cfm_scr"][:, :], in_=act[:, bsl]))
        last[("actfull", bsl)] = st
    return st_dtr


def stage_s2(nc, em, bufs, wd, scr, dbg=None):
    PS = bufs["ps"]
    pbf = bufs["ps_bf"]
    identb = bufs["ident"]
    identf = bufs["ident_f32"]
    id_tok = bufs["ident_tok"]
    idf_tok = bufs["identf_tok"]

    dtr = bufs["s2_dtr"]
    dt = bufs["s2_dt"]
    t16 = bufs["s2_t16"]
    acs = bufs["s2_acs"]
    bcol = bufs["s2_bcol"]
    dtT = bufs["s2_dtT"]
    nacsT = bufs["s2_nacsT"]
    eacsT = bufs["s2_eacsT"]
    sfacT = bufs["s2_sfacT"]
    decfm = bufs["s2_decfm"]
    cd16 = bufs["s2_cd16"]
    cdbc = bufs["s2_cdbc"]
    dbc = bufs["s2_dbc"]
    nwbc = bufs["s2_nwbc"]
    Wo = bufs["s2_wo"]
    xch = bufs["s2_xch"]
    xtm = bufs["s2_xtm"]
    Bc = bufs["s2_bc"]
    Cc = bufs["s2_cc"]
    Btm = bufs["s2_btm"]
    bsa = bufs["s2_bsa"]
    cbt = bufs["s2_cbt"]
    seg1 = bufs["s2_seg1"]
    mta = bufs["s2_mta"]
    DI = bufs["s2_di"]
    carry = bufs["s2_carry"]
    carrybf = bufs["s2_carrybf"]
    yo = bufs["s2_yo"]
    ytm = bufs["s2_ytm"]
    ztm = bufs["s2_ztm"]
    yfm = bufs["s2_yfm"]
    rcol = bufs["s2_rcol"]
    sout = bufs["s2_sout"]

    lu = {}

    def bcast_h(ap2, nh, nq):
        return bass.AP(ap2.tensor, ap2.offset, [ap2.ap[0], [0, nh], list(ap2.ap[1])])

    def bcast_q(ap2, nq):
        return bass.AP(ap2.tensor, ap2.offset, [ap2.ap[0], list(ap2.ap[1]), [0, nq]])

    t_db = em.dma("sy", lambda sy: sy.dma_start(out=bcol[:, 0:1], in_=wd["dt_bias"]))
    t_al = em.dma("sy", lambda sy: sy.dma_start(out=bcol[:, 1:2], in_=wd["A_log"]))
    dsrc = wd["Dh"]
    t_dbc = em.dma("sy", lambda sy: sy.dma_start(
        out=dbc[:], in_=bass.AP(dsrc.tensor, dsrc.offset, [[0, 128], [1, 16]])))
    nsrc = wd["norm_weight"]
    t_nw = em.dma("gp", lambda gp: gp.dma_start(
        out=nwbc[:], in_=bass.AP(nsrc.tensor, nsrc.offset, [[0, 128], [1, 2048]])))
    t_dtr = em.dma("sy", lambda sy: sy.dma_start(out=dtr[:], in_=scr["dtr_scr"][:, :]))
    em.wait("ve", t_dbc)
    em.wait("ve", id_tok)
    t_DI = em.op("ve", lambda ve: ve.tensor_tensor(
        DI[:],
        bass.AP(identb[:].tensor, identb[:].offset,
                [identb[:].ap[0], [0, 16], [1, 128]]),
        bass.AP(dbc[:].tensor, dbc[:].offset, [dbc[:].ap[0], [1, 16], [0, 128]]),
        ALU.mult))

    em.wait("sc", t_dtr)
    em.wait("sc", t_db)
    em.op("sc", lambda sc: sc.activation(out=t16[:], in_=dtr[:], func=AF.Abs,
                                         bias=bcol[:, 0:1]))
    em.op("sc", lambda sc: sc.activation(out=t16[:], in_=t16[:], func=AF.Exp, scale=-1.0))
    em.op("sc", lambda sc: sc.activation(out=t16[:], in_=t16[:], func=AF.Ln, bias=1.0))
    t_sp1 = em.op("sc", lambda sc: sc.activation(out=dt[:], in_=dtr[:], func=AF.Relu,
                                                 bias=bcol[:, 0:1]))
    em.wait("ve", t_sp1)
    t_dt = em.op("ve", lambda ve: ve.tensor_add(dt[:], dt[:], t16[:]))
    em.wait("sc", t_al)
    t_a = em.op("sc", lambda sc: sc.activation(out=bcol[:, 2:3], in_=bcol[:, 1:2], func=AF.Exp))
    em.wait("ve", t_a)
    em.wait("ve", t_dt)
    t_dA = em.op("ve", lambda ve: ve.tensor_scalar(
        t16[:], dt[:], bcol[:, 2:3], -1.0, ALU.mult, ALU.mult))
    zc = bufs["zcol16"]
    t_zc = em.op("ve", lambda ve: ve.memset(zc[:], 0.0))
    scan_toks = []
    for s in range(2):
        for c in range(NCH):
            t0 = SEQ * s + CHUNK * c
            def f_scan(ve, t0=t0):
                zb = bass.AP(zc[:].tensor, zc[:].offset, [zc[:].ap[0], [0, CHUNK]])
                return ve.tensor_tensor_scan(
                    out=acs[:, t0:t0 + CHUNK], data0=t16[:, t0:t0 + CHUNK],
                    data1=zb, initial=0.0, op0=ALU.add, op1=ALU.add)
            scan_toks.append(em.op("ve", f_scan))
    t_acs = scan_toks[-1]
    em.wait("sy", t_acs)
    t_acsd = em.dma("sy", lambda sy: sy.dma_start(out=scr["acs_scr"][:, :], in_=acs[:]))
    t_cdbc = []
    for s in range(2):
        a0 = acs[:]
        ends = bass.AP(a0.tensor, a0.offset + SEQ * s + CHUNK - 1, [a0.ap[0], [CHUNK, NCH]])
        em.wait("sc", t_acs)
        em.wait("sc", lu.get("cd16"))
        t_e = em.op("sc", lambda sc, ends=ends: sc.activation(
            out=cd16[:], in_=ends, func=AF.Exp))
        em.wait("sy", t_e)
        t_w = em.dma("sy", lambda sy, s=s: sy.dma_start(
            out=scr["cdec_scr"][s], in_=cd16[:]))
        em.wait("sy", t_w)
        csrc = scr["cdec_scr"]
        t_b = em.dma("sy", lambda sy, s=s, csrc=csrc: sy.dma_start(
            out=cdbc[:, s], in_=bass.AP(csrc.tensor, csrc.offset + 128 * s,
                                        [[0, 128], [1, 128]])))
        lu["cd16"] = t_w
        t_cdbc.append(t_b)

    em.wait("te", idf_tok)
    for ti in range(32):
        em.wait("te", t_dt)
        em.wait("te", t_acs)
        em.wait("te", lu.get(("tps", ti % 2)))
        em.wait("te", lu.get(("tpsv", ti % 2)))
        bank = PS[ti % 2]
        def f_tr(te, ti=ti, bank=bank):
            te.transpose(bank[0:128, 0:16], dt[:, 128 * ti:128 * (ti + 1)],
                         identf[0:16, 0:16])
            return te.transpose(bank[0:128, 16:32], acs[:, 128 * ti:128 * (ti + 1)],
                                identf[0:16, 0:16])
        t_tr = em.op("te", f_tr)
        em.wait("ve", t_tr)
        t_c1 = em.op("ve", lambda ve, ti=ti, bank=bank: ve.tensor_copy(
            dtT[:, ti], bank[0:128, 0:16]))
        t_c2 = em.op("ve", lambda ve, ti=ti, bank=bank: ve.tensor_scalar_mul(
            nacsT[:, ti], bank[0:128, 16:32], -1.0))
        em.wait("sc", t_tr)
        t_c3 = em.op("sc", lambda sc, ti=ti, bank=bank: sc.activation(
            out=eacsT[:, ti], in_=bank[0:128, 16:32], func=AF.Exp))
        lu[("tps", ti % 2)] = t_c3
        lu[("tpsv", ti % 2)] = t_c2
    t_cols = (t_c1, t_c2, t_c3)
    for s in range(2):
        for c in range(NCH):
            t0 = SEQ * s + CHUNK * c
            em.wait("ve", t_acs)
            em.wait("ve", lu.get("decfm"))
            t_d1 = em.op("ve", lambda ve, t0=t0: ve.tensor_scalar(
                decfm[:], acs[:, t0:t0 + CHUNK], acs[:, t0 + CHUNK - 1:t0 + CHUNK],
                None, ALU.subtract))
            em.wait("sc", t_d1)
            t_d2 = em.op("sc", lambda sc: sc.activation(
                out=decfm[:], in_=decfm[:], func=AF.Exp, scale=-1.0))
            ti0 = 2 * (NCH * s + c)
            em.wait("te", t_d2)
            em.wait("te", lu.get("decps"))
            def f_tr2(te):
                te.transpose(PS[2][0:128, 0:16], decfm[:, 0:128], identf[0:16, 0:16])
                return te.transpose(PS[2][0:128, 16:32], decfm[:, 128:256],
                                    identf[0:16, 0:16])
            t_tr2 = em.op("te", f_tr2)
            for st in range(2):
                em.wait("ve", t_tr2)
                t_sf = em.op("ve", lambda ve, ti0=ti0, st=st: ve.tensor_mul(
                    sfacT[:, ti0 + st], dtT[:, ti0 + st],
                    PS[2][0:128, 16 * st:16 * (st + 1)]))
            lu["decps"] = t_sf
            lu["decfm"] = t_tr2

    em.wait("gp", em.tok("te"))
    em.wait("gp", em.tok("ve"))
    wo_toks = [em.dma("gp", lambda gp, kc=kc: gp.dma_start(
        out=Wo[:, kc], in_=wd["W_outT"][128 * kc:128 * (kc + 1), :]))
        for kc in range(16)]


    def P2a(cc):
        s, c = (cc % 2, cc // 2) if S2_ILV else (cc // NCH, cc % NCH)
        t0 = SEQ * s + CHUNK * c
        g = cc % 2
        em.wait("gp", lu.get(("xchg", g)))
        xsrc = scr["xfm_scr"].rearrange("(c p) t -> p c t", p=128)
        t_xch = em.dma("gp", lambda gp, t0=t0, g=g, xsrc=xsrc: gp.dma_start(
            out=xch[:, g], in_=xsrc[:, :, t0:t0 + CHUNK]))
        em.wait("gp", lu.get(("bcg", g)))
        t_bc = em.dma("gp", lambda gp, t0=t0, g=g: gp.dma_start(
            out=Bc[:, g], in_=scr["bfm_scr"][:, t0:t0 + CHUNK]))
        t_cc_ = em.dma("gp", lambda gp, t0=t0, g=g: gp.dma_start(
            out=Cc[:, g], in_=scr["cfm_scr"][:, t0:t0 + CHUNK]))
        lu[("ccld", g)] = t_cc_
        lu[("ld", cc)] = (t_xch, t_bc, t_cc_)

    def P2(cc):
        s, c = (cc % 2, cc // 2) if S2_ILV else (cc // NCH, cc % NCH)
        t0 = SEQ * s + CHUNK * c
        ti0 = 2 * (NCH * s + c)
        g = cc % 2
        t_xch, t_bc, t_cc_ = lu[("ld", cc)]
        for gg in range(4):
            pb = pbf[gg % 2]
            em.wait("te", t_xch)
            em.wait("te", id_tok)
            em.wait("te", lu.get(("pbf", gg % 2)))
            em.wait("te", lu.get(("xtmg", g)))
            def f_xtr(te, gg=gg, pb=pb, g=g):
                for k in range(8):
                    idx = 8 * gg + k
                    st, c16 = idx // 16, idx % 16
                    ins = te.transpose(pb[:, 128 * k:128 * (k + 1)],
                                       xch[:, g, c16, 128 * st:128 * (st + 1)],
                                       identb[:])
                return ins
            t_xtr = em.op("te", f_xtr)
            em.wait("sc", t_xtr)
            em.wait("sc", lu.get(("xtmg", g)))
            def f_xev(sc, gg=gg, pb=pb, g=g):
                st, c0 = (8 * gg) // 16, (8 * gg) % 16
                return sc.activation(
                    out=xtm[:, g, st, 128 * c0:128 * c0 + 1024],
                    in_=pb[:], func=AF.Copy)
            t_xev = em.op("sc", f_xev)
            lu[("pbf", gg % 2)] = t_xev
        lu[("xchg", g)] = t_xtr
        lu[("xtm", cc)] = t_xev
        em.wait("te", t_bc)
        em.wait("te", lu.get(("btmg", g)))
        em.wait("te", lu.get(("pbf", 0)))
        def f_btr(te, g=g):
            te.transpose(pbf[0][:, 0:128], Bc[:, g, 0:128], identb[:])
            return te.transpose(pbf[0][:, 128:256], Bc[:, g, 128:256], identb[:])
        t_btr = em.op("te", f_btr)
        em.wait("ve", t_btr)
        t_bev = em.op("ve", lambda ve, g=g: ve.tensor_copy(
            Btm[:, g].rearrange("p a b -> p (a b)"), pbf[0][:, 0:256]))
        lu[("pbf", 0)] = t_bev
        lu[("bcg", g)] = t_btr
        lu[("btm", cc)] = t_bev
        em.wait("te", t_cc_)
        em.wait("te", lu.get(("bank", 5)))
        def f_cbt(te, g=g):
            for st in range(2):
                ins = te.matmul(PS[5][:, 256 * st:256 * (st + 1)],
                                Bc[:, g, 128 * st:128 * (st + 1)], Cc[:, g],
                                start=True, stop=True)
            return ins
        t_cbtm = em.op("te", f_cbt)
        em.wait("ve", t_cbtm)
        em.wait("ve", lu.get(("cbtg", g)))
        t_cbt = em.op("ve", lambda ve, g=g: ve.tensor_copy(
            cbt[:, g].rearrange("p a b -> p (a b)"), PS[5][:]))
        lu[("bank", 5)] = t_cbt
        mta_toks = []
        asrc = scr["acs_scr"]
        for st in range(2):
            t_exs = []
            for hh in range(2):
                sl = (2 * st + hh) % 2
                em.wait("gp", lu.get(("seg1", sl)))
                em.wait("gp", t_acsd)
                t_ab = em.dma("gp", lambda gp, t0=t0, hh=hh, sl=sl, asrc=asrc:
                              gp.dma_start(
                    out=seg1[:, sl],
                    in_=bass.AP(asrc.tensor, asrc.offset + 8 * hh * T + t0,
                                [[0, 128], [T, 8], [1, 256]])))
                em.wait("gp", t_ab)
                em.wait("gp", t_cols[1])
                a1 = nacsT[:, ti0 + st, 8 * hh:8 * (hh + 1)]
                t_s1 = em.op("gp", lambda gp, sl=sl, a1=a1: gp.tensor_tensor(
                    seg1[:, sl], seg1[:, sl], bcast_q(a1, 256), ALU.add))
                em.wait("sc", t_s1)
                t_rl = em.op("sc", lambda sc, sl=sl: sc.activation(
                    out=seg1[:, sl], in_=seg1[:, sl], func=AF.Relu, scale=-1.0))
                em.wait("sc", lu.get(("mtag", g, st)))
                t_ex = em.op("sc", lambda sc, st=st, hh=hh, sl=sl, g=g: sc.activation(
                    out=mta[:, g, st, 8 * hh:8 * (hh + 1), :], in_=seg1[:, sl],
                    func=AF.Exp, scale=-1.0))
                lu[("seg1", sl)] = t_ex
                t_exs.append(t_ex)
            for t_ex in t_exs:
                em.wait("ve", t_ex)
            d1 = dtT[:, ti0 + st, :]
            em.op("ve", lambda ve, st=st, d1=d1, g=g: ve.tensor_tensor(
                mta[:, g, st], mta[:, g, st], bcast_q(d1, 256), ALU.mult))
            em.wait("ve", t_cbt)
            c1 = cbt[:, g, st]
            t_m2 = em.op("ve", lambda ve, st=st, c1=c1, g=g: ve.tensor_tensor(
                mta[:, g, st], mta[:, g, st], bcast_h(c1, 16, 256), ALU.mult))
            em.wait("gp", t_m2)
            t_msk = em.op("gp", lambda gp, st=st, g=g: gp.affine_select(
                out=mta[:, g, st], in_=mta[:, g, st], pattern=[[0, 16], [1, 256]],
                compare_op=ALU.is_ge, fill=0.0, base=-128 * st,
                channel_multiplier=-1))
            em.wait("ve", t_msk)
            em.wait("ve", t_DI)
            t_di = em.op("ve", lambda ve, st=st, g=g: ve.tensor_add(
                mta[:, g, st, :, 128 * st:128 * (st + 1)],
                mta[:, g, st, :, 128 * st:128 * (st + 1)], DI[:]))
            mta_toks.append(t_di)
        lu[("cbtg", g)] = t_m2
        lu[("mta_ready", cc)] = mta_toks

    def P3(cc):
        s, c = (cc % 2, cc // 2) if S2_ILV else (cc // NCH, cc % NCH)
        ti0 = 2 * (NCH * s + c)
        g = cc % 2
        if cc < 2:
            em.wait("ve", t_acsd)
            em.wait("ve", t_cols[2])
            t_cz = em.op("ve", lambda ve, s=s: ve.memset(carry[:, s], 0.0))
            lu["carry"] = t_cz
        em.wait("ve", lu.get("bsa_used"))
        em.wait("ve", lu.get(("btm", cc)))
        bsa_toks = []
        for st in range(2):
            b1 = Btm[:, g, st]
            s1_ = sfacT[:, ti0 + st, :]
            t_bs = em.op("ve", lambda ve, st=st, b1=b1, s1_=s1_: ve.tensor_tensor(
                bsa[:, st], bcast_h(b1, 16, 128), bcast_q(s1_, 128), ALU.mult))
            bsa_toks.append(t_bs)
        for h in range(NH):
            par = h % 2
            psA = PS[0 + par]
            psB = PS[2 + par]
            psC = PS[4]
            em.wait("te", lu.get(("bank", par)))
            for tok in lu[("mta_ready", cc)]:
                em.wait("te", tok)
            em.wait("te", lu.get(("xtm", cc)))
            def f_yd(te, h=h, psA=psA, g=g):
                for qt in range(2):
                    for st in range(2):
                        ins = te.matmul(psA[:, 128 * qt:128 * (qt + 1)],
                                        mta[:, g, st, h, 128 * qt:128 * (qt + 1)],
                                        xtm[:, g, st, 128 * h:128 * (h + 1)],
                                        start=(st == 0), stop=(st == 1))
                return ins
            t_yd = em.op("te", f_yd)
            if c > 0:
                em.wait("te", lu.get(("bank", 2 + par)))
                em.wait("te", lu.get(("ccld", g)))
                em.wait("te", lu.get(("carrybf", s)))
                def f_yo(te, h=h, psB=psB, g=g, s=s):
                    for qt in range(2):
                        ins = te.matmul(psB[:, 128 * qt:128 * (qt + 1)],
                                        Cc[:, g, 128 * qt:128 * (qt + 1)],
                                        carrybf[:, s, h], start=True, stop=True)
                    return ins
                t_yo = em.op("te", f_yo)
            em.wait("te", lu.get(("bank", 4)))
            for tok in bsa_toks:
                em.wait("te", tok)
            def f_st(te, h=h, psC=psC, g=g):
                for st in range(2):
                    ins = te.matmul(psC[:, 0:128], bsa[:, st, h],
                                    xtm[:, g, st, 128 * h:128 * (h + 1)],
                                    start=(st == 0), stop=(st == 1))
                return ins
            t_stm = em.op("te", f_st)
            for qt in range(2):
                if c > 0:
                    em.wait("sc", t_yo)
                    em.wait("sc", lu.get(("yo", par)))
                    t_yoev = em.op("sc", lambda sc, qt=qt, ti0=ti0, h=h, psB=psB,
                                   par=par:
                                   sc.activation(out=yo[:, par, qt],
                                                 in_=psB[:, 128 * qt:128 * (qt + 1)],
                                                 func=AF.Copy,
                                                 scale=eacsT[:, ti0 + qt, h:h + 1]))
                    em.wait("ve", t_yoev)
                    em.wait("ve", t_yd)
                    em.wait("ve", lu.get("ytm_used"))
                    t_y2 = em.op("ve", lambda ve, qt=qt, h=h, psA=psA, par=par:
                                 ve.tensor_add(
                                     ytm[:, qt, 128 * h:128 * (h + 1)],
                                     yo[:, par, qt],
                                     psA[:, 128 * qt:128 * (qt + 1)]))
                else:
                    em.wait("sc", t_yd)
                    em.wait("sc", lu.get("ytm_used"))
                    em.wait("sc", lu.get(("yo", par)))
                    t_y2 = em.op("sc", lambda sc, qt=qt, h=h, psA=psA: sc.activation(
                        out=ytm[:, qt, 128 * h:128 * (h + 1)],
                        in_=psA[:, 128 * qt:128 * (qt + 1)], func=AF.Copy))
            lu[("yo", par)] = t_y2
            if c > 0:
                lu[("bank", 2 + par)] = t_y2
            lu[("bank", par)] = t_y2
            em.wait("ve", t_stm)
            em.wait("ve", lu.get("carry"))
            em.wait("ve", t_cdbc[s])
            t_cu = em.op("ve", lambda ve, h=h, c=c, s=s, psC=psC: ve.scalar_tensor_tensor(
                carry[:, s, h], carry[:, s, h], cdbc[:, s, 8 * h + c:8 * h + c + 1],
                psC[:, 0:128], ALU.mult, ALU.add))
            lu[("bank", 4)] = t_cu
            em.wait("gp", t_cu)
            em.wait("gp", lu.get(("cbf", s, h)))
            t_cb2 = em.op("gp", lambda gp, h=h, s=s: gp.tensor_copy(
                carrybf[:, s, h], carry[:, s, h]))
            lu[("cbf", s, h)] = t_cb2
        lu[("carrybf", s)] = t_cb2
        lu["carry_used"] = t_cu
        lu["bsa_used"] = t_stm
        lu[("xtmg", g)] = t_cu
        lu[("mtag", g, 0)] = t_yd
        lu[("mtag", g, 1)] = t_yd
        lu[("btmg", g)] = t_bs
        lu["lasty2"] = t_y2

    def P4(cc):
        s, c = (cc % 2, cc // 2) if S2_ILV else (cc // NCH, cc % NCH)
        t0 = SEQ * s + CHUNK * c
        g = cc % 2
        em.wait("sy", lu.get("ztm_used"))
        t_ztm = em.dma("sy", lambda sy, t0=t0: sy.dma_start(
            out=ztm[:], in_=scr["z_scr"][t0:t0 + CHUNK, :].rearrange(
                "(h p) d -> p h d", p=128)))
        em.wait("sc", t_ztm)
        em.wait("sc", lu.get("ztm_used2"))
        t_sz = em.op("sc", lambda sc: sc.activation(out=ztm[:], in_=ztm[:], func=AF.Silu))
        for qt in range(2):
            em.wait("ve", t_sz)
            em.wait("ve", lu["lasty2"])
            t_g = em.op("ve", lambda ve, qt=qt: ve.tensor_mul(
                ytm[:, qt], ytm[:, qt], ztm[:, qt]))
            t_sq = em.op("ve", lambda ve, qt=qt: ve.scalar_tensor_tensor(
                ztm[:, qt], ytm[:, qt], 1.0, ytm[:, qt], ALU.mult, ALU.mult,
                accum_out=rcol[:, 4 * qt:4 * qt + 1]))
            em.op("ve", lambda ve, qt=qt: ve.tensor_scalar(
                rcol[:, 4 * qt + 1:4 * qt + 2], rcol[:, 4 * qt:4 * qt + 1],
                1.0 / DIN, EPS, ALU.mult, ALU.add))
            t_rc = em.op("ve", lambda ve, qt=qt: ve.reciprocal(
                rcol[:, 4 * qt + 2:4 * qt + 3], rcol[:, 4 * qt + 1:4 * qt + 2]))
            em.wait("sc", t_rc)
            t_rs = em.op("sc", lambda sc, qt=qt: sc.activation(
                out=rcol[:, 4 * qt + 3:4 * qt + 4],
                in_=rcol[:, 4 * qt + 2:4 * qt + 3], func=AF.Sqrt))
            em.wait("ve", t_rs)
            em.wait("ve", t_nw)
            t_y3 = em.op("ve", lambda ve, qt=qt: ve.scalar_tensor_tensor(
                ytm[:, qt], ytm[:, qt], rcol[:, 4 * qt + 3:4 * qt + 4],
                nwbc[:], ALU.mult, ALU.mult))
            lu[("y3", qt)] = t_y3
        lu["ztm_used"] = t_sq
        lu["ztm_used2"] = t_y3
        for gg in range(4):
            pb = pbf[gg % 2]
            em.wait("te", lu[("y3", 0)])
            em.wait("te", lu[("y3", 1)])
            em.wait("te", lu.get(("pbf", gg % 2)))
            em.wait("te", lu.get("yfm_used"))
            def f_ytr(te, gg=gg, pb=pb):
                for k in range(8):
                    idx = 8 * gg + k
                    st, c16 = idx // 16, idx % 16
                    ins = te.transpose(pb[:, 128 * k:128 * (k + 1)],
                                       ytm[:, st, 128 * c16:128 * (c16 + 1)],
                                       identb[:])
                return ins
            t_ytr = em.op("te", f_ytr)
            em.wait("sc", t_ytr)
            em.wait("sc", lu.get("yfm_used"))
            def f_yev(sc, gg=gg, pb=pb):
                st, c0 = (8 * gg) // 16, (8 * gg) % 16
                return sc.activation(
                    out=yfm[:, c0:c0 + 8, 128 * st:128 * (st + 1)],
                    in_=pb[:].rearrange("p (c f) -> p c f", f=128),
                    func=AF.Copy)
            t_yev = em.op("sc", f_yev)
            lu[("pbf", gg % 2)] = t_yev
        lu["ytm_used"] = t_yev
        sev_toks = []
        for qt in range(2):
            for dn in range(2):
                bi = 2 + (qt * 2 + dn) % 2
                bank = PS[bi]
                em.wait("te", t_yev)
                for tok in wo_toks:
                    em.wait("te", tok)
                em.wait("te", lu.get(("bank", bi)))
                def f_op(te, qt=qt, dn=dn, bank=bank):
                    for kc in range(16):
                        ins = te.matmul(bank[:],
                                        yfm[:, kc, 128 * qt:128 * (qt + 1)],
                                        Wo[:, kc, 512 * dn:512 * (dn + 1)],
                                        start=(kc == 0), stop=(kc == 15))
                    return ins
                t_op = em.op("te", f_op)
                em.wait("sc", t_op)
                em.wait("sc", lu.get("sout_used"))
                t_sev = em.op("sc", lambda sc, qt=qt, dn=dn, bank=bank: sc.activation(
                    out=sout[:, qt, 512 * dn:512 * (dn + 1)], in_=bank[:], func=AF.Copy))
                lu[("bank", bi)] = t_sev
                sev_toks.append(t_sev)
        for tok in sev_toks:
            em.wait("sy", tok)
        t_sdma = em.dma("sy", lambda sy, t0=t0: sy.dma_start(
            out=scr["s_scr"][t0:t0 + CHUNK, :].rearrange(
                "(h p) d -> p h d", p=128),
            in_=sout[:]))
        lu["sout_used"] = t_sdma
        lu["yfm_used"] = t_op
        lu["s_done"] = t_sdma

    NCC = 2 * NCH
    PIPE = S2_PIPE
    if PIPE:
        P2a(0)
        P2(0)
        for cc in range(1, NCC):
            P2a(cc)
            P2(cc)
            P3(cc - 1)
            P4(cc - 1)
        P3(NCC - 1)
        P4(NCC - 1)
    else:
        P2a(0)
        for cc in range(NCC):
            P2(cc)
            if cc + 1 < NCC:
                P2a(cc + 1)
            P3(cc)
            P4(cc)
    return lu["s_done"]


def stage_s3(nc, em, bufs, wd, scr, xn_tok):
    xn = bufs["xn_fm"]
    PS = bufs["ps"]
    pbf = bufs["ps_bf"]
    identb = bufs["ident"]
    id_tok = bufs["ident_tok"]
    wg = bufs["s3_wg"]
    bgbc = bufs["s3_bgbc"]
    besb = bufs["s3_be"]
    gcl = bufs["s3_gcl"]
    gsc = bufs["s3_gsc"]
    gax1 = bufs["s3_gax1"]
    gax2 = bufs["s3_gax2"]
    gw_all = bufs["s3_gw"]
    gwb_all = bufs["s3_gwb"]
    wT_all = bufs["s3_wt"]
    We_sb = bufs["s3_we"]
    acc = bufs["s3_acc"]
    ut = bufs["s3_u"]
    st_ = bufs["s3_s"]
    gbf = bufs["s3_g"]
    sg = bufs["s3_sg"]
    x1t = bufs["s3_x1"]
    xn1t = bufs["s3_xn1"]
    sqs = bufs["s1_acc"]
    lu = {}

    t_wg = []
    for kd in range(8):
        t_wg.append(em.dma("gp", lambda gp, kd=kd: gp.dma_start(
            out=wg[:, kd], in_=wd["Wg"][128 * kd:128 * (kd + 1), :])))
    bsrc = wd["bg"]
    t_bg = em.dma("sy", lambda sy: sy.dma_start(
        out=bgbc[:], in_=bass.AP(bsrc.tensor, bsrc.offset, [[0, 128], [1, 8]])))
    t_be = em.dma("gp", lambda gp: gp.dma_start(out=besb[:], in_=wd["be"][:, :]))

    ge_toks = []
    for ti in range(32):
        bank = PS[4 + ti % 2]
        for tok in t_wg:
            em.wait("te", tok)
        em.wait("te", xn_tok)
        em.wait("te", lu.get(("gps", ti % 2)))
        def f_g(te, ti=ti, bank=bank):
            for kd in range(8):
                ins = te.matmul(bank[0:128, 0:8], xn[:, kd, 128 * ti:128 * (ti + 1)],
                                wg[:, kd], start=(kd == 0), stop=(kd == 7))
            return ins
        t_gm = em.op("te", f_g)
        em.wait("ve", t_gm)
        em.wait("ve", t_bg)
        t_ge = em.op("ve", lambda ve, ti=ti, bank=bank: ve.tensor_add(
            gsc[:, ti], bank[0:128, 0:8], bgbc[:]))
        lu[("gps", ti % 2)] = t_ge
        ge_toks.append(t_ge)

    def bmid(ap3, n_last):
        return bass.AP(ap3.tensor, ap3.offset,
                       [ap3.ap[0], list(ap3.ap[1]), [0, n_last]])

    em.wait("ve", ge_toks[-1])
    em.op("ve", lambda ve: ve.reduce_max(gcl[:, 0], gsc[:], axis=AX.X))
    em.op("ve", lambda ve: ve.tensor_tensor(
        gax1[:], gsc[:], bmid(gcl[:, 0], 8), ALU.is_ge))
    em.op("ve", lambda ve: ve.scalar_tensor_tensor(
        gax2[:], gax1[:], -1e30, gsc[:], ALU.mult, ALU.add))
    em.op("ve", lambda ve: ve.reduce_max(gcl[:, 1], gax2[:], axis=AX.X))
    em.op("ve", lambda ve: ve.tensor_tensor(
        gax1[:], gsc[:], bmid(gcl[:, 1], 8), ALU.is_ge))
    t_d = em.op("ve", lambda ve: ve.tensor_tensor(
        gax2[:], gsc[:], bmid(gcl[:, 0], 8), ALU.subtract))
    em.wait("sc", t_d)
    t_e1 = em.op("sc", lambda sc: sc.activation(out=gax2[:], in_=gax2[:], func=AF.Exp))
    em.wait("ve", t_e1)
    em.op("ve", lambda ve: ve.tensor_mul(gax2[:], gax2[:], gax1[:]))
    em.op("ve", lambda ve: ve.reduce_sum(gcl[:, 2], gax2[:], axis=AX.X))
    em.op("ve", lambda ve: ve.reciprocal(gcl[:, 3], gcl[:, 2]))
    em.op("ve", lambda ve: ve.tensor_tensor(
        gw_all[:], gax2[:], bmid(gcl[:, 3], 8), ALU.mult))
    t_gwb = em.op("ve", lambda ve: ve.tensor_copy(gwb_all[:], gw_all[:]))
    lu["gw_done"] = t_gwb
    for tg in range(4):
        em.wait("te", t_gwb)
        em.wait("te", id_tok)
        em.wait("te", lu.get("wtps"))
        def f_wt(te, tg=tg):
            for j in range(8):
                ins = te.transpose(pbf[0][0:8, 128 * j:128 * (j + 1)],
                                   gwb_all[:, 8 * tg + j, :], identb[:])
            return ins
        t_wt = em.op("te", f_wt)
        em.wait("ve", t_wt)
        t_wte = em.op("ve", lambda ve, tg=tg: ve.tensor_copy(
            wT_all[:, 8 * tg:8 * (tg + 1), :],
            pbf[0][0:8, 0:1024].rearrange("p (j f) -> p j f", f=128)))
        lu["wtps"] = t_wte
        for j in range(8):
            lu[("wtok", 8 * tg + j)] = t_wte
    t_gate_done = t_wte

    x1_toks = []
    for half in range(2):
        tis = range(16 * half, 16 * (half + 1))
        for ti in tis:
            tloc = ti - 16 * half
            for dn in range(2):
                bank = PS[4 + (2 * tloc + dn) % 2]
                em.wait("te", lu[("wtok", ti)])
                em.wait("te", t_be)
                em.wait("te", lu.get(("bps", (2 * tloc + dn) % 2)))
                t_bm = em.op("te", lambda te, ti=ti, dn=dn, bank=bank: te.matmul(
                    bank[:], wT_all[:, ti], besb[:, 512 * dn:512 * (dn + 1)],
                    start=True, stop=True))
                em.wait("sc", t_bm)
                em.wait("sc", lu.get(("accw", tloc)))
                t_ai = em.op("sc", lambda sc, tloc=tloc, dn=dn, bank=bank: sc.activation(
                    out=acc[:, tloc, 512 * dn:512 * (dn + 1)], in_=bank[:], func=AF.Copy))
                lu[("bps", (2 * tloc + dn) % 2)] = t_ai
            lu[("acci", tloc)] = t_ai
        for e in range(8):
            sl = e % 2
            we_toks = []
            em.wait("gp", lu.get(("wesl", sl)))
            for kd in range(8):
                we_toks.append(em.dma("gp", lambda gp, e=e, sl=sl, kd=kd: gp.dma_start(
                    out=We_sb[:, sl, kd], in_=wd["We"][e, 128 * kd:128 * (kd + 1), :])))
            for ti in tis:
                tloc = ti - 16 * half
                for dn in range(2):
                    bank = PS[(2 * tloc + dn) % 4]
                    for tok in we_toks:
                        em.wait("te", tok)
                    em.wait("te", lu.get(("eps", (2 * tloc + dn) % 4)))
                    def f_e(te, ti=ti, sl=sl, dn=dn, bank=bank):
                        for kd in range(8):
                            ins = te.matmul(bank[:], xn[:, kd, 128 * ti:128 * (ti + 1)],
                                            We_sb[:, sl, kd, 512 * dn:512 * (dn + 1)],
                                            start=(kd == 0), stop=(kd == 7))
                        return ins
                    t_em = em.op("te", f_e)
                    em.wait("ve", t_em)
                    em.wait("ve", lu.get(("acci", tloc)))
                    t_up = em.op("ve", lambda ve, tloc=tloc, dn=dn, e=e, bank=bank,
                                 ti=ti: ve.scalar_tensor_tensor(
                                     acc[:, tloc, 512 * dn:512 * (dn + 1)],
                                     bank[:], gw_all[:, ti, e:e + 1],
                                     acc[:, tloc, 512 * dn:512 * (dn + 1)],
                                     ALU.mult, ALU.add))
                    lu[("eps", (2 * tloc + dn) % 4)] = t_up
                    lu[("accu", tloc)] = t_up
            lu[("wesl", sl)] = em.tok("te")
        for ti in tis:
            tloc = ti - 16 * half
            sl2 = ti % 2
            em.wait("sy", lu.get(("ut", 0)))
            t_u = em.dma("sy", lambda sy, ti=ti: sy.dma_start(
                out=ut[:, 0], in_=wd["u_d"][128 * ti:128 * (ti + 1), :]))
            t_s = em.dma("sy", lambda sy, ti=ti, sl2=sl2: sy.dma_start(
                out=st_[:, sl2], in_=scr["s_scr"][128 * ti:128 * (ti + 1), :]))
            em.wait("sc", lu.get(("accu", tloc)))
            em.wait("sc", lu.get("gbf"))
            t_g = em.op("sc", lambda sc, tloc=tloc: sc.activation(
                out=gbf[:], in_=acc[:, tloc], func=AF.Silu))
            lu[("accw", tloc)] = t_g
            em.wait("ve", t_g)
            em.wait("ve", t_s)
            em.wait("ve", lu.get("sg"))
            t_sg = em.op("ve", lambda ve, sl2=sl2: ve.tensor_mul(sg[:], st_[:, sl2], gbf[:]))
            lu["gbf"] = t_sg
            em.wait("ve", t_u)
            em.wait("ve", lu.get(("x1", sl2)))
            em.wait("ve", lu.get(("x1d", sl2)))
            t_x1 = em.op("ve", lambda ve, sl2=sl2: ve.tensor_add(
                x1t[:, sl2], sg[:], ut[:, 0]))
            lu["sg"] = t_x1
            lu[("ut", 0)] = t_x1
            em.wait("sy", t_x1)
            t_x1d = em.dma("sy", lambda sy, ti=ti, sl2=sl2: sy.dma_start(
                out=scr["x1_scr"][128 * ti:128 * (ti + 1), :], in_=x1t[:, sl2]))
            lu[("x1d", sl2)] = t_x1d
            x1_toks.append(t_x1d)
            cb = 8 * sl2
            col = bufs["ln_col"]
            em.wait("sc", t_x1)
            em.wait("sc", lu.get(("lncol", sl2)))
            t_sq = em.op("sc", lambda sc, sl2=sl2, cb=cb: sc.activation(
                out=sqs[:, 0:1024], in_=x1t[:, sl2], func=AF.Square,
                accum_out=col[:, cb:cb + 1]))
            em.wait("ve", t_sq)
            em.op("ve", lambda ve, sl2=sl2, cb=cb: ve.reduce_sum(
                col[:, cb + 1:cb + 2], x1t[:, sl2], axis=AX.X))
            em.op("ve", lambda ve, cb=cb: ve.tensor_scalar_mul(
                col[:, cb + 2:cb + 3], col[:, cb + 1:cb + 2], 1.0 / D))
            em.op("ve", lambda ve, cb=cb: ve.tensor_tensor(
                col[:, cb + 3:cb + 4], col[:, cb + 2:cb + 3], col[:, cb + 2:cb + 3],
                ALU.mult))
            em.op("ve", lambda ve, cb=cb: ve.tensor_scalar_mul(
                col[:, cb + 4:cb + 5], col[:, cb:cb + 1], 1.0 / D))
            em.op("ve", lambda ve, cb=cb: ve.tensor_sub(
                col[:, cb + 5:cb + 6], col[:, cb + 4:cb + 5], col[:, cb + 3:cb + 4]))
            em.op("ve", lambda ve, cb=cb: ve.tensor_scalar_add(
                col[:, cb + 6:cb + 7], col[:, cb + 5:cb + 6], EPS))
            t_rec = em.op("ve", lambda ve, cb=cb: ve.reciprocal(
                col[:, cb + 7:cb + 8], col[:, cb + 6:cb + 7]))
            em.wait("sc", t_rec)
            t_rstd = em.op("sc", lambda sc, cb=cb: sc.activation(
                out=col[:, cb + 6:cb + 7], in_=col[:, cb + 7:cb + 8], func=AF.Sqrt))
            em.wait("ve", t_rstd)
            em.wait("ve", lu.get(("xn1", sl2)))
            t_xn1 = em.op("ve", lambda ve, sl2=sl2, cb=cb: ve.tensor_scalar(
                xn1t[:, sl2], x1t[:, sl2], col[:, cb + 2:cb + 3], col[:, cb + 6:cb + 7],
                ALU.subtract, ALU.mult))
            lu[("x1", sl2)] = t_xn1
            lu[("lncol", sl2)] = t_xn1
            em.wait("sy", t_xn1)
            t_xn1d = em.dma("sy", lambda sy, ti=ti, sl2=sl2: sy.dma_start(
                out=scr["xn1_scr"][128 * ti:128 * (ti + 1), :], in_=xn1t[:, sl2]))
            lu[("xn1", sl2)] = t_xn1d
            x1_toks.append(t_xn1d)
    return x1_toks[-1]


def stage_s4(nc, em, bufs, wd, scr):
    PS = bufs["ps"]
    PS2 = bufs["ps2"]
    pbf = bufs["ps_bf"]
    identb = bufs["ident"]
    id_tok = bufs["ident_tok"]
    xn1f = bufs["s4_xn1"]
    tmt = bufs["s4_tm"]
    wsb = bufs["s4_w"]
    bsb = bufs["s4_b"]
    q_sb = bufs["s4_q"]
    k_sb = bufs["s4_k"]
    v_fm = bufs["s4_vf"]
    v_tm = bufs["s4_vt"]
    pt = bufs["s4_pt"]
    zfm = bufs["s4_zfm"]
    projsb = bufs["s4_proj"]
    ones64 = bufs["s4_ones"]
    rrow = bufs["s4_r"]
    rbf = bufs["s4_rbf"]
    bcsb = bufs["s4_bc"]
    s1b = bufs["s4_s1"]
    lu = {}

    bias_toks = []
    for x, name in ((0, "bq"), (1, "bk"), (2, "bv")):
        src = wd[name].rearrange("(c p) o -> p c o", p=128)
        bias_toks.append(em.dma("sy", lambda sy, x=x, src=src: sy.dma_start(
            out=bsb[:, x, :], in_=src[:, :, 0], allow_slow_non_contiguous=True)))
    proj_toks = [em.dma("gp", lambda gp, kc=kc: gp.dma_start(
        out=projsb[:, kc], in_=wd["proj"][128 * kc:128 * (kc + 1), :]))
        for kc in range(8)]
    t_ones = em.op("gp", lambda gp: gp.memset(ones64[:], 1.0))
    em.op("gp", lambda gp: gp.memset(v_tm[:, :, :, 64:65], 1.0))
    t_vones = em.op("gp", lambda gp: gp.memset(v_tm[:, :, :, 129:130], 1.0))

    for s in range(2):
        soff = SEQ * s
        for ti in range(16):
            sl = ti % 2
            em.wait("sy", lu.get(("tmt", sl)))
            t_ld = em.dma("sy", lambda sy, ti=ti, sl=sl, soff=soff: sy.dma_start(
                out=tmt[:, sl], in_=scr["xn1_scr"][soff + 128 * ti:soff + 128 * (ti + 1), :]))
            em.wait("te", t_ld)
            em.wait("te", id_tok)
            em.wait("te", lu.get(("pbt", ti % 2)))
            pb = pbf[ti % 2]
            def f_tr(te, sl=sl, pb=pb):
                for c in range(8):
                    ins = te.transpose(pb[:, 128 * c:128 * (c + 1)],
                                       tmt[:, sl, 128 * c:128 * (c + 1)], identb[:])
                return ins
            t_tr = em.op("te", f_tr)
            lu[("tmt", sl)] = t_tr
            em.wait("sc", t_tr)
            em.wait("sc", lu.get("xn1f_used"))
            t_ev = em.op("sc", lambda sc, ti=ti, pb=pb: sc.activation(
                out=xn1f[:, :, 128 * ti:128 * (ti + 1)],
                in_=pb[:].rearrange("p (c f) -> p c f", f=128), func=AF.Copy))
            lu[("pbt", ti % 2)] = t_ev
        t_xn1f = t_ev

        for hp in range(8):
            pp = 0
            w_toks = []
            em.wait("gp", lu.get(("wsb", pp)))
            for x, name in ((0, "WqT"), (1, "WkT"), (2, "WvT")):
                src = wd[name].rearrange("(c p) m -> p c m", p=128)
                w_toks.append(em.dma("gp", lambda gp, x=x, src=src, hp=hp, pp=pp:
                                     gp.dma_start(
                    out=wsb[:, pp, x], in_=src[:, :, 128 * hp:128 * (hp + 1)])))
            for x, dst in ((0, q_sb), (1, k_sb), (2, v_fm)):
                for tb in range(4):
                    bank = PS[tb % 2]
                    for tok in w_toks:
                        em.wait("te", tok)
                    em.wait("te", t_xn1f)
                    em.wait("te", lu.get(("bank", tb % 2)))
                    def f_mm(te, x=x, tb=tb, bank=bank, pp=pp):
                        for kd in range(8):
                            ins = te.matmul(bank[:], wsb[:, pp, x, kd],
                                            xn1f[:, kd, 512 * tb:512 * (tb + 1)],
                                            start=(kd == 0), stop=(kd == 7))
                        return ins
                    t_mm = em.op("te", f_mm)
                    em.wait("ve", t_mm)
                    for tok in bias_toks:
                        em.wait("ve", tok)
                    em.wait("ve", lu.get(("qkv", x, pp)))
                    t_qe = em.op("ve", lambda ve, x=x, tb=tb, dst=dst, bank=bank,
                                 hp=hp, pp=pp:
                                 ve.tensor_scalar_add(dst[:, pp, 512 * tb:512 * (tb + 1)],
                                                      bank[:], bsb[:, x, hp:hp + 1]))
                    lu[("bank", tb % 2)] = t_qe
                lu[("qkv", x, pp)] = t_qe
            t_qkv = t_qe
            for kc in range(16):
                pb = pbf[kc % 2]
                em.wait("te", t_qkv)
                em.wait("te", lu.get(("pbt", kc % 2)))
                em.wait("te", lu.get(("vtm_used", pp)))
                t_vt = em.op("te", lambda te, kc=kc, pb=pb, pp=pp: te.transpose(
                    pb[:, 0:128], v_fm[:, pp, 128 * kc:128 * (kc + 1)], identb[:]))
                em.wait("sc", t_vt)
                em.wait("sc", lu.get(("vtm_used", pp)))
                vt = v_tm[:, pp, kc]
                vdst = bass.AP(vt.tensor, vt.offset, [vt.ap[0], [65, 2], [1, 64]])
                t_ve = em.op("sc", lambda sc, pb=pb, vdst=vdst: sc.activation(
                    out=vdst, in_=pb[:, 0:128].rearrange("p (a b) -> p a b", b=64),
                    func=AF.Copy))
                lu[("pbt", kc % 2)] = t_ve
                lu[("qkv", 2, pp)] = t_vt
            t_vtm = t_ve
            for qg in range(4):
                nkc = 4 * qg + 4
                for kc in range(nkc):
                    wb = PS2[kc % 2]
                    em.wait("te", t_qkv)
                    em.wait("te", lu.get(("stps", kc % 2)))
                    em.wait("te", lu.get(("bank", 2 * (kc % 2))))
                    em.wait("te", lu.get(("bank", 2 * (kc % 2) + 1)))
                    def f_qk(te, kc=kc, qg=qg, wb=wb, pp=pp):
                        te.matmul(wb[:, 0:512], k_sb[0:64, pp, 128 * kc:128 * (kc + 1)],
                                  q_sb[0:64, pp, 512 * qg:512 * (qg + 1)],
                                  start=True, stop=True, tile_position=(0, 0))
                        return te.matmul(wb[:, 512:1024],
                                         k_sb[64:128, pp, 128 * kc:128 * (kc + 1)],
                                         q_sb[64:128, pp, 512 * qg:512 * (qg + 1)],
                                         start=True, stop=True, tile_position=(64, 0))
                    t_st = em.op("te", f_qk)
                    em.wait("sc", t_st)
                    em.wait("sc", lu.get(("ptw", pp, kc)))
                    def f_ex(sc, kc=kc, wb=wb, pp=pp):
                        return sc.activation(
                            out=pt[:, pp, :, kc, :],
                            in_=wb.rearrange("p (h q) -> p h q", q=512),
                            func=AF.Exp, scale=0.125)
                    t_ex = em.op("sc", f_ex)
                    lu[("stps", kc % 2)] = t_ex
                    lu[("bank", 2 * (kc % 2))] = t_ex
                    lu[("bank", 2 * (kc % 2) + 1)] = t_ex
                    if kc >= 4 * qg:
                        em.wait("gp", t_ex)
                        def f_msk(gp, kc=kc, qg=qg, pp=pp):
                            base = 512 * qg - 128 * kc
                            return gp.affine_select(
                                out=pt[:, pp, :, kc, :], in_=pt[:, pp, :, kc, :],
                                pattern=[[0, 2], [1, 512]], compare_op=ALU.is_ge,
                                fill=0.0, base=base, channel_multiplier=-1)
                        t_msk = em.op("gp", f_msk)
                        lu[("ptm", pp, kc)] = t_msk
                    else:
                        lu[("ptm", pp, kc)] = t_ex
                for hd in range(2):
                    psz = PS[4 + hd]
                    em.wait("te", lu.get(("zps", hd)))
                    em.wait("te", t_vtm)
                    for kc in range(nkc):
                        em.wait("te", lu[("ptm", pp, kc)])
                    def f_av(te, hd=hd, nkc=nkc, psz=psz, pp=pp):
                        for kc in range(nkc):
                            ins = te.matmul(psz[0:65, :],
                                            v_tm[:, pp, kc, 65 * hd:65 * (hd + 1)],
                                            pt[:, pp, hd, kc], start=(kc == 0),
                                            stop=(kc == nkc - 1))
                        return ins
                    t_av = em.op("te", f_av)
                    for kc in range(nkc):
                        lu[("ptw", pp, kc)] = t_av
                    em.wait("sc", t_av)
                    em.wait("sc", lu.get("rrow"))
                    em.op("sc", lambda sc, psz=psz: sc.activation(
                        out=rrow[:], in_=psz[64:65, :], func=AF.Ln, scale=8.0))
                    t_rb = em.op("sc", lambda sc, pp=pp: sc.activation(
                        out=rbf[:, pp], in_=rrow[:], func=AF.Exp, scale=-1.0))
                    em.wait("te", t_rb)
                    em.wait("te", t_ones)
                    t_bc = em.op("te", lambda te, psz=psz, pp=pp: te.matmul(
                        psz[64:128, :], ones64[:], rbf[:, pp], start=True, stop=True,
                        tile_position=(0, 64)))
                    em.wait("sc", t_bc)
                    em.wait("sc", lu.get(("bcsb", pp)))
                    t_bce = em.op("sc", lambda sc, psz=psz, pp=pp: sc.activation(
                        out=bcsb[:, pp], in_=psz[64:128, :][0:64, :], func=AF.Copy))
                    lu["rrow"] = t_bc
                    em.wait("ve", t_bce)
                    em.wait("ve", lu.get("zfm_used"))
                    t_zf = em.op("ve", lambda ve, hd=hd, hp=hp, qg=qg, psz=psz, pp=pp:
                                 ve.tensor_mul(
                                     zfm[64 * hd:64 * (hd + 1), hp, 512 * qg:512 * (qg + 1)],
                                     psz[0:64, :], bcsb[:, pp]))
                    lu[("bcsb", pp)] = t_zf
                    lu[("zps", hd)] = t_zf
            lu[("vtm_used", pp)] = t_av
            lu[("wsb", pp)] = t_st
            lu[("qkv", 0, pp)] = t_st
            lu[("qkv", 1, pp)] = t_st
        lu["xn1f_used"] = t_st
        for ti in range(16):
            for dn in range(2):
                bank = PS[(2 * ti + dn) % 2]
                for tok in proj_toks:
                    em.wait("te", tok)
                em.wait("te", t_zf)
                em.wait("te", lu.get(("bank", (2 * ti + dn) % 2)))
                def f_pr(te, ti=ti, dn=dn, bank=bank):
                    for kc in range(8):
                        ins = te.matmul(bank[:], zfm[:, kc, 128 * ti:128 * (ti + 1)],
                                        projsb[:, kc, 512 * dn:512 * (dn + 1)],
                                        start=(kc == 0), stop=(kc == 7))
                    return ins
                t_pr = em.op("te", f_pr)
                em.wait("sc", t_pr)
                em.wait("sc", lu.get(("s1b", ti % 2)))
                t_se = em.op("sc", lambda sc, ti=ti, dn=dn, bank=bank: sc.activation(
                    out=s1b[:, ti % 2, 512 * dn:512 * (dn + 1)], in_=bank[:], func=AF.Copy))
                lu[("bank", (2 * ti + dn) % 2)] = t_se
            em.wait("sy", t_se)
            t_sd = em.dma("sy", lambda sy, ti=ti, s=s, soff=soff: sy.dma_start(
                out=scr["s1_scr"][soff + 128 * ti:soff + 128 * (ti + 1), :],
                in_=s1b[:, ti % 2]))
            lu[("s1b", ti % 2)] = t_sd
        lu["zfm_used"] = t_pr
    return t_sd


def stage_s5(nc, em, bufs, wd, scr, out_d):
    PS = bufs["ps"]
    pbf = bufs["ps_bf"]
    identb = bufs["ident"]
    id_tok = bufs["ident_tok"]
    wq = bufs["s5_wq"]; wk = bufs["s5_wk"]; wi = bufs["s5_wi"]; w1 = bufs["s5_w1"]
    wo = bufs["s5_wo"]
    vw = bufs["s5_vw"]
    us = bufs["s5_u"]
    bb = bufs["s5_bb"]
    rw2 = bufs["s5_rw2"]
    rb1c = bufs["s5_rb1"]
    ebc = bufs["s5_ebc"]
    alc = bufs["s5_al"]
    invn = bufs["s5_invn"]
    tmt = bufs["s4_tm"]
    xnb = bufs["s5_xnb"]
    qb = bufs["s5_qb"]
    kb = bufs["s5_kb"]
    iqb = bufs["s5_iqb"]
    ikb = bufs["s5_ikb"]
    xv = bufs["s5_xv"]
    yw = bufs["s5_yw"]
    cum = bufs["s5_cum"]
    carry = bufs["s5_carry"]
    hb = bufs["s5_hb"]
    lg = bufs["s5_lg"]
    lcol = bufs["s5_lcol"]
    gwb2 = bufs["s5_gwb"]
    wT = bufs["s5_wt"]
    wexp = bufs["s5_wexp"]
    prodw = bufs["s5_prodw"]
    osum = bufs["s5_osum"]
    x1t = bufs["s5_x1t"]
    s1t = bufs["s5_s1t"]
    g1b = bufs["s5_g1"]
    lu = {}

    wtoks = []
    for dst, name in ((wq, "W_QT"), (wk, "W_KT"), (wi, "W_invT"), (w1, "rw1T"),
                      (wo, "W_OT")):
        for kd in range(8):
            wtoks.append(em.dma("gp", lambda gp, dst=dst, name=name, kd=kd:
                                 gp.dma_start(out=dst[:, kd],
                                              in_=wd[name][128 * kd:128 * (kd + 1), :])))
    for i, name in enumerate(("VfT", "WfT", "ViT", "WiT")):
        for kd in range(8):
            wtoks.append(em.dma("gp", lambda gp, i=i, name=name, kd=kd:
                                 gp.dma_start(out=vw[:, i, kd],
                                              in_=wd[name][128 * kd:128 * (kd + 1), :])))
    for i, name in enumerate(("Uf", "Ui")):
        for kc in range(4):
            wtoks.append(em.dma("gp", lambda gp, i=i, name=name, kc=kc:
                                 gp.dma_start(out=us[:, i, kc],
                                              in_=wd[name][128 * kc:128 * (kc + 1), :])))
    for i, name in enumerate(("b_fwd", "b_inv")):
        wtoks.append(em.dma("gp", lambda gp, i=i, name=name: gp.dma_start(
            out=bb[:, i], in_=wd[name][:, :])))
    for kd in range(8):
        wtoks.append(em.dma("gp", lambda gp, kd=kd: gp.dma_start(
            out=rw2[:, kd], in_=wd["rw2T"][128 * kd:128 * (kd + 1), :])))
    src = wd["rb1"].rearrange("(c p) o -> p c o", p=128)
    t_rb1 = em.dma("sy", lambda sy, src=src: sy.dma_start(
        out=rb1c[:], in_=src[:, :, 0], allow_slow_non_contiguous=True))
    em.wait("sc", t_rb1)
    e0 = wd["rb2"]
    t_eb1 = em.dma("sy", lambda sy, e0=e0: sy.dma_start(
        out=ebc[:, 0], in_=bass.AP(e0.tensor, e0.offset, [[0, 128], [1, 8]])))
    e1 = wd["expert_bias"]
    t_eb2 = em.dma("sy", lambda sy, e1=e1: sy.dma_start(
        out=ebc[:, 1], in_=bass.AP(e1.tensor, e1.offset, [[0, 128], [1, 8]])))
    a0 = wd["alpha_bi"]
    t_al = em.dma("sy", lambda sy, a0=a0: sy.dma_start(
        out=alc[:], in_=bass.AP(a0.tensor, a0.offset, [[0, 128], [1, 1]])))
    em.wait("ve", t_eb1)
    em.wait("ve", t_eb2)
    t_ebs = em.op("ve", lambda ve: ve.tensor_add(ebc[:, 0], ebc[:, 0], ebc[:, 1]))
    t_io = em.op("gp", lambda gp: gp.iota(
        invn[:], pattern=[[0, 2], [128, 16]], base=1, channel_multiplier=1,
        allow_small_or_imprecise_dtypes=True))
    em.wait("ve", t_io)
    t_invn = em.op("ve", lambda ve: ve.reciprocal(invn[:], invn[:]))
    t_cz = em.op("ve", lambda ve: ve.memset(carry[:], 0.0))
    em.op("ve", lambda ve: ve.memset(bufs["zcol128"][:], 0.0))
    lu["carry_z"] = t_cz

    def proj_into(dst, wsb, rhs_buf, tag, nk=8, bias_col=None, act=AF.Copy):
        for dq in range(8):
            bank = PS[dq % 4]
            em.wait("te", lu.get(("bank", dq % 4)))
            def f(te, dq=dq, bank=bank, wsb=wsb, rhs_buf=rhs_buf):
                for kd in range(8):
                    ins = te.matmul(bank[:, 0:512], wsb[:, kd, 128 * dq:128 * (dq + 1)],
                                    rhs_buf[:, kd], start=(kd == 0), stop=(kd == 7))
                return ins
            t_mm = em.op("te", f)
            em.wait("sc", t_mm)
            em.wait("sc", lu.get(("w5", tag)))
            kw = {}
            if bias_col is not None:
                kw["bias"] = bias_col[:, dq:dq + 1]
            t_ev = em.op("sc", lambda sc, dq=dq, bank=bank, dst=dst, kw=kw, act=act:
                         sc.activation(out=dst[:, dq], in_=bank[:, 0:512], func=act, **kw))
            lu[("bank", dq % 4)] = t_ev
        return t_ev

    t_x2d = None
    for b in range(8):
        t0 = 512 * b
        ti0 = 4 * b
        for j in range(4):
            sl = j % 2
            em.wait("sy", lu.get(("tmt5", sl)))
            t_ld = em.dma("sy", lambda sy, t0=t0, j=j, sl=sl: sy.dma_start(
                out=tmt[:, sl], in_=scr["xn1_scr"][t0 + 128 * j:t0 + 128 * (j + 1), :]))
            em.wait("te", t_ld)
            em.wait("te", id_tok)
            em.wait("te", lu.get(("pb5", j % 2)))
            em.wait("te", lu.get("xnb_used"))
            pb = pbf[j % 2]
            def f_tr(te, sl=sl, pb=pb):
                for c in range(8):
                    ins = te.transpose(pb[:, 128 * c:128 * (c + 1)],
                                       tmt[:, sl, 128 * c:128 * (c + 1)], identb[:])
                return ins
            t_tr = em.op("te", f_tr)
            lu[("tmt5", sl)] = t_tr
            em.wait("sc", t_tr)
            em.wait("sc", lu.get("xnb_used"))
            t_xe = em.op("sc", lambda sc, j=j, pb=pb: sc.activation(
                out=xnb[:, :, 128 * j:128 * (j + 1)],
                in_=pb[:].rearrange("p (c f) -> p c f", f=128), func=AF.Copy))
            lu[("pb5", j % 2)] = t_xe
        t_xnb = t_xe
        for tok in wtoks:
            em.wait("te", tok)
        em.wait("te", t_xnb)
        lu[("w5", "q")] = lu.get("qb_used")
        t_q = proj_into(qb, wq, xnb, "q")
        lu[("w5", "k")] = lu.get("kb_used")
        t_k = proj_into(kb, wk, xnb, "k")

        for br in range(2):
            if br == 1:
                em.wait("te", t_q)
                lu[("w5", "iq")] = lu.get("iqb_used")
                t_iq = proj_into(iqb, wi, qb, "iq")
                em.wait("te", t_k)
                lu[("w5", "ik")] = lu.get("br0_om")
                t_ik = proj_into(ikb, wi, kb, "ik")
            Qs = qb if br == 0 else iqb
            Ks = kb if br == 0 else ikb
            t_qs = t_q if br == 0 else t_iq
            t_ks = t_k if br == 0 else t_ik
            voff = 2 * br
            em.wait("te", t_qs)
            lu[("w5", "h")] = lu.get("hb_used")
            t_h = proj_into(hb, w1, Qs, "h", bias_col=rb1c, act=AF.Gelu)
            for qt in range(4):
                bank = PS[4]
                em.wait("te", t_h)
                em.wait("te", lu.get(("bank", 4)))
                def f_lg(te, qt=qt, bank=bank):
                    for rhc in range(8):
                        ins = te.matmul(bank[:, 0:8], hb[:, rhc, 128 * qt:128 * (qt + 1)],
                                        rw2[:, rhc], start=(rhc == 0), stop=(rhc == 7))
                    return ins
                t_lgm = em.op("te", f_lg)
                em.wait("ve", t_lgm)
                em.wait("ve", t_ebs)
                em.wait("ve", lu.get("lg_used"))
                em.op("ve", lambda ve, qt=qt, bank=bank: ve.tensor_add(
                    lg[:, qt], bank[:, 0:8], ebc[:, 0]))
                lu[("bank", 4)] = em.tok("ve")
                em.op("ve", lambda ve, qt=qt: ve.reduce_max(
                    lcol[:, 0:1], lg[:, qt], axis=AX.X, negate=True))
                t_nm = em.op("ve", lambda ve: ve.engine_nop()) if False else None
                em.wait("sc", em.tok("ve"))
                t_e = em.op("sc", lambda sc, qt=qt: sc.activation(
                    out=lg[:, qt], in_=lg[:, qt], func=AF.Exp, bias=lcol[:, 0:1]))
                em.wait("ve", t_e)
                em.op("ve", lambda ve, qt=qt: ve.reduce_sum(
                    lcol[:, 1:2], lg[:, qt], axis=AX.X))
                em.op("ve", lambda ve: ve.reciprocal(lcol[:, 2:3], lcol[:, 1:2]))
                t_wg = em.op("ve", lambda ve, qt=qt: ve.tensor_scalar_mul(
                    lg[:, qt], lg[:, qt], lcol[:, 2:3]))
                em.wait("ve", lu.get("gwb2"))
                t_gc = em.op("ve", lambda ve, qt=qt: ve.tensor_copy(gwb2[:], lg[:, qt]))
                em.wait("te", t_gc)
                em.wait("te", id_tok)
                em.wait("te", lu.get("wtps5"))
                t_wt = em.op("te", lambda te, qt=qt: te.transpose(
                    pbf[0][0:8, 0:128], gwb2[:], identb[:]))
                lu["gwb2"] = t_wt
                em.wait("ve", t_wt)
                em.wait("ve", lu.get(("wT_used", qt)))
                t_wte = em.op("ve", lambda ve, qt=qt: ve.tensor_copy(
                    wT[:, qt], pbf[0][0:8, 0:128]))
                lu["wtps5"] = t_wte
                lu["lg_used"] = t_wte
            t_wT = t_wte
            em.wait("sy", t_wT)
            t_wd = em.dma("sy", lambda sy, br=br, t0=t0: sy.dma_start(
                out=scr["wgt_scr"][br][:, t0:t0 + 512],
                in_=wT[:].rearrange("k q f -> k (q f)")))
            lu[("wT_used", 0)] = t_wd
            lu[("wT_used", 1)] = t_wd
            wg_toks = []
            for krc in range(4):
                for kk in range(2):
                    em.wait("sy", t_wd)
                    em.wait("sy", lu.get(("wexp", krc, kk)))
                    wsrc = scr["wgt_scr"]
                    woff = (br * 8 + 2 * krc + kk) * T + t0
                    wg_toks.append(em.dma("sy", lambda sy, krc=krc, kk=kk, woff=woff,
                                          wsrc=wsrc: sy.dma_start(
                        out=wexp[64 * kk:64 * (kk + 1), krc, :],
                        in_=bass.AP(wsrc.tensor, wsrc.offset + woff, [[0, 64], [1, 512]]))))
            for which, (dst, widx, rhs_buf, t_rhs) in enumerate(
                    ((xv, voff + 0, Qs, t_qs), (yw, voff + 1, Ks, t_ks))):
                for krc in range(4):
                    bank = PS[4 + (krc % 2)]
                    em.wait("te", t_rhs)
                    em.wait("te", lu.get(("bank", 4 + krc % 2)))
                    def f(te, krc=krc, widx=widx, rhs_buf=rhs_buf, bank=bank):
                        for kd in range(8):
                            ins = te.matmul(bank[:, 0:512],
                                            vw[:, widx, kd, 128 * krc:128 * (krc + 1)],
                                            rhs_buf[:, kd], start=(kd == 0), stop=(kd == 7))
                        return ins
                    t_mm = em.op("te", f)
                    em.wait("ve", t_mm)
                    em.wait("ve", lu.get(("xvyw", which, krc)))
                    t_ev = em.op("ve", lambda ve, krc=krc, dst=dst, bank=bank:
                                 ve.tensor_copy(dst[:, krc], bank[:, 0:512]))
                    lu[("bank", 4 + krc % 2)] = t_ev
                    lu[("xvyw2", which, krc)] = t_ev
            if b == 4 and br == 0:
                em.op("ve", lambda ve: ve.memset(carry[:], 0.0))
            for krc in range(4):
                em.wait("ve", lu[("xvyw2", 1, krc)])
                def f_scan(ve, br=br, krc=krc):
                    zb = bufs["zcol128"][:]
                    z0 = bass.AP(zb.tensor, zb.offset, [zb.ap[0], [0, 512]])
                    return ve.tensor_tensor_scan(
                        out=cum[:, krc], data0=yw[:, krc], data1=z0,
                        initial=carry[:, br, krc:krc + 1], op0=ALU.add, op1=ALU.add)
                t_sc = em.op("ve", f_scan)
                t_cu = em.op("ve", lambda ve, br=br, krc=krc: ve.tensor_copy(
                    carry[:, br, krc:krc + 1], cum[:, krc, 511:512]))
                lu[("cum", krc)] = t_cu
            lu["carry_z"] = t_cu
            pw_toks = []
            for krc in range(4):
                em.wait("ve", lu[("cum", krc)])
                em.wait("ve", lu[("xvyw2", 0, krc)])
                em.wait("ve", wg_toks[2 * krc])
                em.wait("ve", wg_toks[2 * krc + 1])
                em.wait("ve", lu.get(("prodw", krc)))
                em.op("ve", lambda ve, krc=krc: ve.tensor_mul(
                    xv[:, krc], xv[:, krc], cum[:, krc]))
                lu[("xvyw", 0, krc)] = em.tok("ve")
                t_pw = em.op("ve", lambda ve, krc=krc: ve.tensor_mul(
                    prodw[:, krc], xv[:, krc], wexp[:, krc]))
                lu[("xvyw", 1, krc)] = t_pw
                lu[("wexp", krc, 0)] = t_pw
                lu[("wexp", krc, 1)] = t_pw
                pw_toks.append(t_pw)
            for dq in range(8):
                bank = PS[4 + dq % 2]
                for tok in pw_toks:
                    em.wait("te", tok)
                em.wait("te", t_wT)
                em.wait("te", lu.get(("bank", 4 + dq % 2)))
                def f_o(te, dq=dq, br=br, bank=bank):
                    for krc in range(4):
                        te.matmul(bank[:, 0:512], us[:, br, krc, 128 * dq:128 * (dq + 1)],
                                  prodw[:, krc], start=(krc == 0), stop=False)
                    return te.matmul(bank[:, 0:512], bb[:, br, 128 * dq:128 * (dq + 1)],
                                     wT[:].rearrange("k q f -> k (q f)"),
                                     start=False, stop=True)
                t_om = em.op("te", f_o)
                if br == 0:
                    em.wait("sc", t_om)
                    em.wait("sc", lu.get(("osum", dq)))
                    t_oe = em.op("sc", lambda sc, dq=dq, bank=bank: sc.activation(
                        out=osum[:, dq], in_=bank[:, 0:512], func=AF.Copy))
                else:
                    em.wait("ve", t_om)
                    em.wait("ve", t_al)
                    em.wait("ve", lu.get(("osum2", dq)))
                    t_oe = em.op("ve", lambda ve, dq=dq, bank=bank: ve.scalar_tensor_tensor(
                        osum[:, dq], bank[:, 0:512], alc[:, 0:1], osum[:, dq],
                        ALU.mult, ALU.add))
                    lu[("osum", dq)] = t_oe
                lu[("bank", 4 + dq % 2)] = t_oe
                lu[("osum2", dq)] = t_oe
            for krc in range(4):
                lu[("prodw", krc)] = t_om
            lu["qb_used"] = t_om if br == 1 else lu.get("qb_used")
        lu["kb_used"] = t_om
        lu["iqb_used"] = t_om
        lu["ikb_used"] = t_om
        lu["qb_used"] = t_om
        lu["hb_used"] = t_om
        lu["xnb_used"] = t_ik
        for qt in range(4):
            sl = 0
            em.wait("sy", lu.get(("tmt5", 0)))
            em.wait("sy", lu.get(("tmt5", 1)))
            em.wait("sy", lu.get(("x1t", sl)))
            t_x1 = em.dma("sy", lambda sy, t0=t0, qt=qt, sl=sl: sy.dma_start(
                out=x1t[:, sl], in_=scr["x1_scr"][t0 + 128 * qt:t0 + 128 * (qt + 1), :]))
            t_s1 = em.dma("sy", lambda sy, t0=t0, qt=qt, sl=sl: sy.dma_start(
                out=s1t[:, sl], in_=scr["s1_scr"][t0 + 128 * qt:t0 + 128 * (qt + 1), :]))
            for dn in range(2):
                bank = PS[dn % 2]
                for dq_ in range(8):
                    em.wait("te", lu[("osum2", dq_)])
                em.wait("te", lu.get(("bank", dn % 2)))
                def f_wo(te, qt=qt, dn=dn, bank=bank):
                    for kd in range(8):
                        ins = te.matmul(bank[:], osum[:, kd, 128 * qt:128 * (qt + 1)],
                                        wo[:, kd, 512 * dn:512 * (dn + 1)],
                                        start=(kd == 0), stop=(kd == 7))
                    return ins
                t_wom = em.op("te", f_wo)
                em.wait("sc", t_wom)
                em.wait("sc", t_invn)
                em.wait("sc", lu.get("g1_used"))
                t_g1 = em.op("sc", lambda sc, dn=dn, bank=bank, ti0=ti0, qt=qt:
                             sc.activation(out=g1b[:, 512 * dn:512 * (dn + 1)],
                                           in_=bank[:], func=AF.Silu,
                                           scale=invn[:, ti0 + qt:ti0 + qt + 1]))
                lu[("bank", dn % 2)] = t_g1
            em.wait("ve", t_g1)
            em.wait("ve", t_s1)
            t_sg = em.op("ve", lambda ve, sl=sl: ve.tensor_mul(
                g1b[:], s1t[:, sl], g1b[:]))
            em.wait("ve", t_x1)
            em.wait("ve", lu.get(("x2", sl)))
            t_x2 = em.op("ve", lambda ve, sl=sl: ve.tensor_add(
                x1t[:, sl], x1t[:, sl], g1b[:]))
            lu["g1_used"] = t_x2
            lu[("x1t", sl)] = t_x2
            em.wait("sy", t_x2)
            t_x2d = em.dma("sy", lambda sy, t0=t0, qt=qt, sl=sl: sy.dma_start(
                out=out_d[t0 + 128 * qt:t0 + 128 * (qt + 1), :], in_=x1t[:, sl]))
            lu[("x2", sl)] = t_x2d
            lu[("tmt5", 0)] = t_x2d
            lu[("tmt5", 1)] = t_x2d
        lu["osum_used"] = t_wom
        for dq in range(8):
            lu[("osum", dq)] = t_wom
    return t_x2d



ARENA_UNITS = 106240


def build_fused_program():
    nc = bass.Bass("TRN2", target_bir_lowering=False)
    wd = {}
    wd["u_d"] = nc.dram_tensor("u_d", [T, D], F32, kind="ExternalInput").ap()
    wd["W_inT"] = nc.dram_tensor("W_inT", [1024, 4368], F32, kind="ExternalInput").ap()
    wd["conv_w"] = nc.dram_tensor("conv_w", [NJC, 128, 4], F32, kind="ExternalInput").ap()
    wd["conv_b"] = nc.dram_tensor("conv_b", [NJC, 128, 1], F32, kind="ExternalInput").ap()
    wd["dt_bias"] = nc.dram_tensor("dt_bias", [16, 1], F32, kind="ExternalInput").ap()
    wd["A_log"] = nc.dram_tensor("A_log", [16, 1], F32, kind="ExternalInput").ap()
    wd["Dh"] = nc.dram_tensor("Dh", [1, 16], F32, kind="ExternalInput").ap()
    wd["norm_weight"] = nc.dram_tensor("norm_weight", [1, DIN], F32, kind="ExternalInput").ap()
    wd["W_outT"] = nc.dram_tensor("W_outT", [DIN, D], F32, kind="ExternalInput").ap()
    wd["Wg"] = nc.dram_tensor("Wg", [1024, 8], F32, kind="ExternalInput").ap()
    wd["bg"] = nc.dram_tensor("bg", [1, 8], F32, kind="ExternalInput").ap()
    wd["We"] = nc.dram_tensor("We", [8, 1024, 1024], F32, kind="ExternalInput").ap()
    wd["be"] = nc.dram_tensor("be", [8, 1024], F32, kind="ExternalInput").ap()
    for n_ in ("WqT", "WkT", "WvT", "proj", "W_QT", "W_KT", "W_invT", "rw1T", "W_OT"):
        wd[n_] = nc.dram_tensor(n_, [1024, 1024], F32, kind="ExternalInput").ap()
    for n_ in ("bq", "bk", "bv", "rb1"):
        wd[n_] = nc.dram_tensor(n_, [1024, 1], F32, kind="ExternalInput").ap()
    for n_ in ("VfT", "WfT", "ViT", "WiT"):
        wd[n_] = nc.dram_tensor(n_, [1024, 512], F32, kind="ExternalInput").ap()
    for n_ in ("Uf", "Ui"):
        wd[n_] = nc.dram_tensor(n_, [512, 1024], F32, kind="ExternalInput").ap()
    for n_ in ("b_fwd", "b_inv"):
        wd[n_] = nc.dram_tensor(n_, [8, 1024], F32, kind="ExternalInput").ap()
    wd["rw2T"] = nc.dram_tensor("rw2T", [1024, 8], F32, kind="ExternalInput").ap()
    wd["rb2"] = nc.dram_tensor("rb2", [1, 8], F32, kind="ExternalInput").ap()
    wd["expert_bias"] = nc.dram_tensor("expert_bias", [1, 8], F32, kind="ExternalInput").ap()
    wd["alpha_bi"] = nc.dram_tensor("alpha_bi", [1, 1], F32, kind="ExternalInput").ap()
    out_d = nc.dram_tensor("x2", [T, D], F32, kind="ExternalOutput").ap()

    scr = {
        "xn_scr": nc.dram_tensor("xn_scr", [D, T], BF16, kind="Internal").ap(),
        "z_scr": nc.dram_tensor("z_scr", [T, DIN], BF16, kind="Internal").ap(),
        "xfm_scr": nc.dram_tensor("xfm_scr", [DIN, T], BF16, kind="Internal").ap(),
        "bfm_scr": nc.dram_tensor("bfm_scr", [128, T], BF16, kind="Internal").ap(),
        "cfm_scr": nc.dram_tensor("cfm_scr", [128, T], BF16, kind="Internal").ap(),
        "dtr_scr": nc.dram_tensor("dtr_scr", [16, T], F32, kind="Internal").ap(),
        "acs_scr": nc.dram_tensor("acs_scr", [16, T], F32, kind="Internal").ap(),
        "cdec_scr": nc.dram_tensor("cdec_scr", [2, 16, 8], F32, kind="Internal").ap(),
        "s_scr": nc.dram_tensor("s_scr", [T, D], BF16, kind="Internal").ap(),
        "x1_scr": nc.dram_tensor("x1_scr", [T, D], F32, kind="Internal").ap(),
        "xn1_scr": nc.dram_tensor("xn1_scr", [T, D], BF16, kind="Internal").ap(),
        "s1_scr": nc.dram_tensor("s1_scr", [T, D], BF16, kind="Internal").ap(),
        "wgt_scr": nc.dram_tensor("wgt_scr", [2, 8, T], BF16, kind="Internal").ap(),
    }

    em = Em()
    with contextlib.ExitStack() as stx:
        bufs = {}
        def sb(name, shape, dtype):
            return stx.enter_context(nc.sbuf_tensor(name, shape, dtype))
        bufs["ident"] = sb("ident", [128, 128], BF16)
        bufs["ln_col"] = sb("ln_col", [128, 16], F32)
        AR = sb("AR", [128, ARENA_UNITS], BF16)
        ar = Arena(AR, ARENA_UNITS)
        mk_psum_pool(nc, stx, bufs)
        sems = em.make_sems(nc, stx)
        blk = stx.enter_context(nc.Block())

        bufs["ident_tok"] = build_identity(nc, em, bufs)

        xn = ar.alloc([128, 8, T], BF16)
        bufs["xn_fm"] = xn
        m_xn = ar.mark()
        bufs["ln_x"] = ar.alloc([128, 2, D], F32)
        bufs["ln_sq"] = ar.alloc([128, D], F32)
        bufs["ln_xn"] = ar.alloc([128, 2, D], BF16)
        t_fm = stage_ln(nc, em, bufs, wd["u_d"], xn, key="ln")
        em.barrier()
        dump_fm(em, xn, scr["xn_scr"], 8, T)
        ar.release(m_xn)
        mk_s1_bufs(nc, stx, bufs, alloc=ar.alloc)
        stage_s1(nc, em, bufs, wd, scr, t_fm)
        em.barrier()
        ar.release(0)

        mk_s2_bufs(nc, stx, bufs, alloc=ar.alloc)
        bufs["identf_tok"] = build_identity_f32(nc, em, bufs)
        stage_s2(nc, em, bufs, wd, scr)
        em.barrier()
        ar.release(0)

        xn2 = ar.alloc([128, 8, T], BF16)
        bufs["xn_fm"] = xn2
        xn_tok = load_fm(em, xn2, scr["xn_scr"], 8, T)
        mk_s3_bufs(nc, stx, bufs, alloc=ar.alloc)
        bufs["s1_acc"] = ar.alloc([128, 1024], F32)
        stage_s3(nc, em, bufs, wd, scr, xn_tok)
        em.barrier()
        ar.release(0)

        mk_s4_bufs(nc, stx, bufs, alloc=ar.alloc)
        stage_s4(nc, em, bufs, wd, scr)
        em.barrier()
        ar.release(0)

        mk_s5_bufs(nc, stx, bufs, alloc=ar.alloc)
        out_tok = stage_s5(nc, em, bufs, wd, scr, out_d)
        em.barrier()
        em.wait("sy", out_tok)
        em.emit(nc, blk, sems)
    return nc


_FUSED_NC = None


def kernel_fused(**inputs):
    global _FUSED_NC
    w = host_prep(inputs)
    u = np.asarray(inputs["L0_attn_u"])
    cores = list(range(NCORES))
    if _FUSED_NC is None:
        _FUSED_NC = build_fused_program()
    in_maps = []
    for c in cores:
        m = dict(w)
        m["u_d"] = np.ascontiguousarray(u[2 * c:2 * c + 2].reshape(T, D))
        in_maps.append(m)
    res = run_bass_kernel_spmd(_FUSED_NC, in_maps, core_ids=cores)
    out = np.stack([np.asarray(res.results[c]["x2"]).reshape(2, SEQ, D)
                    for c in cores])
    return out.reshape(16, SEQ, D)



_S0 = None
_S1 = None
_S2 = None
_S3 = None
_S4 = None
_S5 = None


def _programs():
    global _S0, _S1, _S2, _S3, _S4, _S5
    if _S0 is None:
        _S0 = build_s0_program()
        _S1 = build_s1_program()
        _S2 = build_s2_program()
        _S3 = build_s3_program()
        _S4 = build_s4_program()
        _S5 = build_s5_program()
    return _S0, _S1, _S2, _S3, _S4, _S5


FUSED = True


def kernel(**inputs):
    if FUSED:
        return kernel_fused(**inputs)
    w = host_prep(inputs)
    u = np.asarray(inputs["L0_attn_u"])
    cores = list(range(NCORES))
    s0, s1, s2, s3, s4, s5 = _programs()

    u_c = [np.ascontiguousarray(u[2 * c:2 * c + 2].reshape(T, D)) for c in cores]

    r0 = run_bass_kernel_spmd(s0, [dict(u=u_c[c]) for c in cores], core_ids=cores)
    xn = [r0.results[c]["xn_fm"] for c in cores]

    r1 = run_bass_kernel_spmd(s1, [dict(
        xn_fm=xn[c], W_inT=w["W_inT"], conv_w=w["conv_w"], conv_b=w["conv_b"])
        for c in cores], core_ids=cores)

    r2 = run_bass_kernel_spmd(s2, [dict(
        dt_bias=w["dt_bias"], A_log=w["A_log"], Dh=w["Dh"],
        norm_weight=w["norm_weight"], W_outT=w["W_outT"],
        xfm_scr=r1.results[c]["xfm_scr"], bfm_scr=r1.results[c]["bfm_scr"],
        cfm_scr=r1.results[c]["cfm_scr"], dtr_scr=r1.results[c]["dtr_scr"],
        z_scr=r1.results[c]["z_scr"]) for c in cores], core_ids=cores)

    r3 = run_bass_kernel_spmd(s3, [dict(
        xn_fm=xn[c], Wg=w["Wg"], bg=w["bg"], We=w["We"], be=w["be"],
        u_d=u_c[c], s_scr=r2.results[c]["s_scr"]) for c in cores], core_ids=cores)

    r4 = run_bass_kernel_spmd(s4, [dict(
        WqT=w["WqT"], WkT=w["WkT"], WvT=w["WvT"], bq=w["bq"], bk=w["bk"],
        bv=w["bv"], proj=w["proj"], xn1_scr=r3.results[c]["xn1_scr"])
        for c in cores], core_ids=cores)

    in5 = dict(W_QT=w["W_QT"], W_KT=w["W_KT"], W_invT=w["W_invT"], rw1T=w["rw1T"],
               W_OT=w["W_OT"], VfT=w["VfT"], WfT=w["WfT"], ViT=w["ViT"],
               WiT=w["WiT"], Uf=w["Uf"], Ui=w["Ui"], b_fwd=w["b_fwd"],
               b_inv=w["b_inv"], rw2T=w["rw2T"], rb1=w["rb1"], rb2=w["rb2"],
               expert_bias=w["expert_bias"], alpha_bi=w["alpha_bi"])
    r5 = run_bass_kernel_spmd(s5, [dict(
        xn1_scr=r3.results[c]["xn1_scr"],
        x1_scr=r3.results[c]["x1_scr"], s1_scr=r4.results[c]["s1_scr"], **in5)
        for c in cores], core_ids=cores)

    out = np.stack([np.asarray(r5.results[c]["x2"]).reshape(2, SEQ, D)
                    for c in cores])
    return out.reshape(16, SEQ, D)



def dump_fm(em, sbuf_fm, dram_fm, nchunks, ncols, q="sy"):
    dst = dram_fm.rearrange("(c p) t -> p c t", p=128)
    return em.dma(q, lambda e: e.dma_start(out=dst, in_=sbuf_fm[:, 0:nchunks, 0:ncols]))


def build_identity_f32(nc, em, bufs):
    identf = bufs["ident_f32"]
    em.op("gp", lambda gp: gp.memset(identf[:], 1.0))
    return em.op("gp", lambda gp: gp.affine_select(
        out=identf[:], in_=identf[:], pattern=[[-1, 128]],
        compare_op=ALU.is_equal, fill=0.0, base=0, channel_multiplier=1))


def mk_s2_bufs(nc, stx, bufs, alloc=None):
    def sb(name, shape, dtype):
        if alloc is not None:
            return alloc(shape, dtype)
        return stx.enter_context(nc.sbuf_tensor(name, shape, dtype))
    bufs["ident_f32"] = sb("ident_f32", [128, 128], F32)
    regA = sb("s2_regA", [128, 8192], BF16)
    bufs["s2_dtr"] = carve(regA, 0, [16, T], F32)
    bufs["s2_carry"] = carve(regA, 0, [128, 2, 16, 128], F32)
    region = sb("s2_region", [128, 16384], BF16)
    bufs["s2_dt"] = carve(region, 0, [16, T], F32)
    bufs["s2_t16"] = carve(region, 8192, [16, T], F32)
    bufs["s2_wo"] = carve(region, 0, [128, 16, 1024], BF16)
    bufs["s2_acs"] = bufs["s2_dtr"]
    bufs["s2_bcol"] = sb("s2_bcol", [16, 4], F32)
    bufs["zcol16"] = sb("zcol16", [16, 1], F32)
    bufs["s2_cd16"] = sb("s2_cd16", [16, 8], F32)
    bufs["s2_decfm"] = sb("s2_decfm", [16, 256], F32)
    bufs["s2_dtT"] = sb("s2_dtT", [128, 32, 16], F32)
    bufs["s2_nacsT"] = sb("s2_nacsT", [128, 32, 16], F32)
    bufs["s2_eacsT"] = sb("s2_eacsT", [128, 32, 16], F32)
    bufs["s2_sfacT"] = sb("s2_sfacT", [128, 32, 16], F32)
    bufs["s2_cdbc"] = sb("s2_cdbc", [128, 2, 128], F32)
    bufs["s2_dbc"] = sb("s2_dbc", [128, 16], F32)
    bufs["s2_nwbc"] = sb("s2_nwbc", [128, 2048], BF16)
    bufs["s2_xch"] = sb("s2_xch", [128, 2, 16, 256], BF16)
    bufs["s2_xtm"] = sb("s2_xtm", [128, 2, 2, 2048], BF16)
    bufs["s2_bc"] = sb("s2_bc", [128, 2, 256], BF16)
    bufs["s2_cc"] = sb("s2_cc", [128, 2, 256], BF16)
    bufs["s2_btm"] = sb("s2_btm", [128, 2, 2, 128], BF16)
    bufs["s2_bsa"] = sb("s2_bsa", [128, 2, 16, 128], BF16)
    bufs["s2_cbt"] = sb("s2_cbt", [128, 2, 2, 256], BF16)
    bufs["s2_seg1"] = sb("s2_seg1", [128, 2, 8, 256], F32)
    bufs["s2_mta"] = sb("s2_mta", [128, 2, 2, 16, 256], BF16)
    bufs["s2_di"] = sb("s2_di", [128, 16, 128], BF16)
    bufs["s2_carrybf"] = sb("s2_carrybf", [128, 2, 16, 128], BF16)
    bufs["s2_yo"] = sb("s2_yo", [128, 2, 2, 128], F32)
    bufs["s2_ytm"] = sb("s2_ytm", [128, 2, 2048], BF16)
    bufs["s2_ztm"] = sb("s2_ztm", [128, 2, 2048], BF16)
    bufs["s2_yfm"] = sb("s2_yfm", [128, 16, 256], BF16)
    bufs["s2_rcol"] = sb("s2_rcol", [128, 8], F32)
    bufs["s2_sout"] = sb("s2_sout", [128, 2, 1024], BF16)


def build_s2_program():
    nc = bass.Bass("TRN2", target_bir_lowering=False)
    wd = {
        "dt_bias": nc.dram_tensor("dt_bias", [16, 1], F32, kind="ExternalInput").ap(),
        "A_log": nc.dram_tensor("A_log", [16, 1], F32, kind="ExternalInput").ap(),
        "Dh": nc.dram_tensor("Dh", [1, 16], F32, kind="ExternalInput").ap(),
        "norm_weight": nc.dram_tensor("norm_weight", [1, DIN], F32, kind="ExternalInput").ap(),
        "W_outT": nc.dram_tensor("W_outT", [DIN, D], F32, kind="ExternalInput").ap(),
    }
    scr = {
        "xfm_scr": nc.dram_tensor("xfm_scr", [DIN, T], BF16, kind="ExternalInput").ap(),
        "bfm_scr": nc.dram_tensor("bfm_scr", [128, T], BF16, kind="ExternalInput").ap(),
        "cfm_scr": nc.dram_tensor("cfm_scr", [128, T], BF16, kind="ExternalInput").ap(),
        "dtr_scr": nc.dram_tensor("dtr_scr", [16, T], F32, kind="ExternalInput").ap(),
        "z_scr": nc.dram_tensor("z_scr", [T, DIN], BF16, kind="ExternalInput").ap(),
        "acs_scr": nc.dram_tensor("acs_scr", [16, T], F32, kind="Internal").ap(),
        "cdec_scr": nc.dram_tensor("cdec_scr", [2, 16, 8], F32, kind="Internal").ap(),
        "s_scr": nc.dram_tensor("s_scr", [T, D], BF16, kind="ExternalOutput").ap(),
    }
    em = Em()
    with contextlib.ExitStack() as stx:
        bufs = {}
        def sb(name, shape, dtype):
            return stx.enter_context(nc.sbuf_tensor(name, shape, dtype))
        bufs["ident"] = sb("ident", [128, 128], BF16)
        mk_s2_bufs(nc, stx, bufs)
        mk_psum_pool(nc, stx, bufs)
        sems = em.make_sems(nc, stx)
        blk = stx.enter_context(nc.Block())

        bufs["ident_tok"] = build_identity(nc, em, bufs)
        bufs["identf_tok"] = build_identity_f32(nc, em, bufs)
        stage_s2(nc, em, bufs, wd, scr)
        em.fence("sy")
        em.emit(nc, blk, sems)
    return nc


def load_fm(em, sbuf_fm, dram_fm, nchunks, ncols, q="sy"):
    src = dram_fm.rearrange("(c p) t -> p c t", p=128)
    return em.dma(q, lambda e: e.dma_start(out=sbuf_fm[:, 0:nchunks, 0:ncols], in_=src))


def mk_s1_bufs(nc, stx, bufs, alloc=None):
    def sb(name, shape, dtype):
        if alloc is not None:
            return alloc(shape, dtype)
        return stx.enter_context(nc.sbuf_tensor(name, shape, dtype))
    bufs["s1_w"] = sb("s1_w", [128, 8, 2304], BF16)
    bufs["s1_wdt"] = sb("s1_wdt", [128, 8, 16], BF16)
    bufs["s1_z"] = sb("s1_z", [128, 2, 2048], BF16)
    bufs["s1_raw"] = sb("s1_raw", [128, 1, 4102], F32)
    bufs["s1_acc"] = sb("s1_acc", [128, 1, 4096], F32)
    bufs["s1_act"] = sb("s1_act", [128, 1, 4096], BF16)
    bufs["s1_cw"] = sb("s1_cw", [128, 2, 4], F32)
    bufs["s1_cb"] = sb("s1_cb", [128, 2, 1], F32)
    bufs["s1_dtr"] = sb("s1_dtr", [16, T], F32)


def build_s1_program():
    nc = bass.Bass("TRN2", target_bir_lowering=False)
    xn_d = nc.dram_tensor("xn_fm", [D, T], BF16, kind="ExternalInput").ap()
    wd = {
        "W_inT": nc.dram_tensor("W_inT", [1024, 4368], F32, kind="ExternalInput").ap(),
        "conv_w": nc.dram_tensor("conv_w", [NJC, 128, 4], F32, kind="ExternalInput").ap(),
        "conv_b": nc.dram_tensor("conv_b", [NJC, 128, 1], F32, kind="ExternalInput").ap(),
    }
    scr = {
        "z_scr": nc.dram_tensor("z_scr", [T, DIN], BF16, kind="ExternalOutput").ap(),
        "xfm_scr": nc.dram_tensor("xfm_scr", [DIN, T], BF16, kind="ExternalOutput").ap(),
        "bfm_scr": nc.dram_tensor("bfm_scr", [128, T], BF16, kind="ExternalOutput").ap(),
        "cfm_scr": nc.dram_tensor("cfm_scr", [128, T], BF16, kind="ExternalOutput").ap(),
        "dtr_scr": nc.dram_tensor("dtr_scr", [16, T], F32, kind="ExternalOutput").ap(),
    }
    em = Em()
    with contextlib.ExitStack() as stx:
        bufs = {}
        def sb(name, shape, dtype):
            return stx.enter_context(nc.sbuf_tensor(name, shape, dtype))
        bufs["xn_fm"] = sb("xn_fm_sb", [128, 8, T], BF16)
        mk_s1_bufs(nc, stx, bufs)
        mk_psum_pool(nc, stx, bufs)
        sems = em.make_sems(nc, stx)
        blk = stx.enter_context(nc.Block())

        xn_tok = load_fm(em, bufs["xn_fm"], xn_d, 8, T)
        for e in ("te", "ve", "sc", "gp"):
            em.wait(e, xn_tok)
        stage_s1(nc, em, bufs, wd, scr, xn_tok)
        em.fence("sy")
        em.emit(nc, blk, sems)
    return nc


def mk_s3_bufs(nc, stx, bufs, alloc=None):
    def sb(name, shape, dtype):
        if alloc is not None:
            return alloc(shape, dtype)
        return stx.enter_context(nc.sbuf_tensor(name, shape, dtype))
    bufs["s3_wg"] = sb("s3_wg", [128, 8, 8], BF16)
    bufs["s3_bgbc"] = sb("s3_bgbc", [128, 8], F32)
    bufs["s3_be"] = sb("s3_be", [8, 1024], BF16)
    bufs["s3_gcl"] = sb("s3_gcl", [128, 4, 32], F32)
    bufs["s3_gsc"] = sb("s3_gsc", [128, 32, 8], F32)
    bufs["s3_gax1"] = sb("s3_gax1", [128, 32, 8], F32)
    bufs["s3_gax2"] = sb("s3_gax2", [128, 32, 8], F32)
    bufs["s3_gw"] = bufs["s3_gax2"]
    bufs["s3_gwb"] = sb("s3_gwb", [128, 32, 8], BF16)
    bufs["s3_wt"] = sb("s3_wt", [8, 32, 128], BF16)
    bufs["s3_we"] = sb("s3_we", [128, 2, 8, 1024], BF16)
    bufs["s3_acc"] = sb("s3_acc", [128, 16, 1024], F32)
    bufs["s3_u"] = sb("s3_u", [128, 1, 1024], F32)
    bufs["s3_s"] = sb("s3_s", [128, 2, 1024], BF16)
    bufs["s3_g"] = sb("s3_g", [128, 1024], BF16)
    bufs["s3_sg"] = sb("s3_sg", [128, 1024], F32)
    bufs["s3_x1"] = sb("s3_x1", [128, 2, 1024], F32)
    bufs["s3_xn1"] = sb("s3_xn1", [128, 2, 1024], BF16)


def build_s3_program():
    nc = bass.Bass("TRN2", target_bir_lowering=False)
    xn_d = nc.dram_tensor("xn_fm", [D, T], BF16, kind="ExternalInput").ap()
    wd = {
        "Wg": nc.dram_tensor("Wg", [1024, 8], F32, kind="ExternalInput").ap(),
        "bg": nc.dram_tensor("bg", [1, 8], F32, kind="ExternalInput").ap(),
        "We": nc.dram_tensor("We", [8, 1024, 1024], F32, kind="ExternalInput").ap(),
        "be": nc.dram_tensor("be", [8, 1024], F32, kind="ExternalInput").ap(),
        "u_d": nc.dram_tensor("u_d", [T, D], F32, kind="ExternalInput").ap(),
    }
    scr = {
        "s_scr": nc.dram_tensor("s_scr", [T, D], BF16, kind="ExternalInput").ap(),
        "x1_scr": nc.dram_tensor("x1_scr", [T, D], F32, kind="ExternalOutput").ap(),
        "xn1_scr": nc.dram_tensor("xn1_scr", [T, D], BF16, kind="ExternalOutput").ap(),
    }
    em = Em()
    with contextlib.ExitStack() as stx:
        bufs = {}
        def sb(name, shape, dtype):
            return stx.enter_context(nc.sbuf_tensor(name, shape, dtype))
        bufs["ident"] = sb("ident", [128, 128], BF16)
        bufs["xn_fm"] = sb("xn_fm_sb", [128, 8, T], BF16)
        bufs["ln_col"] = sb("ln_col", [128, 16], F32)
        bufs["s1_acc"] = sb("s1_acc", [128, 1024], F32)
        mk_s3_bufs(nc, stx, bufs)
        mk_psum_pool(nc, stx, bufs)
        sems = em.make_sems(nc, stx)
        blk = stx.enter_context(nc.Block())

        bufs["ident_tok"] = build_identity(nc, em, bufs)
        xn_tok = load_fm(em, bufs["xn_fm"], xn_d, 8, T)
        stage_s3(nc, em, bufs, wd, scr, xn_tok)
        em.fence("sy")
        em.emit(nc, blk, sems)
    return nc


def mk_s4_bufs(nc, stx, bufs, alloc=None):
    def sb(name, shape, dtype):
        if alloc is not None:
            return alloc(shape, dtype)
        return stx.enter_context(nc.sbuf_tensor(name, shape, dtype))
    bufs["s4_xn1"] = sb("s4_xn1", [128, 8, 2048], BF16)
    bufs["s4_tm"] = sb("s4_tm", [128, 2, 1024], BF16)
    bufs["s4_w"] = sb("s4_w", [128, 2, 3, 8, 128], BF16)
    bufs["s4_b"] = sb("s4_b", [128, 3, 8], F32)
    bufs["s4_q"] = sb("s4_q", [128, 2, 2048], BF16)
    bufs["s4_k"] = sb("s4_k", [128, 2, 2048], BF16)
    bufs["s4_vf"] = sb("s4_vf", [128, 2, 2048], BF16)
    bufs["s4_vt"] = sb("s4_vt", [128, 2, 16, 130], BF16)
    bufs["s4_pt"] = sb("s4_pt", [128, 2, 2, 16, 512], BF16)
    bufs["s4_zfm"] = sb("s4_zfm", [128, 8, 2048], BF16)
    bufs["s4_proj"] = sb("s4_proj", [128, 8, 1024], BF16)
    bufs["s4_ones"] = sb("s4_ones", [1, 64], BF16)
    bufs["s4_r"] = sb("s4_r", [1, 512], F32)
    bufs["s4_rbf"] = sb("s4_rbf", [1, 2, 512], BF16)
    bufs["s4_bc"] = sb("s4_bc", [64, 2, 512], F32)
    bufs["s4_s1"] = sb("s4_s1", [128, 2, 1024], BF16)


def build_s4_program():
    nc = bass.Bass("TRN2", target_bir_lowering=False)
    wd = {
        "WqT": nc.dram_tensor("WqT", [1024, 1024], F32, kind="ExternalInput").ap(),
        "WkT": nc.dram_tensor("WkT", [1024, 1024], F32, kind="ExternalInput").ap(),
        "WvT": nc.dram_tensor("WvT", [1024, 1024], F32, kind="ExternalInput").ap(),
        "bq": nc.dram_tensor("bq", [1024, 1], F32, kind="ExternalInput").ap(),
        "bk": nc.dram_tensor("bk", [1024, 1], F32, kind="ExternalInput").ap(),
        "bv": nc.dram_tensor("bv", [1024, 1], F32, kind="ExternalInput").ap(),
        "proj": nc.dram_tensor("proj", [1024, 1024], F32, kind="ExternalInput").ap(),
    }
    scr = {
        "xn1_scr": nc.dram_tensor("xn1_scr", [T, D], BF16, kind="ExternalInput").ap(),
        "s1_scr": nc.dram_tensor("s1_scr", [T, D], BF16, kind="ExternalOutput").ap(),
    }
    em = Em()
    with contextlib.ExitStack() as stx:
        bufs = {}
        def sb(name, shape, dtype):
            return stx.enter_context(nc.sbuf_tensor(name, shape, dtype))
        bufs["ident"] = sb("ident", [128, 128], BF16)
        mk_s4_bufs(nc, stx, bufs)
        mk_psum_pool(nc, stx, bufs)
        sems = em.make_sems(nc, stx)
        blk = stx.enter_context(nc.Block())

        bufs["ident_tok"] = build_identity(nc, em, bufs)
        stage_s4(nc, em, bufs, wd, scr)
        em.fence("sy")
        em.emit(nc, blk, sems)
    return nc


def mk_s5_bufs(nc, stx, bufs, alloc=None):
    def sb(name, shape, dtype):
        if alloc is not None:
            return alloc(shape, dtype)
        return stx.enter_context(nc.sbuf_tensor(name, shape, dtype))
    bufs["s5_wq"] = sb("s5_wq", [128, 8, 1024], BF16)
    bufs["s5_wk"] = sb("s5_wk", [128, 8, 1024], BF16)
    bufs["s5_wi"] = sb("s5_wi", [128, 8, 1024], BF16)
    bufs["s5_w1"] = sb("s5_w1", [128, 8, 1024], BF16)
    bufs["s5_wo"] = sb("s5_wo", [128, 8, 1024], BF16)
    bufs["s5_vw"] = sb("s5_vw", [128, 4, 8, 512], BF16)
    bufs["s5_u"] = sb("s5_u", [128, 2, 4, 1024], BF16)
    bufs["s5_bb"] = sb("s5_bb", [8, 2, 1024], BF16)
    bufs["s5_rw2"] = sb("s5_rw2", [128, 8, 8], BF16)
    bufs["s5_rb1"] = sb("s5_rb1", [128, 8], F32)
    bufs["s5_ebc"] = sb("s5_ebc", [128, 2, 8], F32)
    bufs["s5_al"] = sb("s5_al", [128, 1], F32)
    bufs["s5_invn"] = sb("s5_invn", [128, 32], F32)
    reg5 = sb("s5_reg5", [128, 4096], BF16)
    bufs["s5_xnb"] = carve(reg5, 0, [128, 8, 512], BF16)
    bufs["s5_wexp"] = carve(reg5, 0, [128, 4, 512], BF16)
    bufs["s5_prodw"] = carve(reg5, 2048, [128, 4, 512], BF16)
    bufs["s5_qb"] = sb("s5_qb", [128, 8, 512], BF16)
    bufs["s5_kb"] = sb("s5_kb", [128, 8, 512], BF16)
    bufs["s5_iqb"] = sb("s5_iqb", [128, 8, 512], BF16)
    bufs["s5_ikb"] = bufs["s5_qb"]
    bufs["s5_xv"] = sb("s5_xv", [128, 4, 512], F32)
    bufs["s5_yw"] = sb("s5_yw", [128, 4, 512], F32)
    bufs["s5_cum"] = bufs["s5_yw"]
    bufs["s5_carry"] = sb("s5_carry", [128, 2, 4], F32)
    bufs["s5_hb"] = sb("s5_hb", [128, 8, 512], BF16)
    bufs["s5_lg"] = sb("s5_lg", [128, 4, 8], F32)
    bufs["s5_lcol"] = sb("s5_lcol", [128, 8], F32)
    bufs["s5_gwb"] = sb("s5_gwb", [128, 8], BF16)
    bufs["s5_wt"] = sb("s5_wt", [8, 4, 128], BF16)

    bufs["s5_osum"] = sb("s5_osum", [128, 8, 512], BF16)
    reg6 = sb("s5_reg6", [128, 2048], BF16)
    bufs["s4_tm"] = carve(reg6, 0, [128, 2, 1024], BF16)
    bufs["s5_x1t"] = carve(reg6, 0, [128, 1, 1024], F32)
    bufs["s5_s1t"] = sb("s5_s1t", [128, 1, 1024], BF16)
    bufs["s5_g1"] = sb("s5_g1", [128, 1024], BF16)

    bufs["zcol128"] = sb("zcol128", [128, 1], F32)


def build_s5_program():
    nc = bass.Bass("TRN2", target_bir_lowering=False)
    wd = {}
    for name in ("W_QT", "W_KT", "W_invT", "rw1T", "W_OT"):
        wd[name] = nc.dram_tensor(name, [1024, 1024], F32, kind="ExternalInput").ap()
    for name in ("VfT", "WfT", "ViT", "WiT"):
        wd[name] = nc.dram_tensor(name, [1024, 512], F32, kind="ExternalInput").ap()
    for name in ("Uf", "Ui"):
        wd[name] = nc.dram_tensor(name, [512, 1024], F32, kind="ExternalInput").ap()
    for name in ("b_fwd", "b_inv"):
        wd[name] = nc.dram_tensor(name, [8, 1024], F32, kind="ExternalInput").ap()
    wd["rw2T"] = nc.dram_tensor("rw2T", [1024, 8], F32, kind="ExternalInput").ap()
    wd["rb1"] = nc.dram_tensor("rb1", [1024, 1], F32, kind="ExternalInput").ap()
    wd["rb2"] = nc.dram_tensor("rb2", [1, 8], F32, kind="ExternalInput").ap()
    wd["expert_bias"] = nc.dram_tensor("expert_bias", [1, 8], F32, kind="ExternalInput").ap()
    wd["alpha_bi"] = nc.dram_tensor("alpha_bi", [1, 1], F32, kind="ExternalInput").ap()
    scr = {
        "xn1_scr": nc.dram_tensor("xn1_scr", [T, D], BF16, kind="ExternalInput").ap(),
        "x1_scr": nc.dram_tensor("x1_scr", [T, D], F32, kind="ExternalInput").ap(),
        "s1_scr": nc.dram_tensor("s1_scr", [T, D], BF16, kind="ExternalInput").ap(),
        "wgt_scr": nc.dram_tensor("wgt_scr", [2, 8, T], BF16, kind="Internal").ap(),
    }
    out_d = nc.dram_tensor("x2", [T, D], F32, kind="ExternalOutput").ap()
    em = Em()
    with contextlib.ExitStack() as stx:
        bufs = {}
        def sb(name, shape, dtype):
            return stx.enter_context(nc.sbuf_tensor(name, shape, dtype))
        bufs["ident"] = sb("ident", [128, 128], BF16)
        mk_s5_bufs(nc, stx, bufs)
        mk_psum_pool(nc, stx, bufs)
        sems = em.make_sems(nc, stx)
        blk = stx.enter_context(nc.Block())

        bufs["ident_tok"] = build_identity(nc, em, bufs)
        stage_s5(nc, em, bufs, wd, scr, out_d)
        em.fence("sy")
        em.emit(nc, blk, sems)
    return nc


def build_s0_program():
    nc = bass.Bass("TRN2", target_bir_lowering=False)
    u = nc.dram_tensor("u", [T, D], F32, kind="ExternalInput").ap()
    xn_fm_d = nc.dram_tensor("xn_fm", [D, T], BF16, kind="ExternalOutput").ap()
    em = Em()
    with contextlib.ExitStack() as stx:
        bufs = {}
        def sb(name, shape, dtype):
            h = stx.enter_context(nc.sbuf_tensor(name, shape, dtype))
            return h
        bufs["ident"] = sb("ident", [128, 128], BF16)
        bufs["ln_x"] = sb("ln_x", [128, 2, D], F32)
        bufs["ln_sq"] = sb("ln_sq", [128, D], F32)
        bufs["ln_col"] = sb("ln_col", [128, 16], F32)
        bufs["ln_xn"] = sb("ln_xn", [128, 2, D], BF16)
        xn_fm = sb("xn_fm_sb", [128, 8, T], BF16)
        mk_psum_pool(nc, stx, bufs)
        sems = em.make_sems(nc, stx)
        blk = stx.enter_context(nc.Block())

        bufs["ident_tok"] = build_identity(nc, em, bufs)
        t_fm = stage_ln(nc, em, bufs, u, xn_fm)
        em.fence("sy")
        out_tok = dump_fm(em, xn_fm, xn_fm_d, 8, T)
        em.wait("sy", out_tok)
        em.emit(nc, blk, sems)
    return nc
